# version 01
# baseline (speedup 1.0000x reference)
import numpy as np
import ml_dtypes
import contextlib
import concourse.bass as bass
import concourse.mybir as mybir
from concourse.bass_utils import run_bass_kernel_spmd
from concourse.bass_core_rust import IndirectOffsetOnAxis

F32 = mybir.dt.float32
BF16 = mybir.dt.bfloat16
I32 = mybir.dt.int32
AF = mybir.ActivationFunctionType
ALU = mybir.AluOpType
AX = mybir.AxisListType

NC = 8
T = 16384
D = 2048
NH = 16
HD = 128
CH = 256
E = 8
HF = 2816
CLOC = 512
TPC = T // NC
CAP = 2048
NTT = TPC // 128
NCH = TPC // CH


def _rope_tables():
    inv = 1.0 / (10000.0 ** (np.arange(0, HD, 2, dtype=np.float32) / HD))
    pos = np.arange(T, dtype=np.float32)
    emb = np.concatenate([np.outer(pos, inv)] * 2, -1)
    return np.cos(emb).astype(np.float32), np.sin(emb).astype(np.float32)


def _p_signed():
    P = np.zeros((HD, HD), np.float32)
    for i in range(HD // 2):
        P[2 * i + 1, 2 * i] = -1.0
        P[2 * i, 2 * i + 1] = 1.0
    return P.astype(ml_dtypes.bfloat16)


def build_l1():
    nc = bass.Bass("TRN2", target_bir_lowering=False)
    X = nc.dram_tensor("X", [TPC, D], F32, kind="ExternalInput").ap()
    WQKVT = nc.dram_tensor("WQKVT", [D, 3 * D], F32, kind="ExternalInput").ap()
    COSF = nc.dram_tensor("COSF", [HD, TPC], F32, kind="ExternalInput").ap()
    SINF = nc.dram_tensor("SINF", [HD, TPC], F32, kind="ExternalInput").ap()
    PSGN = nc.dram_tensor("PSGN", [HD, HD], BF16, kind="ExternalInput").ap()
    IDN = nc.dram_tensor("IDN", [128, 128], BF16, kind="ExternalInput").ap()

    QF = nc.dram_tensor("QF", [D, TPC], BF16, kind="ExternalOutput").ap()
    KF = nc.dram_tensor("KF", [D, TPC], BF16, kind="ExternalOutput").ap()
    KT = nc.dram_tensor("KT", [TPC, D], BF16, kind="ExternalOutput").ap()
    VT = nc.dram_tensor("VT", [TPC, D], BF16, kind="ExternalOutput").ap()
    SZT = nc.dram_tensor("SZT", [NH, HD, HD + 1], F32, kind="ExternalOutput").ap()

    X0BF = nc.dram_tensor("X0BF", [TPC, D], BF16, kind="Internal").ap()

    Tn = nc.alloc_sbuf_tensor
    P = nc.alloc_psum_tensor
    x0fm = Tn("x0fm", [128, 16 * TPC], BF16)
    cosf = Tn("cosf", [128, TPC], F32)
    sinf = Tn("sinf", [128, TPC], F32)
    psgn = Tn("psgn", [128, 128], BF16)
    idn = Tn("idn", [128, 128], BF16)
    wqk = Tn("wqk", [128, 2 * 2048], BF16)
    wv = Tn("wv", [128, 2 * 8192], BF16)
    af32 = Tn("af32", [128, 512], F32)
    abf = Tn("abf", [128, 512], BF16)
    t1 = Tn("t1", [128, 512], F32)
    t2 = Tn("t2", [128, 512], F32)
    escr = Tn("escr", [128, 512], F32)
    qkout = Tn("qkout", [128, TPC], BF16)
    ktok = Tn("ktok", [128, 16 * 128], BF16)
    vtok = Tn("vtok", [128, 16 * 512], BF16)
    szev = Tn("szev", [128, HD + 1], F32)
    ones1 = Tn("ones1", [128, 1], BF16)

    pq = P("pq", [128, 512], F32)
    psw = P("psw", [128, 512], F32)
    pv = [P("pv0", [128, 512], F32), P("pv1", [128, 512], F32)]
    ptp = [P("ptp0", [128, 128], BF16), P("ptp1", [128, 128], BF16)]
    psz = [P("psz0", [128, HD + 1], F32), P("psz1", [128, HD + 1], F32)]

    es = contextlib.ExitStack()
    sgp = es.enter_context(nc.semaphore("sgp"))
    sld = es.enter_context(nc.semaphore("sld"))
    spe = es.enter_context(nc.semaphore("spe"))
    sdv = es.enter_context(nc.semaphore("sdv"))
    sac = es.enter_context(nc.semaphore("sac"))
    sst = es.enter_context(nc.semaphore("sst"))

    cnt = {"gp": 0, "ld": 0, "pe": 0, "dv": 0, "ac": 0, "st": 0}
    M = {}

    with nc.Block() as block:
        @block.gpsimd
        def _(gp):
            gp.dma_start(out=X0BF, in_=X).then_inc(sgp, 16)
            cnt["gp"] += 16
            M["x0bf"] = cnt["gp"]

        def load_weights(kind, idx):
            @block.gpsimd
            def _(gp, kind=kind, idx=idx):
                slot = 0 if kind != "k" else 1
                key = ("wfree", kind, idx)
                if key in M:
                    gp.wait_ge(spe, M[key])
                if kind == "v":
                    s = WQKVT[:, 2 * D + idx * 512: 2 * D + (idx + 1) * 512]
                    s = s.rearrange("(c p) f -> p c f", p=128)
                    d_ = wv[:, (idx % 2) * 8192:(idx % 2 + 1) * 8192]
                    d_ = d_.rearrange("p (c f) -> p c f", f=512)
                else:
                    base = 0 if kind == "q" else D
                    s = WQKVT[:, base + idx * 128: base + (idx + 1) * 128]
                    s = s.rearrange("(c p) f -> p c f", p=128)
                    d_ = wqk[:, slot * 2048:(slot + 1) * 2048]
                    d_ = d_.rearrange("p (c f) -> p c f", f=128)
                gp.dma_start(out=d_, in_=s).then_inc(sgp, 16)
                cnt["gp"] += 16
                M[("wld", kind, idx)] = cnt["gp"]

        @block.sync
        def _(sy):
            for ap_s, ap_d in ((COSF, cosf), (SINF, sinf), (PSGN, psgn), (IDN, idn)):
                sy.dma_start(out=ap_d[:], in_=ap_s).then_inc(sld, 16)
                cnt["ld"] += 16
            sy.wait_ge(sgp, M["x0bf"])
            for c in range(16):
                sy.dma_start_transpose(
                    out=x0fm[:, c * TPC:(c + 1) * TPC],
                    in_=X0BF[:, c * 128:(c + 1) * 128],
                ).then_inc(sld, 16)
                cnt["ld"] += 16
            M["ld_all"] = cnt["ld"]

        @block.vector
        def _(dv):
            dv.memset(ones1[:], 1.0).then_inc(sdv, 1)
            cnt["dv"] += 1

        for g in range(4):
            load_weights("v", g)
            pe0 = cnt["pe"]
            dv0 = cnt["dv"]

            @block.tensor
            def _(te, g=g, pe0=pe0, dv0=dv0):
                te.wait_ge(sgp, M[("wld", "v", g)])
                te.wait_ge(sld, M["ld_all"])
                for tt in range(16):
                    if tt >= 2:
                        te.wait_ge(sdv, dv0 + tt - 1)
                    last = None
                    for c in range(16):
                        last = te.matmul(
                            out=pv[tt % 2][:],
                            lhsT=x0fm[:, c * TPC + tt * 128: c * TPC + (tt + 1) * 128],
                            rhs=wv[:, (g % 2) * 8192 + c * 512:(g % 2) * 8192 + (c + 1) * 512],
                            start=(c == 0), stop=(c == 15),
                        )
                    last.then_inc(spe, 1)
                    cnt["pe"] += 1
                M[("vmm", g)] = cnt["pe"]
                M[("wfree", "v", g + 2)] = cnt["pe"]

            @block.vector
            def _(dv, g=g, pe0=pe0):
                for tt in range(16):
                    dv.wait_ge(spe, pe0 + tt + 1)
                    if g > 0:
                        dv.wait_ge(spe, M[("szdone", g - 1)])
                        dv.wait_ge(sst, M[("vstash", g - 1)])
                    dv.tensor_copy(
                        out=vtok[:, tt * 512:(tt + 1) * 512], in_=pv[tt % 2][:]
                    ).then_inc(sdv, 1)
                    cnt["dv"] += 1
                M[("vev", g)] = cnt["dv"]

            @block.sync
            def _(sy, g=g, dv0=dv0):
                for tt in range(16):
                    sy.wait_ge(sdv, dv0 + tt + 1)
                    sy.dma_start(
                        out=VT[tt * 128:(tt + 1) * 128, g * 512:(g + 1) * 512],
                        in_=vtok[:, tt * 512:(tt + 1) * 512],
                    ).then_inc(sst, 16)
                    cnt["st"] += 16
                M[("vstash", g)] = cnt["st"]

            for hh in range(4):
                h = 4 * g + hh
                for part in ("q", "k"):
                    load_weights(part, h)
                    slot = 0 if part == "q" else 1
                    dst = QF if part == "q" else KF
                    pe0 = cnt["pe"]
                    dv0 = cnt["dv"]
                    ac0 = cnt["ac"]
                    st_prev = cnt["st"]

                    @block.tensor
                    def _(te, h=h, part=part, slot=slot, pe0=pe0, dv0=dv0):
                        te.wait_ge(sgp, M[("wld", part, h)])
                        te.wait_ge(sst, M.get("qkstash", 0))
                        for tc in range(4):
                            if tc > 0:
                                te.wait_ge(sdv, dv0 + 3 * (tc - 1) + 1)
                            last = None
                            for c in range(16):
                                last = te.matmul(
                                    out=pq[:],
                                    lhsT=wqk[:, slot * 2048 + c * 128: slot * 2048 + (c + 1) * 128],
                                    rhs=x0fm[:, c * TPC + tc * 512: c * TPC + (tc + 1) * 512],
                                    start=(c == 0), stop=(c == 15),
                                )
                            last.then_inc(spe, 1)
                            cnt["pe"] += 1
                            te.wait_ge(sdv, dv0 + 3 * tc + 1)
                            if tc > 0:
                                te.wait_ge(sdv, dv0 + 3 * (tc - 1) + 2)
                            te.matmul(out=psw[:], lhsT=psgn[:], rhs=abf[:],
                                      start=True, stop=True).then_inc(spe, 1)
                            cnt["pe"] += 1
                        M[("wfree", part, h + (2 if part == "v" else 1))] = cnt["pe"]

                    @block.vector
                    def _(dv, pe0=pe0, ac0=ac0, st_prev=st_prev):
                        for tc in range(4):
                            dv.wait_ge(spe, pe0 + 2 * tc + 1)
                            if tc == 0:
                                dv.wait_ge(sst, M.get("qkstash", 0))
                            dv.tensor_copy(out=af32[:], in_=pq[:])
                            dv.tensor_copy(out=abf[:], in_=af32[:]).then_inc(sdv, 1)
                            cnt["dv"] += 1
                            dv.wait_ge(spe, pe0 + 2 * tc + 2)
                            sl = slice(tc * 512, (tc + 1) * 512)
                            dv.tensor_mul(out=t1[:], in0=af32[:], in1=cosf[:, sl])
                            dv.tensor_mul(out=t2[:], in0=psw[:], in1=sinf[:, sl])
                            dv.tensor_add(out=t1[:], in0=t1[:], in1=t2[:]).then_inc(sdv, 1)
                            cnt["dv"] += 1
                            dv.wait_ge(sac, ac0 + tc + 1)
                            dv.tensor_scalar_min(out=t2[:], in0=escr[:], scalar1=1.0)
                            dv.scalar_tensor_tensor(
                                out=qkout[:, sl], in0=t1[:], scalar=0.0, in1=t2[:],
                                op0=ALU.max, op1=ALU.add,
                            ).then_inc(sdv, 1)
                            cnt["dv"] += 1
                        M["qkready"] = cnt["dv"]

                    @block.scalar
                    def _(ac, dv0=dv0):
                        for tc in range(4):
                            ac.wait_ge(sdv, dv0 + 3 * tc + 2)
                            ac.activation(out=escr[:], in_=t1[:], func=AF.Exp).then_inc(sac, 1)
                            cnt["ac"] += 1

                    if part == "k":
                        pe1 = cnt["pe"]
                        dv1 = cnt["dv"]

                        @block.tensor
                        def _(te, dv1=dv1):
                            te.wait_ge(sdv, M["qkready"])
                            for tt in range(16):
                                if tt >= 2:
                                    te.wait_ge(sdv, dv1 + tt - 1)
                                te.transpose(
                                    out=ptp[tt % 2][:],
                                    in_=qkout[:, tt * 128:(tt + 1) * 128],
                                    identity=idn[:],
                                ).then_inc(spe, 1)
                                cnt["pe"] += 1

                        @block.vector
                        def _(dv, pe1=pe1, h=h):
                            dv.wait_ge(sst, M.get("ktstash", 0))
                            for tt in range(16):
                                dv.wait_ge(spe, pe1 + tt + 1)
                                dv.tensor_copy(
                                    out=ktok[:, tt * 128:(tt + 1) * 128],
                                    in_=ptp[tt % 2][:],
                                ).then_inc(sdv, 1)
                                cnt["dv"] += 1
                            M["ktokready"] = cnt["dv"]

                    @block.sync
                    def _(sy, h=h, part=part, dst=dst):
                        sy.wait_ge(sdv, M["qkready"])
                        sy.dma_start(out=dst[h * 128:(h + 1) * 128, :],
                                     in_=qkout[:]).then_inc(sst, 16)
                        cnt["st"] += 16
                        M["qkstash"] = cnt["st"]
                        if part == "k":
                            sy.wait_ge(sdv, M["ktokready"])
                            sy.dma_start(
                                out=KT[:, h * 128:(h + 1) * 128].rearrange(
                                    "(tt p) d -> p tt d", p=128),
                                in_=ktok[:].rearrange("p (tt d) -> p tt d", d=128),
                            ).then_inc(sst, 16)
                            cnt["st"] += 16
                            M["ktstash"] = cnt["st"]

                @block.tensor
                def _(te, g=g, hh=hh, h=h):
                    te.wait_ge(sdv, M["ktokready"])
                    te.wait_ge(sdv, M[("vev", g)])
                    te.wait_ge(sdv, 1)
                    b = psz[h % 2]
                    te.wait_ge(sdv, M.get(("szfree", h % 2), 0))
                    last = None
                    for tt in range(16):
                        last = te.matmul(
                            out=b[:, 0:HD],
                            lhsT=ktok[:, tt * 128:(tt + 1) * 128],
                            rhs=vtok[:, tt * 512 + hh * 128: tt * 512 + (hh + 1) * 128],
                            start=(tt == 0), stop=False,
                        )
                    for tt in range(16):
                        last = te.matmul(
                            out=b[:, HD:HD + 1],
                            lhsT=ktok[:, tt * 128:(tt + 1) * 128],
                            rhs=ones1[:],
                            start=False, stop=(tt == 15),
                        )
                    last.then_inc(spe, 1)
                    cnt["pe"] += 1
                    M["szmm"] = cnt["pe"]
                    if hh == 3:
                        M[("szdone", g)] = cnt["pe"]

                @block.vector
                def _(dv, h=h, g=g, hh=hh):
                    dv.wait_ge(spe, M["szmm"])
                    dv.wait_ge(sst, M.get("szstash", 0))
                    dv.tensor_copy(out=szev[:], in_=psz[h % 2][:]).then_inc(sdv, 1)
                    cnt["dv"] += 1
                    M[("szfree", h % 2)] = cnt["dv"]
                    if hh == 3:
                        M[("vconsumed", g)] = cnt["dv"]

                @block.sync
                def _(sy, h=h):
                    sy.wait_ge(sdv, M[("szfree", h % 2)])
                    sy.dma_start(out=SZT[h], in_=szev[:]).then_inc(sst, 16)
                    cnt["st"] += 16
                    M["szstash"] = cnt["st"]

        @block.sync
        def _(sy):
            sy.wait_ge(sst, cnt["st"])

    es.close()
    return nc



def build_l2(debug=False):
    nc = bass.Bass("TRN2", target_bir_lowering=False)
    QF = nc.dram_tensor("QF", [D, TPC], BF16, kind="ExternalInput").ap()
    KF = nc.dram_tensor("KF", [D, TPC], BF16, kind="ExternalInput").ap()
    KT = nc.dram_tensor("KT", [TPC, D], BF16, kind="ExternalInput").ap()
    VT = nc.dram_tensor("VT", [TPC, D], BF16, kind="ExternalInput").ap()
    SZALL = nc.dram_tensor("SZALL", [NC, NH, HD, HD + 1], F32, kind="ExternalInput").ap()
    X = nc.dram_tensor("X", [TPC, D], F32, kind="ExternalInput").ap()
    WOUTT = nc.dram_tensor("WOUTT", [D, D], F32, kind="ExternalInput").ap()
    WG = nc.dram_tensor("WG", [D, E], F32, kind="ExternalInput").ap()
    MASK = nc.dram_tensor("MASK", [128, 256], BF16, kind="ExternalInput").ap()
    IDN = nc.dram_tensor("IDN", [128, 128], BF16, kind="ExternalInput").ap()
    IDNF = nc.dram_tensor("IDNF", [128, 128], F32, kind="ExternalInput").ap()
    UT = nc.dram_tensor("UT", [128, 128], F32, kind="ExternalInput").ap()
    ONESR = nc.dram_tensor("ONESR", [1, 128], F32, kind="ExternalInput").ap()
    IOTA8 = nc.dram_tensor("IOTA8", [128, E], F32, kind="ExternalInput").ap()
    CPREFB = nc.dram_tensor("CPREFB", [128, NC], F32, kind="ExternalInput").ap()

    X1T = nc.dram_tensor("X1T", [TPC, D], BF16, kind="ExternalOutput").ap()
    X1F = nc.dram_tensor("X1F", [D, TPC], BF16, kind="ExternalOutput").ap()
    PBUF = nc.dram_tensor("PBUF", [E * CLOC + 1, D], BF16, kind="ExternalOutput").ap()
    CNT = nc.dram_tensor("CNT", [1, E], F32, kind="ExternalOutput").ap()
    EIDX = nc.dram_tensor("EIDX", [128, NTT], F32, kind="ExternalOutput").ap()
    LPOS = nc.dram_tensor("LPOS", [128, NTT], F32, kind="ExternalOutput").ap()
    GATE = nc.dram_tensor("GATE", [128, NTT], F32, kind="ExternalOutput").ap()
    if debug:
        DB_SZB = nc.dram_tensor("DB_SZB", [128, NH * 129], BF16, kind="ExternalOutput").ap()
        DB_S1 = nc.dram_tensor("DB_S1", [128, 2 * D], F32, kind="ExternalOutput").ap()
        DB_ATT = nc.dram_tensor("DB_ATT", [128, 384], BF16, kind="ExternalOutput").ap()
        DB_NUM = nc.dram_tensor("DB_NUM", [128, 258], F32, kind="ExternalOutput").ap()
        DB_RD = nc.dram_tensor("DB_RD", [128, 2], F32, kind="ExternalOutput").ap()
        DB_NUMN = nc.dram_tensor("DB_NUMN", [128, 256], BF16, kind="ExternalOutput").ap()
        DB_ZSB = nc.dram_tensor("DB_ZSB", [8, TPC], F32, kind="ExternalOutput").ap()
        DB_NUMF = nc.dram_tensor("DB_NUMF", [128, NH * 256], BF16, kind="ExternalOutput").ap()

    Tn = nc.alloc_sbuf_tensor
    P = nc.alloc_psum_tensor
    qf = Tn("qf", [128, NH * 256], BF16)
    kf = Tn("kf", [128, NH * 256], BF16)
    kt = Tn("kt", [128, 2 * D], BF16)
    vext = [Tn("vext0", [128, NH * 129], BF16), Tn("vext1", [128, NH * 129], BF16)]
    x0 = Tn("x0", [128, 2 * D], F32)
    wout = Tn("wout", [128, 2 * NH * 512], BF16)
    numfm = Tn("numfm", [128, NH * 256], BF16)
    attsb0 = Tn("attsb0", [128, 256], BF16)
    attsb1 = Tn("attsb1", [128, 128], BF16)
    numn = Tn("numn", [128, 256], BF16)
    szf = Tn("szf", [128, NH * 129], F32)
    szb = Tn("szb", [128, NH * 129], BF16)
    szj = Tn("szj", [128, NH * 129], F32)
    s1 = Tn("s1", [128, 2 * D], F32)
    sqscr = Tn("sqscr", [128, D], F32)
    x1tf = Tn("x1tf", [128, D], F32)
    x1tb = Tn("x1tb", [128, D], BF16)
    x1ff = Tn("x1ff", [128, 16 * 256], F32)
    x1fb = Tn("x1fb", [128, 16 * 256], BF16)
    wg = Tn("wg", [128, 16 * E], F32)
    zsb = Tn("zsb", [8, TPC], F32)
    mask = Tn("mask", [128, 256], BF16)
    idn = Tn("idn", [128, 128], BF16)
    idnf = Tn("idnf", [128, 128], F32)
    ut = Tn("ut", [128, 128], F32)
    onesr = Tn("onesr", [1, 128], F32)
    iota8 = Tn("iota8", [128, E], F32)
    cprefb = Tn("cprefb", [128, NC], F32)
    rd = [Tn("rd0", [128, 1], F32), Tn("rd1", [128, 1], F32)]
    dencp = Tn("dencp", [128, 4], F32)
    stats = Tn("stats", [128, 8], F32)
    ztile = Tn("ztile", [128, D], BF16)
    xrow = Tn("xrow", [128, D], BF16)
    zt = Tn("zt", [128, E], F32)
    oh0 = Tn("oh0", [128, E], F32)
    ohsc = Tn("ohsc", [128, E], F32)
    base8 = Tn("base8", [1, E], F32)
    negm = Tn("negm", [128, 1], F32)
    esum = Tn("esum", [128, 1], F32)
    escr8 = Tn("escr8", [128, E], F32)
    possel = Tn("possel", [128, 1], F32)
    q1 = Tn("q1", [128, 1], F32)
    okm = Tn("okm", [128, 1], F32)
    gate16 = Tn("gate16", [128, 16], F32)
    eidx16 = Tn("eidx16", [128, 16], F32)
    lpos16 = Tn("lpos16", [128, 16], F32)
    pidx16 = Tn("pidx16", [128, 16], I32)
    if debug:
        dbatt = Tn("dbatt", [128, 384], BF16)
        dbnum = Tn("dbnum", [128, 258], F32)
        dbrd = Tn("dbrd", [128, 2], F32)
        dbnumn = Tn("dbnumn", [128, 256], BF16)

    patt = P("patt", [128, 384], F32)
    patt0 = patt[:, 0:256]
    patt1 = patt[:, 256:384]
    pnum = P("pnum", [128, 2 * (HD + 1)], F32)
    pnum0 = pnum[:, 0:HD + 1]
    pnum1 = pnum[:, HD + 1:2 * (HD + 1)]
    ptpt = P("ptpt", [128, 256], BF16)
    ptp = [ptpt[:, 0:128], ptpt[:, 128:256]]
    pout = [P("pout0", [128, 512], F32), P("pout1", [128, 512], F32)]
    px1t = [P("px1t0", [128, 128], F32), P("px1t1", [128, 128], F32)]
    pbig = P("pbig", [128, 512], F32)
    pz = pbig[0:8, 0:256]
    pszd = pbig[:, 256:256 + HD + 1]
    pcum = pbig[:, 392:392 + E]
    pt8 = pbig[0:1, 400:400 + E]

    es = contextlib.ExitStack()
    sgp = es.enter_context(nc.semaphore("sgp"))
    sld = es.enter_context(nc.semaphore("sld"))
    spe = es.enter_context(nc.semaphore("spe"))
    sdv = es.enter_context(nc.semaphore("sdv"))
    sac = es.enter_context(nc.semaphore("sac"))
    sst = es.enter_context(nc.semaphore("sst"))

    cnt = {"gp": 0, "ld": 0, "pe": 0, "dv": 0, "ac": 0, "st": 0}
    M = {}

    def bar(dv, inst):
        inst.then_inc(sdv, 1)
        cnt["dv"] += 1
        dv.wait_ge(sdv, cnt["dv"])

    with nc.Block() as block:
        @block.sync
        def _(sy):
            for ap_s, ap_d in ((MASK, mask), (IDN, idn), (IDNF, idnf), (UT, ut),
                               (ONESR, onesr), (IOTA8, iota8), (CPREFB, cprefb)):
                sy.dma_start(out=ap_d[:], in_=ap_s).then_inc(sld, 16)
                cnt["ld"] += 16
            sy.dma_start(out=wg[:], in_=WG.rearrange("(c p) e -> p c e", p=128)).then_inc(sld, 16)
            cnt["ld"] += 16
            M["consts"] = cnt["ld"]
            for j in range(NC):
                if j > 0:
                    sy.wait_ge(sdv, 1 + 16 * j)
                sy.dma_start(out=szj[:],
                             in_=SZALL[j].rearrange("h p e -> p h e")).then_inc(sld, 16)
                cnt["ld"] += 16
                M[("szjld", j)] = cnt["ld"]

        @block.vector
        def _(dv):
            dv.memset(szf[:], 0.0).then_inc(sdv, 1)
            cnt["dv"] += 1
            for j in range(NC):
                dv.wait_ge(sld, M[("szjld", j)])
                for h in range(NH):
                    sl = slice(h * 129, (h + 1) * 129)
                    dv.tensor_scalar(out=szj[:, sl], in0=szj[:, sl],
                                     scalar1=cprefb[:, j:j + 1], scalar2=None,
                                     op0=ALU.mult)
                    dv.tensor_add(out=szf[:, sl], in0=szf[:, sl], in1=szj[:, sl]).then_inc(sdv, 1)
                    cnt["dv"] += 1
            dv.tensor_copy(out=szb[:], in_=szf[:]).then_inc(sdv, 1)
            cnt["dv"] += 1
            dv.memset(vext[0][:], 1.0)
            dv.memset(vext[1][:], 1.0)
            dv.memset(ztile[:], 0.0)
            dv.memset(base8[:], 0.0).then_inc(sdv, 1)
            cnt["dv"] += 1
            M["szinit"] = cnt["dv"]

        if debug:
            @block.sync
            def _(sy):
                sy.wait_ge(sdv, M["szinit"])
                sy.dma_start(out=DB_SZB, in_=szb[:]).then_inc(sst, 16)
                cnt["st"] += 16
                M["dbszb"] = cnt["st"]

        @block.gpsimd
        def _(gp):
            gp.wait_ge(sdv, M["szinit"])
            for b in range(32):
                gp.dma_start(out=PBUF[b * 128:(b + 1) * 128, :], in_=ztile[:]).then_inc(sgp, 16)
                cnt["gp"] += 16
            gp.dma_start(out=PBUF[E * CLOC:E * CLOC + 1, :], in_=ztile[0:1, :]).then_inc(sgp, 16)
            cnt["gp"] += 16
            M["pbufz"] = cnt["gp"]

        for c in range(NCH):
            @block.sync
            def _(sy, c=c):
                if c > 0:
                    sy.wait_ge(sdv, M[("chunkdone", c - 1)])
                    sy.wait_ge(sst, cnt["st"])
                else:
                    sy.wait_ge(sdv, M["szinit"])
                sy.dma_start(out=qf[:].rearrange("p (h t) -> p h t", t=256),
                             in_=QF[:, c * 256:(c + 1) * 256].rearrange("(h p) t -> p h t", p=128)).then_inc(sld, 16)
                sy.dma_start(out=kf[:].rearrange("p (h t) -> p h t", t=256),
                             in_=KF[:, c * 256:(c + 1) * 256].rearrange("(h p) t -> p h t", p=128)).then_inc(sld, 16)
                sy.dma_start(out=kt[:].rearrange("p (m d) -> p m d", d=D),
                             in_=KT[c * 256:(c + 1) * 256, :].rearrange("(m p) d -> p m d", p=128)).then_inc(sld, 16)
                cnt["ld"] += 48
                for mt in range(2):
                    sy.dma_start(
                        out=vext[mt][:].rearrange("p (h e) -> p h e", e=129)[:, :, 0:HD],
                        in_=VT[c * 256 + mt * 128: c * 256 + (mt + 1) * 128, :].rearrange("p (h d) -> p h d", d=HD),
                    ).then_inc(sld, 16)
                    cnt["ld"] += 16
                sy.dma_start(out=x0[:].rearrange("p (m d) -> p m d", d=D),
                             in_=X[c * 256:(c + 1) * 256, :].rearrange("(m p) d -> p m d", p=128)).then_inc(sld, 16)
                cnt["ld"] += 16
                M[("chld", c)] = cnt["ld"]

            for h in range(NH):
                pe0 = cnt["pe"]
                dv0 = cnt["dv"]
                ac0 = cnt["ac"]

                @block.tensor
                def _(te, c=c, h=h, pe0=pe0, dv0=dv0, ac0=ac0):
                    qsl = slice(h * 256, h * 256 + 256)
                    q0 = slice(h * 256, h * 256 + 128)
                    q1s = slice(h * 256 + 128, h * 256 + 256)
                    vsl = slice(h * 129, (h + 1) * 129)
                    te.wait_ge(sld, M[("chld", c)])
                    te.wait_ge(sdv, M["szinit"])
                    if c > 0:
                        te.wait_ge(sdv, M[("szcast", c - 1, h)])
                    prevmask = M.get(("maskev", c, h - 1), M.get(("maskev", c - 1, NH - 1), 0))
                    te.wait_ge(sdv, prevmask)
                    te.matmul(out=patt0, lhsT=kf[:, q0], rhs=qf[:, qsl],
                              start=True, stop=True).then_inc(spe, 1)
                    te.matmul(out=patt1, lhsT=kf[:, q1s], rhs=qf[:, q1s],
                              start=True, stop=True).then_inc(spe, 1)
                    cnt["pe"] += 2
                    te.wait_ge(sdv, dv0 + 1)
                    te.wait_ge(sac, ac0)
                    te.matmul(out=pnum0, lhsT=attsb0[:, 0:128], rhs=vext[0][:, vsl],
                              start=True, stop=False)
                    te.matmul(out=pnum0, lhsT=qf[:, q0], rhs=szb[:, vsl],
                              start=False, stop=True).then_inc(spe, 1)
                    te.matmul(out=pnum1, lhsT=attsb0[:, 128:256], rhs=vext[0][:, vsl],
                              start=True, stop=False)
                    te.matmul(out=pnum1, lhsT=attsb1[:], rhs=vext[1][:, vsl],
                              start=False, stop=False)
                    te.matmul(out=pnum1, lhsT=qf[:, q1s], rhs=szb[:, vsl],
                              start=False, stop=True).then_inc(spe, 1)
                    cnt["pe"] += 2
                    prevszc = M.get(("szcast", c, h - 1), M.get(("szcast", c - 1, NH - 1), 0))
                    te.wait_ge(sdv, prevszc)
                    te.matmul(out=pszd, lhsT=kt[:, h * 128:(h + 1) * 128],
                              rhs=vext[0][:, vsl], start=True, stop=False)
                    te.matmul(out=pszd, lhsT=kt[:, D + h * 128:D + (h + 1) * 128],
                              rhs=vext[1][:, vsl], start=False, stop=True).then_inc(spe, 1)
                    cnt["pe"] += 1
                    te.wait_ge(sac, ac0 + 2)
                    te.transpose(out=ptp[0], in_=numn[:, 0:128], identity=idn[:]).then_inc(spe, 1)
                    te.transpose(out=ptp[1], in_=numn[:, 128:256], identity=idn[:]).then_inc(spe, 1)
                    cnt["pe"] += 2

                @block.vector
                def _(dv, c=c, h=h, pe0=pe0):
                    dv.wait_ge(spe, pe0 + 2)
                    dv.tensor_mul(out=attsb0[:], in0=patt0, in1=mask[:])
                    dv.tensor_mul(out=attsb1[:], in0=patt1, in1=mask[:, 0:128]).then_inc(sdv, 1)
                    cnt["dv"] += 1
                    M[("maskev", c, h)] = cnt["dv"]
                    if debug and c == 0 and h == 0:
                        dv.tensor_copy(out=dbatt[:, 0:256], in_=attsb0[:])
                        dv.tensor_copy(out=dbatt[:, 256:384], in_=attsb1[:])
                    dv.wait_ge(spe, pe0 + 4)
                    dv.tensor_copy(out=dencp[:, 0:2], in_=pnum0[:, HD - 1:HD + 1])
                    bar(dv, dv.tensor_copy(out=dencp[:, 2:4], in_=pnum1[:, HD - 1:HD + 1]))
                    dv.tensor_scalar_max(out=rd[0][:], in0=dencp[:, 1:2], scalar1=1e-6)
                    bar(dv, dv.tensor_scalar_max(out=rd[1][:], in0=dencp[:, 3:4], scalar1=1e-6))
                    dv.reciprocal(out=rd[0][:], in_=rd[0][:])
                    dv.reciprocal(out=rd[1][:], in_=rd[1][:]).then_inc(sdv, 1)
                    cnt["dv"] += 1
                    M[("denrd", c, h)] = cnt["dv"]
                    if debug and c == 0 and h == 0:
                        dv.tensor_copy(out=dbnum[:, 0:129], in_=pnum0)
                        dv.tensor_copy(out=dbnum[:, 129:258], in_=pnum1)
                        dv.tensor_copy(out=dbrd[:, 0:1], in_=rd[0][:])
                        dv.tensor_copy(out=dbrd[:, 1:2], in_=rd[1][:])
                    if c == 0 and h == 0 and "dbszb" in M:
                        dv.wait_ge(sst, M["dbszb"])
                    dv.wait_ge(spe, pe0 + 5)
                    vsl = slice(h * 129, (h + 1) * 129)
                    bar(dv, dv.tensor_add(out=szf[:, vsl], in0=szf[:, vsl], in1=pszd))
                    dv.tensor_copy(out=szb[:, vsl], in_=szf[:, vsl]).then_inc(sdv, 1)
                    cnt["dv"] += 1
                    M[("szcast", c, h)] = cnt["dv"]
                    dv.wait_ge(spe, pe0 + 7)
                    if debug and c == 0 and h == 0:
                        dv.tensor_copy(out=dbnumn[:], in_=numn[:])
                    dv.tensor_copy(out=numfm[:, h * 256:h * 256 + 128], in_=ptp[0])
                    dv.tensor_copy(out=numfm[:, h * 256 + 128:h * 256 + 256], in_=ptp[1]).then_inc(sdv, 1)
                    cnt["dv"] += 1
                    M[("numfm", c, h)] = cnt["dv"]

                @block.scalar
                def _(ac, c=c, h=h):
                    ac.wait_ge(sdv, M[("denrd", c, h)])
                    ac.activation(out=numn[:, 0:128], in_=pnum0[:, 0:HD], func=AF.Copy,
                                  scale=rd[0][:, 0:1]).then_inc(sac, 1)
                    ac.activation(out=numn[:, 128:256], in_=pnum1[:, 0:HD], func=AF.Copy,
                                  scale=rd[1][:, 0:1]).then_inc(sac, 1)
                    cnt["ac"] += 2

            if debug and c == 0:
                @block.sync
                def _(sy):
                    sy.wait_ge(sdv, M[("numfm", 0, 0)])
                    sy.dma_start(out=DB_ATT, in_=dbatt[:]).then_inc(sst, 16)
                    sy.dma_start(out=DB_NUM, in_=dbnum[:]).then_inc(sst, 16)
                    sy.dma_start(out=DB_RD, in_=dbrd[:]).then_inc(sst, 16)
                    sy.dma_start(out=DB_NUMN, in_=dbnumn[:]).then_inc(sst, 16)
                    cnt["st"] += 64

                @block.sync
                def _(sy):
                    sy.wait_ge(sdv, M[("numfm", 0, NH - 1)])
                    sy.dma_start(out=DB_NUMF, in_=numfm[:]).then_inc(sst, 16)
                    cnt["st"] += 16

            dv1 = cnt["dv"]
            pe1 = cnt["pe"]
            for oc in range(4):
                @block.gpsimd
                def _(gp, c=c, oc=oc):
                    slot = oc % 2
                    key = ("woutfree", slot)
                    if key in M:
                        gp.wait_ge(spe, M[key])
                    gp.dma_start(
                        out=wout[:, slot * NH * 512:(slot + 1) * NH * 512].rearrange("p (hb f) -> p hb f", f=512),
                        in_=WOUTT[:, oc * 512:(oc + 1) * 512].rearrange("(hb p) f -> p hb f", p=128),
                    ).then_inc(sgp, 16)
                    cnt["gp"] += 16
                    M[("woutld", c, oc)] = cnt["gp"]

                @block.tensor
                def _(te, c=c, oc=oc, pe1=pe1, dv1=dv1):
                    te.wait_ge(sdv, M[("numfm", c, NH - 1)])
                    te.wait_ge(sgp, M[("woutld", c, oc)])
                    slot = oc % 2
                    for lt in range(2):
                        k = 2 * oc + lt
                        if k >= 2:
                            te.wait_ge(sdv, dv1 + k - 1)
                        last = None
                        for h in range(NH):
                            last = te.matmul(
                                out=pout[lt][:],
                                lhsT=numfm[:, h * 256 + lt * 128: h * 256 + (lt + 1) * 128],
                                rhs=wout[:, slot * NH * 512 + h * 512: slot * NH * 512 + (h + 1) * 512],
                                start=(h == 0), stop=(h == NH - 1),
                            )
                        last.then_inc(spe, 1)
                        cnt["pe"] += 1
                    M[("woutfree", slot)] = cnt["pe"]

            @block.vector
            def _(dv, c=c, pe1=pe1):
                for oc in range(4):
                    for lt in range(2):
                        k = 2 * oc + lt
                        dv.wait_ge(spe, pe1 + k + 1)
                        dv.scalar_tensor_tensor(
                            out=s1[:, lt * D + oc * 512: lt * D + (oc + 1) * 512],
                            in0=pout[lt][:], scalar=1.0,
                            in1=x0[:, lt * D + oc * 512: lt * D + (oc + 1) * 512],
                            op0=ALU.mult, op1=ALU.add,
                        ).then_inc(sdv, 1)
                        cnt["dv"] += 1
                M[("s1done", c)] = cnt["dv"]

            if debug and c == 0:
                @block.sync
                def _(sy):
                    sy.wait_ge(sdv, M[("s1done", 0)])
                    sy.dma_start(out=DB_S1, in_=s1[:]).then_inc(sst, 16)
                    cnt["st"] += 16

            for lt in range(2):
                dv2 = cnt["dv"]
                ac2 = cnt["ac"]
                pe2 = cnt["pe"]

                @block.vector
                def _(dv, c=c, lt=lt, ac2=ac2, pe2=pe2):
                    dv.wait_ge(sdv, M[("s1done", c)])
                    dv.reduce_sum(out=stats[:, 0:1], in_=s1[:, lt * D:(lt + 1) * D], axis=AX.X).then_inc(sdv, 1)
                    cnt["dv"] += 1
                    dv.wait_ge(sac, ac2 + 1)
                    dv.tensor_scalar_mul(out=stats[:, 0:1], in0=stats[:, 0:1], scalar1=1.0 / D)
                    bar(dv, dv.tensor_scalar_mul(out=stats[:, 1:2], in0=stats[:, 1:2], scalar1=1.0 / D))
                    bar(dv, dv.tensor_mul(out=stats[:, 6:7], in0=stats[:, 0:1], in1=stats[:, 0:1]))
                    bar(dv, dv.tensor_sub(out=stats[:, 2:3], in0=stats[:, 1:2], in1=stats[:, 6:7]))
                    dv.tensor_scalar_add(out=stats[:, 2:3], in0=stats[:, 2:3], scalar1=1e-5).then_inc(sdv, 1)
                    cnt["dv"] += 1
                    M[("var", c, lt)] = cnt["dv"]
                    dv.wait_ge(sac, ac2 + 2)
                    dv.tensor_copy(out=stats[:, 4:5], in_=stats[:, 3:4]).then_inc(sdv, 1)
                    cnt["dv"] += 1
                    M[("lnv2", c, lt)] = cnt["dv"]
                    dv.wait_ge(sac, ac2 + 3)
                    dv.wait_ge(sdv, cnt["dv"])
                    dv.tensor_scalar(out=x1tf[:], in0=s1[:, lt * D:(lt + 1) * D],
                                     scalar1=stats[:, 0:1], scalar2=stats[:, 5:6],
                                     op0=ALU.subtract, op1=ALU.mult).then_inc(sdv, 1)
                    cnt["dv"] += 1
                    dv.wait_ge(sst, M.get("x1stash", 0))
                    dv.tensor_copy(out=x1tb[:], in_=x1tf[:]).then_inc(sdv, 1)
                    cnt["dv"] += 1
                    M[("x1tb", c, lt)] = cnt["dv"]
                    for c16 in range(16):
                        dv.wait_ge(spe, pe2 + c16 + 1)
                        dv.tensor_copy(out=x1ff[:, c16 * 256 + lt * 128: c16 * 256 + lt * 128 + 128],
                                       in_=px1t[c16 % 2][:]).then_inc(sdv, 1)
                        cnt["dv"] += 1
                    M[("x1ff", c, lt)] = cnt["dv"]

                @block.scalar
                def _(ac, c=c, lt=lt, dv2=dv2):
                    ac.wait_ge(sdv, dv2 + 1)
                    ac.activation(out=sqscr[:], in_=s1[:, lt * D:(lt + 1) * D], func=AF.Square,
                                  accum_out=stats[:, 1:2]).then_inc(sac, 1)
                    ac.wait_ge(sdv, M[("var", c, lt)])
                    ac.activation(out=stats[:, 3:4], in_=stats[:, 2:3], func=AF.Ln).then_inc(sac, 1)
                    ac.wait_ge(sdv, M[("lnv2", c, lt)])
                    ac.activation(out=stats[:, 5:6], in_=stats[:, 4:5], func=AF.Exp,
                                  scale=-0.5).then_inc(sac, 1)
                    cnt["ac"] += 3

                @block.sync
                def _(sy, c=c, lt=lt):
                    sy.wait_ge(sdv, M[("x1tb", c, lt)])
                    sy.dma_start(out=X1T[c * 256 + lt * 128: c * 256 + (lt + 1) * 128, :],
                                 in_=x1tb[:]).then_inc(sst, 16)
                    cnt["st"] += 16
                    M["x1stash"] = cnt["st"]
                    M[("x1stash", c, lt)] = cnt["st"]

                @block.tensor
                def _(te, c=c, lt=lt, pe2=pe2):
                    te.wait_ge(sdv, M[("x1tb", c, lt)])
                    base_dv = M[("x1tb", c, lt)]
                    for c16 in range(16):
                        if c16 >= 2:
                            te.wait_ge(sdv, base_dv + c16 - 1)
                        te.transpose(out=px1t[c16 % 2][:],
                                     in_=x1tf[:, c16 * 128:(c16 + 1) * 128],
                                     identity=idnf[:]).then_inc(spe, 1)
                        cnt["pe"] += 1

            @block.tensor
            def _(te, c=c):
                te.wait_ge(sdv, M[("x1ff", c, 1)])
                last = None
                for c16 in range(16):
                    last = te.matmul(out=pz, lhsT=wg[:, c16 * E:(c16 + 1) * E],
                                     rhs=x1ff[:, c16 * 256:(c16 + 1) * 256],
                                     start=(c16 == 0), stop=(c16 == 15))
                last.then_inc(spe, 1)
                cnt["pe"] += 1
                M[("zmm", c)] = cnt["pe"]

            @block.vector
            def _(dv, c=c):
                dv.wait_ge(spe, M[("zmm", c)])
                dv.wait_ge(sdv, M[("x1ff", c, 1)])
                dv.tensor_copy(out=zsb[:, c * 256:(c + 1) * 256], in_=pz).then_inc(sdv, 1)
                cnt["dv"] += 1
                dv.tensor_copy(out=x1fb[:], in_=x1ff[:]).then_inc(sdv, 1)
                cnt["dv"] += 1
                M[("x1fb", c)] = cnt["dv"]
                M[("chunkdone", c)] = cnt["dv"]

            @block.sync
            def _(sy, c=c):
                sy.wait_ge(sdv, M[("x1fb", c)])
                sy.dma_start(out=X1F[:, c * 256:(c + 1) * 256].rearrange("(c16 p) t -> p c16 t", p=128),
                             in_=x1fb[:].rearrange("p (c16 t) -> p c16 t", t=256)).then_inc(sst, 16)
                cnt["st"] += 16
                M[("x1fstash", c)] = cnt["st"]

        if debug:
            @block.sync
            def _(sy):
                sy.wait_ge(sdv, M[("chunkdone", NCH - 1)])
                sy.dma_start(out=DB_ZSB, in_=zsb[:]).then_inc(sst, 16)
                cnt["st"] += 16

        for tt in range(NTT):
            pe4 = cnt["pe"]
            dv4 = cnt["dv"]
            ac4 = cnt["ac"]

            @block.tensor
            def _(te, tt=tt):
                te.wait_ge(sdv, M[("chunkdone", NCH - 1)])
                if tt > 0:
                    te.wait_ge(sdv, M[("pidx", tt - 1)])
                te.transpose(out=pcum, in_=zsb[0:8, tt * 128:(tt + 1) * 128],
                             identity=idnf[0:8, 0:8]).then_inc(spe, 1)
                cnt["pe"] += 1

            @block.vector
            def _(dv, tt=tt, pe4=pe4, ac4=ac4):
                dv.wait_ge(spe, pe4 + 1)
                bar(dv, dv.tensor_copy(out=zt[:], in_=pcum))
                cnt["dv"] += 0
                M[("ztev", tt)] = cnt["dv"]
                bar(dv, dv.reduce_max(out=negm[:], in_=zt[:], axis=AX.X, negate=True))
                sdvmark = cnt["dv"]
                dv.then_maybe_inc(None) if False else None
                ncnt = cnt["dv"] + 1
                cnt["dv"] = ncnt - 1
                dv.sem_inc(sdv, 1)
                cnt["dv"] += 1
                M[("negmready", tt)] = cnt["dv"]
                dv.wait_ge(sac, ac4 + 1)
                bar(dv, dv.reciprocal(out=gate16[:, tt:tt + 1], in_=esum[:]))
                bar(dv, dv.tensor_scalar(out=oh0[:], in0=zt[:], scalar1=negm[:, 0:1], scalar2=0.0,
                                         op0=ALU.add, op1=ALU.is_ge))
                bar(dv, dv.tensor_tensor_scan(out=ohsc[:], data0=oh0[:], data1=oh0[:],
                                              initial=0.0, op0=ALU.add, op1=ALU.bypass))
                bar(dv, dv.tensor_single_scalar(out=ohsc[:], in_=ohsc[:], scalar=1.0, op=ALU.is_equal))
                bar(dv, dv.tensor_mul(out=oh0[:], in0=oh0[:], in1=ohsc[:]))
                bar(dv, dv.tensor_mul(out=ohsc[:], in0=oh0[:], in1=iota8[:]))
                dv.reduce_sum(out=eidx16[:, tt:tt + 1], in_=ohsc[:], axis=AX.X).then_inc(sdv, 1)
                cnt["dv"] += 1
                M[("ohready", tt)] = cnt["dv"]

            @block.scalar
            def _(ac, tt=tt):
                ac.wait_ge(sdv, M[("negmready", tt)])
                ac.activation(out=escr8[:], in_=zt[:], func=AF.Exp, bias=negm[:, 0:1],
                              accum_out=esum[:, 0:1]).then_inc(sac, 1)
                cnt["ac"] += 1

            @block.tensor
            def _(te, tt=tt):
                te.wait_ge(sdv, M[("ohready", tt)])
                te.matmul(out=pcum, lhsT=ut[:], rhs=oh0[:], start=True, stop=False)
                te.matmul(out=pcum, lhsT=onesr[:], rhs=base8[:], start=False,
                          stop=True).then_inc(spe, 1)
                te.matmul(out=pt8, lhsT=ut[:, 127:128], rhs=oh0[:], start=True,
                          stop=True).then_inc(spe, 1)
                cnt["pe"] += 2
                M[("cummm", tt)] = cnt["pe"]

            @block.vector
            def _(dv, tt=tt):
                dv.wait_ge(spe, M[("cummm", tt)])
                bar(dv, dv.tensor_mul(out=ohsc[:], in0=oh0[:], in1=pcum))
                bar(dv, dv.reduce_sum(out=possel[:], in_=ohsc[:], axis=AX.X))
                bar(dv, dv.tensor_add(out=base8[:], in0=base8[:], in1=pt8))
                bar(dv, dv.tensor_scalar_sub(out=lpos16[:, tt:tt + 1], in0=possel[:], scalar1=1.0))
                bar(dv, dv.tensor_single_scalar(out=okm[:], in_=possel[:], scalar=float(CLOC), op=ALU.is_le))
                bar(dv, dv.scalar_tensor_tensor(out=q1[:], in0=eidx16[:, tt:tt + 1], scalar=float(CLOC),
                                                in1=possel[:], op0=ALU.mult, op1=ALU.add))
                bar(dv, dv.tensor_scalar_sub(out=q1[:], in0=q1[:], scalar1=float(E * CLOC + 1)))
                bar(dv, dv.tensor_mul(out=q1[:], in0=q1[:], in1=okm[:]))
                bar(dv, dv.tensor_scalar_add(out=q1[:], in0=q1[:], scalar1=float(E * CLOC)))
                dv.tensor_copy(out=pidx16[:, tt:tt + 1], in_=q1[:]).then_inc(sdv, 1)
                cnt["dv"] += 1
                M[("pidx", tt)] = cnt["dv"]

        @block.gpsimd
        def _(gp):
            gp.wait_ge(sgp, M["pbufz"])
            gp.wait_ge(sst, M[("x1stash", NCH - 1, 1)])
            for tt in range(NTT):
                gp.wait_ge(sdv, M[("pidx", tt)])
                gp.dma_start(out=xrow[:], in_=X1T[tt * 128:(tt + 1) * 128, :]).then_inc(sgp, 16)
                cnt["gp"] += 16
                gp.wait_ge(sgp, cnt["gp"])
                gp.indirect_dma_start(
                    out=PBUF, out_offset=IndirectOffsetOnAxis(pidx16[:, tt:tt + 1], 0),
                    in_=xrow[:], in_offset=None,
                ).then_inc(sgp, 16)
                cnt["gp"] += 16
                gp.wait_ge(sgp, cnt["gp"])
            M["scatdone"] = cnt["gp"]

        @block.sync
        def _(sy):
            sy.wait_ge(sdv, M[("pidx", NTT - 1)])
            sy.dma_start(out=CNT, in_=base8[:]).then_inc(sst, 16)
            sy.dma_start(out=EIDX, in_=eidx16[:]).then_inc(sst, 16)
            sy.dma_start(out=LPOS, in_=lpos16[:]).then_inc(sst, 16)
            sy.dma_start(out=GATE, in_=gate16[:]).then_inc(sst, 16)
            cnt["st"] += 64
            sy.wait_ge(sst, cnt["st"])
            sy.wait_ge(sgp, M["scatdone"])

    es.close()
    return nc


def build_l3():
    nc = bass.Bass("TRN2", target_bir_lowering=False)
    RUNS = nc.dram_tensor("RUNS", [NC * CLOC, D], BF16, kind="ExternalInput").ap()
    CNTA = nc.dram_tensor("CNTA", [NC, E], F32, kind="ExternalInput").ap()
    X1F = nc.dram_tensor("X1F", [D, TPC], BF16, kind="ExternalInput").ap()
    X1T = nc.dram_tensor("X1T", [TPC, D], BF16, kind="ExternalInput").ap()
    W1E = nc.dram_tensor("W1E", [D, HF], F32, kind="ExternalInput").ap()
    W3E = nc.dram_tensor("W3E", [D, HF], F32, kind="ExternalInput").ap()
    W2E = nc.dram_tensor("W2E", [HF, D], F32, kind="ExternalInput").ap()
    SW1 = nc.dram_tensor("SW1", [D, HF], F32, kind="ExternalInput").ap()
    SW3 = nc.dram_tensor("SW3", [D, HF], F32, kind="ExternalInput").ap()
    SW2 = nc.dram_tensor("SW2", [HF, D], F32, kind="ExternalInput").ap()
    IDN = nc.dram_tensor("IDN", [128, 128], BF16, kind="ExternalInput").ap()
    EJC = nc.dram_tensor("EJC", [8, NC * 128], F32, kind="ExternalInput").ap()
    SELB = nc.dram_tensor("SELB", [8, E], F32, kind="ExternalInput").ap()
    SLT = nc.dram_tensor("SLT", [8, 8], F32, kind="ExternalInput").ap()
    LPC = nc.dram_tensor("LPC", [128, 4], F32, kind="ExternalInput").ap()
    SLC = nc.dram_tensor("SLC", [128, 16], F32, kind="ExternalInput").ap()

    RETP = nc.dram_tensor("RETP", [NC * CLOC + 1, D], BF16, kind="ExternalOutput").ap()
    S2P = nc.dram_tensor("S2P", [TPC, D], F32, kind="ExternalOutput").ap()

    XD = nc.dram_tensor("XD", [CAP + 1, D], BF16, kind="Internal").ap()

    NHC = HF // 128

    Tn = nc.alloc_sbuf_tensor
    P = nc.alloc_psum_tensor
    cnta = Tn("cnta", [8, E], F32)
    selb = Tn("selb", [8, E], F32)
    slt = Tn("slt", [8, 8], F32)
    ejc = Tn("ejc", [8, NC * 128], F32)
    lpc = Tn("lpc", [128, 4], F32)
    slc = Tn("slc", [128, 16], F32)
    idn = Tn("idn", [128, 128], BF16)
    scr8 = Tn("scr8", [8, E], F32)
    gcol = Tn("gcol", [8, 2], F32)
    gbb = Tn("gbb", [128, 16], F32)
    ubb = Tn("ubb", [128, 8], F32)
    t1 = Tn("t1", [128, 1], F32)
    t2 = Tn("t2", [128, 1], F32)
    t3 = Tn("t3", [128, 1], F32)
    acc = Tn("acc", [128, 1], F32)
    vany = Tn("vany", [128, 1], F32)
    desti = Tn("desti", [128, 32], I32)
    retidx = Tn("retidx", [128, 16], I32)
    ztile = Tn("ztile", [128, D], BF16)
    rbuf = Tn("rbuf", [128, D], BF16)
    xsf = Tn("xsf", [128, 4 * D], BF16)
    xdf = Tn("xdf", [128, 16 * 512], BF16)
    hfm = Tn("hfm", [128, NHC * 512], BF16)
    hs = Tn("hs", [128, 512], BF16)
    w13 = Tn("w13", [128, 2 * 2048], BF16)
    w2s = Tn("w2s", [128, NHC * 512], BF16)
    oesb = Tn("oesb", [128, 4 * D], BF16)
    s2all = Tn("s2all", [128, 4 * D], F32)
    x1r4 = Tn("x1r4", [128, 4 * D], BF16)

    paux = P("paux", [128, 512], F32)
    pgb = paux[0:8, 0:8]
    pbc = paux[:, 8:24]
    h1p = P("h1p", [128, 512], F32)
    h3p = P("h3p", [128, 512], F32)
    oep = P("oep", [128, 512], F32)
    ptpa = P("ptpa", [128, 128], BF16)
    ptpb = P("ptpb", [128, 128], BF16)
    ptp2 = [ptpa, ptpb]

    es = contextlib.ExitStack()
    sgp = es.enter_context(nc.semaphore("sgp"))
    sld = es.enter_context(nc.semaphore("sld"))
    spe = es.enter_context(nc.semaphore("spe"))
    sdv = es.enter_context(nc.semaphore("sdv"))
    sac = es.enter_context(nc.semaphore("sac"))
    sst = es.enter_context(nc.semaphore("sst"))

    cnt = {"gp": 0, "ld": 0, "pe": 0, "dv": 0, "ac": 0, "st": 0}
    M = {}

    def bar(dv, inst):
        inst.then_inc(sdv, 1)
        cnt["dv"] += 1
        dv.wait_ge(sdv, cnt["dv"])

    with nc.Block() as block:
        @block.sync
        def _(sy):
            for ap_s, ap_d in ((CNTA, cnta), (SELB, selb), (SLT, slt), (EJC, ejc),
                               (LPC, lpc), (SLC, slc), (IDN, idn)):
                sy.dma_start(out=ap_d[:], in_=ap_s).then_inc(sld, 16)
                cnt["ld"] += 16
            M["consts"] = cnt["ld"]

        @block.tensor
        def _(te):
            te.wait_ge(sld, M["consts"])
            te.matmul(out=pgb, lhsT=slt[:], rhs=cnta[:], start=True, stop=True).then_inc(spe, 1)
            cnt["pe"] += 1
            M["gbmm"] = cnt["pe"]

        @block.vector
        def _(dv):
            dv.wait_ge(spe, M["gbmm"])
            bar(dv, dv.tensor_mul(out=scr8[:], in0=pgb, in1=selb[:]))
            bar(dv, dv.reduce_sum(out=gcol[0:8, 0:1], in_=scr8[:], axis=AX.X))
            bar(dv, dv.tensor_mul(out=scr8[:], in0=cnta[:], in1=selb[:]))
            bar(dv, dv.reduce_sum(out=gcol[0:8, 1:2], in_=scr8[:], axis=AX.X))
            M["gcol"] = cnt["dv"]

        @block.tensor
        def _(te):
            te.wait_ge(sdv, M["gcol"])
            last = None
            for j in range(NC):
                te.matmul(out=pbc[:, j:j + 1], lhsT=ejc[:, j * 128:(j + 1) * 128],
                          rhs=gcol[0:8, 0:1], start=True, stop=True)
                last = te.matmul(out=pbc[:, 8 + j: 9 + j], lhsT=ejc[:, j * 128:(j + 1) * 128],
                                 rhs=gcol[0:8, 1:2], start=True, stop=True)
            last.then_inc(spe, 1)
            cnt["pe"] += 1
            M["bcmm"] = cnt["pe"]

        @block.vector
        def _(dv):
            dv.wait_ge(spe, M["bcmm"])
            bar(dv, dv.tensor_copy(out=gbb[:], in_=pbc))
            bar(dv, dv.tensor_scalar_min(out=ubb[:], in0=gbb[:, 8:16], scalar1=float(CLOC)))
            bar(dv, dv.tensor_add(out=ubb[:], in0=ubb[:], in1=gbb[:, 0:8]))
            for tt2 in range(32):
                j, q = tt2 // 4, tt2 % 4
                bar(dv, dv.tensor_add(out=t1[:], in0=lpc[:, q:q + 1], in1=gbb[:, j:j + 1]))
                bar(dv, dv.tensor_tensor(out=t2[:], in0=lpc[:, q:q + 1], in1=gbb[:, 8 + j:9 + j],
                                         op=ALU.is_lt))
                bar(dv, dv.tensor_single_scalar(out=t3[:], in_=t1[:], scalar=float(CAP), op=ALU.is_lt))
                bar(dv, dv.tensor_mul(out=t2[:], in0=t2[:], in1=t3[:]))
                bar(dv, dv.tensor_scalar_sub(out=t1[:], in0=t1[:], scalar1=float(CAP)))
                bar(dv, dv.tensor_mul(out=t1[:], in0=t1[:], in1=t2[:]))
                bar(dv, dv.tensor_scalar_add(out=t1[:], in0=t1[:], scalar1=float(CAP)))
                bar(dv, dv.tensor_copy(out=desti[:, tt2:tt2 + 1], in_=t1[:]))
            M["desti"] = cnt["dv"]
            for stt in range(16):
                bar(dv, dv.memset(acc[:], 0.0))
                bar(dv, dv.memset(vany[:], 0.0))
                for j in range(NC):
                    bar(dv, dv.tensor_tensor(out=t1[:], in0=slc[:, stt:stt + 1],
                                             in1=gbb[:, j:j + 1], op=ALU.is_ge))
                    bar(dv, dv.tensor_tensor(out=t2[:], in0=slc[:, stt:stt + 1],
                                             in1=ubb[:, j:j + 1], op=ALU.is_lt))
                    bar(dv, dv.tensor_mul(out=t1[:], in0=t1[:], in1=t2[:]))
                    bar(dv, dv.tensor_sub(out=t2[:], in0=slc[:, stt:stt + 1], in1=gbb[:, j:j + 1]))
                    bar(dv, dv.tensor_scalar_add(out=t2[:], in0=t2[:], scalar1=float(j * CLOC)))
                    bar(dv, dv.tensor_mul(out=t2[:], in0=t2[:], in1=t1[:]))
                    bar(dv, dv.tensor_add(out=acc[:], in0=acc[:], in1=t2[:]))
                    bar(dv, dv.tensor_add(out=vany[:], in0=vany[:], in1=t1[:]))
                bar(dv, dv.tensor_scalar(out=t1[:], in0=vany[:], scalar1=float(-NC * CLOC),
                                         scalar2=float(NC * CLOC), op0=ALU.mult, op1=ALU.add))
                bar(dv, dv.tensor_add(out=acc[:], in0=acc[:], in1=t1[:]))
                bar(dv, dv.tensor_copy(out=retidx[:, stt:stt + 1], in_=acc[:]))
            dv.memset(ztile[:], 0.0).then_inc(sdv, 1)
            cnt["dv"] += 1
            M["retidx"] = cnt["dv"]

        @block.gpsimd
        def _(gp):
            gp.wait_ge(sdv, M["retidx"])
            for b in range(16):
                gp.dma_start(out=XD[b * 128:(b + 1) * 128, :], in_=ztile[:]).then_inc(sgp, 16)
                cnt["gp"] += 16
            gp.dma_start(out=XD[CAP:CAP + 1, :], in_=ztile[0:1, :]).then_inc(sgp, 16)
            cnt["gp"] += 16
            for b in range(32):
                gp.dma_start(out=RETP[b * 128:(b + 1) * 128, :], in_=ztile[:]).then_inc(sgp, 16)
                cnt["gp"] += 16
            gp.dma_start(out=RETP[NC * CLOC:NC * CLOC + 1, :], in_=ztile[0:1, :]).then_inc(sgp, 16)
            cnt["gp"] += 16
            for tt2 in range(32):
                gp.dma_start(out=rbuf[:], in_=RUNS[tt2 * 128:(tt2 + 1) * 128, :]).then_inc(sgp, 16)
                cnt["gp"] += 16
                gp.wait_ge(sgp, cnt["gp"])
                gp.indirect_dma_start(
                    out=XD, out_offset=IndirectOffsetOnAxis(desti[:, tt2:tt2 + 1], 0),
                    in_=rbuf[:], in_offset=None,
                ).then_inc(sgp, 16)
                cnt["gp"] += 16
                gp.wait_ge(sgp, cnt["gp"])
            M["asm"] = cnt["gp"]

        def ffn_chunk(tag, sc, w1ap, w3ap, w2ap):
            for hc in range(NHC):
                @block.gpsimd
                def _(gp, hc=hc, tag=tag, sc=sc):
                    if ("hlast",) in M:
                        gp.wait_ge(sdv, M[("hlast",)])
                    gp.dma_start(
                        out=w13[:, 0:2048].rearrange("p (c f) -> p c f", f=128),
                        in_=w1ap[:, hc * 128:(hc + 1) * 128].rearrange("(c p) f -> p c f", p=128),
                    ).then_inc(sgp, 16)
                    gp.dma_start(
                        out=w13[:, 2048:4096].rearrange("p (c f) -> p c f", f=128),
                        in_=w3ap[:, hc * 128:(hc + 1) * 128].rearrange("(c p) f -> p c f", p=128),
                    ).then_inc(sgp, 16)
                    cnt["gp"] += 32
                    M[("w13", tag, sc, hc)] = cnt["gp"]

                @block.tensor
                def _(te, hc=hc, tag=tag, sc=sc):
                    te.wait_ge(sgp, M[("w13", tag, sc, hc)])
                    te.wait_ge(sdv, M[(tag, "xdf", sc)])
                    if ("hlast",) in M:
                        te.wait_ge(sdv, M[("hlast",)])
                    if ("oevprev",) in M:
                        te.wait_ge(sdv, M[("oevprev",)])
                    last = None
                    for c16 in range(16):
                        last = te.matmul(out=h1p[:], lhsT=w13[:, c16 * 128:(c16 + 1) * 128],
                                         rhs=xdf[:, c16 * 512:(c16 + 1) * 512],
                                         start=(c16 == 0), stop=(c16 == 15))
                    last.then_inc(spe, 1)
                    cnt["pe"] += 1
                    for c16 in range(16):
                        last = te.matmul(out=h3p[:], lhsT=w13[:, 2048 + c16 * 128: 2048 + (c16 + 1) * 128],
                                         rhs=xdf[:, c16 * 512:(c16 + 1) * 512],
                                         start=(c16 == 0), stop=(c16 == 15))
                    last.then_inc(spe, 1)
                    cnt["pe"] += 1
                    M[("hmm", tag, sc, hc)] = cnt["pe"]

                @block.scalar
                def _(ac, hc=hc, tag=tag, sc=sc):
                    ac.wait_ge(spe, M[("hmm", tag, sc, hc)] - 1)
                    ac.activation(out=hs[:], in_=h1p[:], func=AF.Silu).then_inc(sac, 1)
                    cnt["ac"] += 1
                    M[("silu", tag, sc, hc)] = cnt["ac"]

                @block.vector
                def _(dv, hc=hc, tag=tag, sc=sc):
                    dv.wait_ge(sac, M[("silu", tag, sc, hc)])
                    dv.wait_ge(spe, M[("hmm", tag, sc, hc)])
                    dv.tensor_mul(out=hfm[:, hc * 512:(hc + 1) * 512], in0=hs[:],
                                  in1=h3p[:]).then_inc(sdv, 1)
                    cnt["dv"] += 1
                    M[("hlast",)] = cnt["dv"]
            M[(tag, "hdone", sc)] = M[("hlast",)]

            for oc2 in range(4):
                @block.gpsimd
                def _(gp, oc2=oc2, tag=tag, sc=sc):
                    if ("w2cons",) in M:
                        gp.wait_ge(spe, M[("w2cons",)])
                    gp.dma_start(
                        out=w2s[:].rearrange("p (c f) -> p c f", f=512),
                        in_=w2ap[:, oc2 * 512:(oc2 + 1) * 512].rearrange("(c p) f -> p c f", p=128),
                    ).then_inc(sgp, 16)
                    cnt["gp"] += 16
                    M[("w2", tag, sc, oc2)] = cnt["gp"]

                @block.tensor
                def _(te, oc2=oc2, tag=tag, sc=sc):
                    te.wait_ge(sgp, M[("w2", tag, sc, oc2)])
                    te.wait_ge(sdv, M[(tag, "hdone", sc)])
                    for st2 in range(4):
                        key = ("oev", tag, sc, oc2, st2 - 2)
                        if key in M:
                            te.wait_ge(sdv, M[key])
                        elif st2 < 2 and ("oevprev",) in M:
                            te.wait_ge(sdv, M[("oevprev",)])
                        dstp = oep[:] if st2 % 2 == 0 else h3p[:]
                        last = None
                        for hc in range(NHC):
                            last = te.matmul(
                                out=dstp,
                                lhsT=hfm[:, hc * 512 + st2 * 128: hc * 512 + (st2 + 1) * 128],
                                rhs=w2s[:, hc * 512:(hc + 1) * 512],
                                start=(hc == 0), stop=(hc == NHC - 1),
                            )
                        last.then_inc(spe, 1)
                        cnt["pe"] += 1
                        M[("omm", tag, sc, oc2, st2)] = cnt["pe"]
                    M[("w2cons",)] = cnt["pe"]

                @block.vector
                def _(dv, oc2=oc2, tag=tag, sc=sc):
                    if tag == "e" and ("e", "scat", sc - 1) in M:
                        dv.wait_ge(sgp, M[("e", "scat", sc - 1)])
                    if tag == "s" and ("s", "store", sc - 1) in M:
                        dv.wait_ge(sst, M[("s", "store", sc - 1)])
                    for st2 in range(4):
                        dv.wait_ge(spe, M[("omm", tag, sc, oc2, st2)])
                        src = oep[:] if st2 % 2 == 0 else h3p[:]
                        osl = slice(st2 * D + oc2 * 512, st2 * D + (oc2 + 1) * 512)
                        if tag == "e":
                            dv.tensor_copy(out=oesb[:, osl], in_=src).then_inc(sdv, 1)
                        else:
                            dv.wait_ge(sgp, M[("x1r", sc)])
                            dv.scalar_tensor_tensor(
                                out=s2all[:, osl], in0=src, scalar=1.0,
                                in1=x1r4[:, osl], op0=ALU.mult, op1=ALU.add,
                            ).then_inc(sdv, 1)
                        cnt["dv"] += 1
                        M[("oev", tag, sc, oc2, st2)] = cnt["dv"]
                    M[("oevprev",)] = cnt["dv"]
            M[(tag, "chunk", sc)] = cnt["dv"]

        for sc in range(4):
            @block.gpsimd
            def _(gp, sc=sc):
                gp.wait_ge(sgp, M["asm"])
                if ("e", "tp", sc - 1) in M:
                    gp.wait_ge(spe, M[("e", "tp", sc - 1)])
                for rt in range(4):
                    gp.dma_start(out=xsf[:, rt * D:(rt + 1) * D],
                                 in_=XD[sc * 512 + rt * 128: sc * 512 + (rt + 1) * 128, :]).then_inc(sgp, 16)
                    cnt["gp"] += 16
                M[("e", "rows", sc)] = cnt["gp"]

            pe0 = cnt["pe"]
            dv0 = cnt["dv"]

            @block.tensor
            def _(te, sc=sc, dv0=dv0):
                te.wait_ge(sgp, M[("e", "rows", sc)])
                if ("e", "xdffree", sc - 1) in M:
                    te.wait_ge(sdv, M[("e", "xdffree", sc - 1)])
                k = 0
                for rt in range(4):
                    for c16 in range(16):
                        if k >= 2:
                            te.wait_ge(sdv, dv0 + k - 1)
                        te.transpose(out=ptp2[k % 2][:],
                                     in_=xsf[:, rt * D + c16 * 128: rt * D + (c16 + 1) * 128],
                                     identity=idn[:]).then_inc(spe, 1)
                        cnt["pe"] += 1
                        k += 1
                M[("e", "tp", sc)] = cnt["pe"]

            @block.vector
            def _(dv, sc=sc, pe0=pe0):
                k = 0
                for rt in range(4):
                    for c16 in range(16):
                        dv.wait_ge(spe, pe0 + k + 1)
                        dv.tensor_copy(
                            out=xdf[:, c16 * 512 + rt * 128: c16 * 512 + (rt + 1) * 128],
                            in_=ptp2[k % 2][:],
                        ).then_inc(sdv, 1)
                        cnt["dv"] += 1
                        k += 1
                M[("e", "xdf", sc)] = cnt["dv"]

            ffn_chunk("e", sc, W1E, W3E, W2E)
            M[("e", "xdffree", sc)] = M[("e", "hdone", sc)]

            @block.gpsimd
            def _(gp, sc=sc):
                for st2 in range(4):
                    gp.wait_ge(sdv, M[("oev", "e", sc, 3, st2)])
                    gp.indirect_dma_start(
                        out=RETP,
                        out_offset=IndirectOffsetOnAxis(retidx[:, sc * 4 + st2: sc * 4 + st2 + 1], 0),
                        in_=oesb[:, st2 * D:(st2 + 1) * D], in_offset=None,
                    ).then_inc(sgp, 16)
                    cnt["gp"] += 16
                    gp.wait_ge(sgp, cnt["gp"])
                M[("e", "scat", sc)] = cnt["gp"]

        for sc in range(4):
            @block.gpsimd
            def _(gp, sc=sc):
                if ("s", "xdffree", sc - 1) in M:
                    gp.wait_ge(sdv, M[("s", "xdffree", sc - 1)])
                elif sc == 0:
                    gp.wait_ge(sdv, M[("e", "hdone", 3)])
                gp.dma_start(
                    out=xdf[:].rearrange("p (c t) -> p c t", t=512),
                    in_=X1F[:, sc * 512:(sc + 1) * 512].rearrange("(c p) t -> p c t", p=128),
                ).then_inc(sgp, 16)
                cnt["gp"] += 16
                if ("s", "chunk", sc - 1) in M:
                    gp.wait_ge(sdv, M[("s", "chunk", sc - 1)])
                for rt in range(4):
                    gp.dma_start(out=x1r4[:, rt * D:(rt + 1) * D],
                                 in_=X1T[sc * 512 + rt * 128: sc * 512 + (rt + 1) * 128, :]).then_inc(sgp, 16)
                    cnt["gp"] += 16
                M[("x1r", sc)] = cnt["gp"]

            @block.vector
            def _(dv, sc=sc):
                dv.wait_ge(sgp, M[("x1r", sc)])
                dv.sem_inc(sdv, 1)
                cnt["dv"] += 1
                M[("s", "xdf", sc)] = cnt["dv"]

            ffn_chunk("s", sc, SW1, SW3, SW2)
            M[("s", "xdffree", sc)] = M[("s", "hdone", sc)]

            @block.sync
            def _(sy, sc=sc):
                for st2 in range(4):
                    sy.wait_ge(sdv, M[("oev", "s", sc, 3, st2)])
                    sy.dma_start(
                        out=S2P[sc * 512 + st2 * 128: sc * 512 + (st2 + 1) * 128, :],
                        in_=s2all[:, st2 * D:(st2 + 1) * D],
                    ).then_inc(sst, 16)
                    cnt["st"] += 16
                M[("s", "store", sc)] = cnt["st"]

        @block.sync
        def _(sy):
            sy.wait_ge(sst, cnt["st"])
            sy.wait_ge(sgp, M[("e", "scat", 3)])

    es.close()
    return nc


def build_l4():
    nc = bass.Bass("TRN2", target_bir_lowering=False)
    RETR = nc.dram_tensor("RETR", [E * CLOC, D], BF16, kind="ExternalInput").ap()
    EIDX = nc.dram_tensor("EIDX", [128, NTT], F32, kind="ExternalInput").ap()
    LPOS = nc.dram_tensor("LPOS", [128, NTT], F32, kind="ExternalInput").ap()
    GATE = nc.dram_tensor("GATE", [128, NTT], F32, kind="ExternalInput").ap()
    CNTA = nc.dram_tensor("CNTA", [NC, E], F32, kind="ExternalInput").ap()
    S2P = nc.dram_tensor("S2P", [TPC, D], F32, kind="ExternalInput").ap()
    SELC = nc.dram_tensor("SELC", [8, 1], F32, kind="ExternalInput").ap()
    SLT = nc.dram_tensor("SLT", [8, 8], F32, kind="ExternalInput").ap()
    IOTA8 = nc.dram_tensor("IOTA8", [128, E], F32, kind="ExternalInput").ap()
    TOKI = nc.dram_tensor("TOKI", [128, NTT], I32, kind="ExternalInput").ap()
    ONESR = nc.dram_tensor("ONESR", [1, 128], F32, kind="ExternalInput").ap()

    OUT = nc.dram_tensor("OUT", [TPC, D], F32, kind="ExternalOutput").ap()

    RTD = nc.dram_tensor("RTD", [TPC + 1, D], BF16, kind="Internal").ap()
    TKM = nc.dram_tensor("TKM", [E * CLOC + 1, 1], I32, kind="Internal").ap()

    Tn = nc.alloc_sbuf_tensor
    P = nc.alloc_psum_tensor
    eidx = Tn("eidx", [128, NTT], F32)
    lpos = Tn("lpos", [128, NTT], F32)
    gate = Tn("gate", [128, NTT], F32)
    cnta = Tn("cnta", [8, E], F32)
    selc = Tn("selc", [8, 1], F32)
    slt = Tn("slt", [8, 8], F32)
    iota8 = Tn("iota8", [128, E], F32)
    toki = Tn("toki", [128, NTT], I32)
    onesr = Tn("onesr", [1, 128], F32)
    gbme = Tn("gbme", [128, E], F32)
    oh8 = Tn("oh8", [128, E], F32)
    t1 = Tn("t1", [128, 1], F32)
    t2 = Tn("t2", [128, 1], F32)
    gk16 = Tn("gk16", [128, NTT], F32)
    pidx4 = Tn("pidx4", [128, NTT], I32)
    zti = Tn("zti", [128, 32], I32)
    ztile = Tn("ztile", [128, D], BF16)
    tmsb = Tn("tmsb", [128, 1], I32)
    rb = Tn("rb", [128, D], BF16)
    rgf = Tn("rgf", [128, D], F32)
    s2f = Tn("s2f", [128, D], F32)
    s2pr = Tn("s2pr", [128, D], F32)
    sqscr = Tn("sqscr", [128, D], F32)
    stats = Tn("stats", [128, 8], F32)
    outf = Tn("outf", [128, D], F32)

    pg4 = P("pg4", [128, 512], F32)
    pgb = pg4[0:8, 0:8]
    pgr = pg4[0:1, 8:16]
    pgm = pg4[:, 16:24]

    es = contextlib.ExitStack()
    sgp = es.enter_context(nc.semaphore("sgp"))
    sld = es.enter_context(nc.semaphore("sld"))
    spe = es.enter_context(nc.semaphore("spe"))
    sdv = es.enter_context(nc.semaphore("sdv"))
    sac = es.enter_context(nc.semaphore("sac"))
    sst = es.enter_context(nc.semaphore("sst"))

    cnt = {"gp": 0, "ld": 0, "pe": 0, "dv": 0, "ac": 0, "st": 0}
    M = {}

    def bar(dv, inst):
        inst.then_inc(sdv, 1)
        cnt["dv"] += 1
        dv.wait_ge(sdv, cnt["dv"])

    with nc.Block() as block:
        @block.sync
        def _(sy):
            for ap_s, ap_d in ((EIDX, eidx), (LPOS, lpos), (GATE, gate), (CNTA, cnta),
                               (SELC, selc), (SLT, slt), (IOTA8, iota8), (TOKI, toki),
                               (ONESR, onesr)):
                sy.dma_start(out=ap_d[:], in_=ap_s).then_inc(sld, 16)
                cnt["ld"] += 16
            M["consts"] = cnt["ld"]

        @block.tensor
        def _(te):
            te.wait_ge(sld, M["consts"])
            te.matmul(out=pgb, lhsT=slt[:], rhs=cnta[:], start=True, stop=True)
            te.matmul(out=pgr, lhsT=selc[:], rhs=pgb if False else cnta[:],
                      start=True, stop=True).then_inc(spe, 1)
            cnt["pe"] += 1
            M["g1"] = cnt["pe"]

        @block.vector
        def _(dv):
            dv.wait_ge(spe, M["g1"])
            bar(dv, dv.tensor_copy(out=gbme[0:8, :], in_=pgb))
            M["gbev"] = cnt["dv"]

        @block.tensor
        def _(te):
            te.wait_ge(sdv, M["gbev"])
            te.matmul(out=pgr, lhsT=selc[:], rhs=gbme[0:8, :], start=True, stop=True).then_inc(spe, 1)
            cnt["pe"] += 1
            M["g2"] = cnt["pe"]

        @block.vector
        def _(dv):
            dv.wait_ge(spe, M["g2"])
            bar(dv, dv.tensor_copy(out=gbme[0:1, :], in_=pgr))
            M["grow"] = cnt["dv"]

        @block.tensor
        def _(te):
            te.wait_ge(sdv, M["grow"])
            te.matmul(out=pgm, lhsT=onesr[:], rhs=gbme[0:1, :], start=True, stop=True).then_inc(spe, 1)
            cnt["pe"] += 1
            M["g3"] = cnt["pe"]

        @block.vector
        def _(dv):
            dv.wait_ge(spe, M["g3"])
            bar(dv, dv.tensor_copy(out=gbme[:], in_=pgm))
            for tt in range(NTT):
                bar(dv, dv.tensor_scalar(out=oh8[:], in0=iota8[:], scalar1=eidx[:, tt:tt + 1],
                                         scalar2=None, op0=ALU.is_equal))
                bar(dv, dv.tensor_mul(out=oh8[:], in0=oh8[:], in1=gbme[:]))
                bar(dv, dv.reduce_sum(out=t1[:], in_=oh8[:], axis=AX.X))
                bar(dv, dv.tensor_add(out=t1[:], in0=t1[:], in1=lpos[:, tt:tt + 1]))
                bar(dv, dv.tensor_single_scalar(out=t1[:], in_=t1[:], scalar=float(CAP), op=ALU.is_lt))
                bar(dv, dv.tensor_mul(out=gk16[:, tt:tt + 1] if False else t2[:],
                                      in0=t1[:], in1=gate[:, tt:tt + 1]))
                bar(dv, dv.tensor_copy(out=gk16[:, tt:tt + 1], in_=t2[:]))
                bar(dv, dv.tensor_single_scalar(out=t2[:], in_=lpos[:, tt:tt + 1],
                                                scalar=float(CLOC), op=ALU.is_lt))
                bar(dv, dv.scalar_tensor_tensor(out=t1[:], in0=eidx[:, tt:tt + 1],
                                                scalar=float(CLOC), in1=lpos[:, tt:tt + 1],
                                                op0=ALU.mult, op1=ALU.add))
                bar(dv, dv.tensor_scalar_sub(out=t1[:], in0=t1[:], scalar1=float(E * CLOC)))
                bar(dv, dv.tensor_mul(out=t1[:], in0=t1[:], in1=t2[:]))
                bar(dv, dv.tensor_scalar_add(out=t1[:], in0=t1[:], scalar1=float(E * CLOC)))
                bar(dv, dv.tensor_copy(out=pidx4[:, tt:tt + 1], in_=t1[:]))
            dv.memset(ztile[:], 0.0)
            bar(dv, dv.memset(zti[:], CAP))
            M["routmath"] = cnt["dv"]

        @block.gpsimd
        def _(gp):
            gp.wait_ge(sdv, M["routmath"])
            for b in range(16):
                gp.dma_start(out=RTD[b * 128:(b + 1) * 128, :], in_=ztile[:]).then_inc(sgp, 16)
                cnt["gp"] += 16
            gp.dma_start(out=RTD[TPC:TPC + 1, :], in_=ztile[0:1, :]).then_inc(sgp, 16)
            cnt["gp"] += 16
            gp.dma_start(out=TKM[0:E * CLOC, 0:1].rearrange("(p a) c -> p a c", a=32),
                         in_=zti[:].rearrange("p (a c) -> p a c", c=1)).then_inc(sgp, 16)
            cnt["gp"] += 16
            gp.dma_start(out=TKM[E * CLOC:E * CLOC + 1, 0:1], in_=zti[0:1, 0:1]).then_inc(sgp, 16)
            cnt["gp"] += 16
            gp.wait_ge(sgp, cnt["gp"])
            for tt in range(NTT):
                gp.indirect_dma_start(
                    out=TKM, out_offset=IndirectOffsetOnAxis(pidx4[:, tt:tt + 1], 0),
                    in_=toki[:, tt:tt + 1], in_offset=None,
                ).then_inc(sgp, 16)
                cnt["gp"] += 16
            gp.wait_ge(sgp, cnt["gp"])
            for rt2 in range(32):
                gp.dma_start(out=tmsb[:], in_=TKM[rt2 * 128:(rt2 + 1) * 128, 0:1]).then_inc(sgp, 16)
                gp.dma_start(out=rb[:], in_=RETR[rt2 * 128:(rt2 + 1) * 128, :]).then_inc(sgp, 16)
                cnt["gp"] += 32
                gp.wait_ge(sgp, cnt["gp"])
                gp.indirect_dma_start(
                    out=RTD, out_offset=IndirectOffsetOnAxis(tmsb[:, 0:1], 0),
                    in_=rb[:], in_offset=None,
                ).then_inc(sgp, 16)
                cnt["gp"] += 16
                gp.wait_ge(sgp, cnt["gp"])
            M["rscat"] = cnt["gp"]

        for tt in range(NTT):
            @block.sync
            def _(sy, tt=tt):
                sy.wait_ge(sgp, M["rscat"])
                if tt > 0:
                    sy.wait_ge(sdv, M[("tiledone", tt - 1)])
                sy.dma_start(out=rb[:], in_=RTD[tt * 128:(tt + 1) * 128, :]).then_inc(sld, 16)
                cnt["ld"] += 16
                sy.dma_start(out=s2pr[:], in_=S2P[tt * 128:(tt + 1) * 128, :]).then_inc(sld, 16)
                cnt["ld"] += 16
                M[("tl", tt)] = cnt["ld"]

            ac0 = cnt["ac"]
            dv0 = cnt["dv"]

            @block.scalar
            def _(ac, tt=tt, dv0=dv0):
                ac.wait_ge(sld, M[("tl", tt)])
                if tt > 0:
                    ac.wait_ge(sdv, M[("s2f", tt - 1)])
                ac.activation(out=rgf[:], in_=rb[:], func=AF.Copy,
                              scale=gk16[:, tt:tt + 1]).then_inc(sac, 1)
                cnt["ac"] += 1
                M[("rg", tt)] = cnt["ac"]
                ac.wait_ge(sdv, dv0 + 2)
                ac.activation(out=sqscr[:], in_=s2f[:], func=AF.Square,
                              accum_out=stats[:, 1:2]).then_inc(sac, 1)
                cnt["ac"] += 1
                M[("sq", tt)] = cnt["ac"]

            @block.vector
            def _(dv, tt=tt):
                dv.wait_ge(sac, M[("rg", tt)])
                dv.tensor_add(out=s2f[:], in0=s2pr[:], in1=rgf[:]).then_inc(sdv, 1)
                cnt["dv"] += 1
                M[("s2f", tt)] = cnt["dv"]
                dv.reduce_sum(out=stats[:, 0:1], in_=s2f[:], axis=AX.X).then_inc(sdv, 1)
                cnt["dv"] += 1
                dv.wait_ge(sac, M[("sq", tt)])
                dv.tensor_scalar_mul(out=stats[:, 0:1], in0=stats[:, 0:1], scalar1=1.0 / D)
                bar(dv, dv.tensor_scalar_mul(out=stats[:, 1:2], in0=stats[:, 1:2], scalar1=1.0 / D))
                bar(dv, dv.tensor_mul(out=stats[:, 6:7], in0=stats[:, 0:1], in1=stats[:, 0:1]))
                bar(dv, dv.tensor_sub(out=stats[:, 2:3], in0=stats[:, 1:2], in1=stats[:, 6:7]))
                dv.tensor_scalar_add(out=stats[:, 2:3], in0=stats[:, 2:3], scalar1=1e-5).then_inc(sdv, 1)
                cnt["dv"] += 1
                M[("var", tt)] = cnt["dv"]
                dv.wait_ge(sac, M[("sq", tt)] + 1)
                bar(dv, dv.tensor_copy(out=stats[:, 4:5], in_=stats[:, 3:4]))
                M[("lnv2", tt)] = cnt["dv"]
                dv.wait_ge(sac, M[("sq", tt)] + 2)
                dv.wait_ge(sdv, cnt["dv"])
                if tt > 0:
                    dv.wait_ge(sst, M[("ost", tt - 1)])
                dv.tensor_scalar(out=outf[:], in0=s2f[:], scalar1=stats[:, 0:1],
                                 scalar2=stats[:, 5:6], op0=ALU.subtract,
                                 op1=ALU.mult).then_inc(sdv, 1)
                cnt["dv"] += 1
                M[("outf", tt)] = cnt["dv"]

            @block.scalar
            def _(ac, tt=tt):
                ac.wait_ge(sdv, M[("var", tt)])
                ac.activation(out=stats[:, 3:4], in_=stats[:, 2:3], func=AF.Ln).then_inc(sac, 1)
                cnt["ac"] += 1
                ac.wait_ge(sdv, M[("lnv2", tt)])
                ac.activation(out=stats[:, 5:6], in_=stats[:, 4:5], func=AF.Exp,
                              scale=-0.5).then_inc(sac, 1)
                cnt["ac"] += 1

            @block.sync
            def _(sy, tt=tt):
                sy.wait_ge(sdv, M[("outf", tt)])
                sy.dma_start(out=OUT[tt * 128:(tt + 1) * 128, :], in_=outf[:]).then_inc(sst, 16)
                cnt["st"] += 16
                M[("ost", tt)] = cnt["st"]
                M[("tiledone", tt)] = cnt["dv"]

        @block.sync
        def _(sy):
            sy.wait_ge(sst, cnt["st"])

    es.close()
    return nc


_PROGS = {}


def _get(name, builder):
    if name not in _PROGS:
        _PROGS[name] = builder()
    return _PROGS[name]


def _run_l1(xs, WqkvT, cosT, sinT):
    nc1 = _get("l1", build_l1)
    psgn = np.asarray(_p_signed())
    idn = np.eye(128, dtype=ml_dtypes.bfloat16)
    in_maps = []
    for c in range(NC):
        sl = slice(c * TPC, (c + 1) * TPC)
        in_maps.append(dict(
            X=np.ascontiguousarray(xs[sl]),
            WQKVT=WqkvT,
            COSF=np.ascontiguousarray(cosT[:, sl]),
            SINF=np.ascontiguousarray(sinT[:, sl]),
            PSGN=psgn, IDN=idn,
        ))
    res = run_bass_kernel_spmd(nc1, in_maps, core_ids=list(range(NC)))
    return res.results




def _consts_l2():
    mask = np.zeros((128, 256), ml_dtypes.bfloat16)
    for m in range(128):
        mask[m, m:] = 1.0
    ut = np.zeros((128, 128), np.float32)
    for k in range(128):
        ut[k, k:] = 1.0
    return dict(
        MASK=mask,
        IDN=np.eye(128, dtype=ml_dtypes.bfloat16),
        IDNF=np.eye(128, dtype=np.float32),
        UT=ut,
        ONESR=np.ones((1, 128), np.float32),
        IOTA8=np.tile(np.arange(E, dtype=np.float32), (128, 1)),
    )


def _run_l2(l1res, xs, WoutT, Wg):
    nc2 = _get("l2", build_l2)
    consts = _consts_l2()
    szall = np.stack([np.asarray(l1res[c]["SZT"]) for c in range(NC)])
    in_maps = []
    for c in range(NC):
        sl = slice(c * TPC, (c + 1) * TPC)
        cpref = np.zeros((128, NC), np.float32)
        cpref[:, :c] = 1.0
        in_maps.append(dict(
            QF=np.asarray(l1res[c]["QF"]), KF=np.asarray(l1res[c]["KF"]),
            KT=np.asarray(l1res[c]["KT"]), VT=np.asarray(l1res[c]["VT"]),
            SZALL=szall, X=np.ascontiguousarray(xs[sl]),
            WOUTT=WoutT, WG=Wg, CPREFB=cpref, **consts,
        ))
    res = run_bass_kernel_spmd(nc2, in_maps, core_ids=list(range(NC)))
    return res.results




def _consts_l3():
    ejc = np.zeros((8, NC * 128), np.float32)
    for j in range(NC):
        ejc[j, j * 128:(j + 1) * 128] = 1.0
    slt = np.zeros((8, 8), np.float32)
    for k in range(8):
        slt[k, k + 1:] = 1.0
    lpc = (np.arange(128)[:, None] + 128 * np.arange(4)[None, :]).astype(np.float32)
    slc = (np.arange(128)[:, None] + 128 * np.arange(16)[None, :]).astype(np.float32)
    return dict(EJC=ejc, SLT=slt, LPC=lpc, SLC=slc,
                IDN=np.eye(128, dtype=ml_dtypes.bfloat16))


def _run_l3(l2res, W1, W3, W2, sw1, sw3, sw2):
    nc3 = _get("l3", build_l3)
    consts = _consts_l3()
    cnta = np.concatenate([np.asarray(l2res[c]["CNT"]) for c in range(NC)], 0)
    pbufs = [np.asarray(l2res[c]["PBUF"]) for c in range(NC)]
    in_maps = []
    for i in range(NC):
        runs = np.concatenate([pbufs[j][i * CLOC:(i + 1) * CLOC] for j in range(NC)], 0)
        selb = np.zeros((8, E), np.float32)
        selb[:, i] = 1.0
        in_maps.append(dict(
            RUNS=np.ascontiguousarray(runs), CNTA=cnta,
            X1F=np.asarray(l2res[i]["X1F"]), X1T=np.asarray(l2res[i]["X1T"]),
            W1E=np.ascontiguousarray(W1[i]), W3E=np.ascontiguousarray(W3[i]),
            W2E=np.ascontiguousarray(W2[i]),
            SW1=sw1, SW3=sw3, SW2=sw2, SELB=selb, **consts,
        ))
    res = run_bass_kernel_spmd(nc3, in_maps, core_ids=list(range(NC)))
    return res.results




def _run_l4(l2res, l3res):
    nc4 = _get("l4", build_l4)
    cnta = np.concatenate([np.asarray(l2res[c]["CNT"]) for c in range(NC)], 0)
    retps = [np.asarray(l3res[i]["RETP"]) for i in range(NC)]
    slt = np.zeros((8, 8), np.float32)
    for k in range(8):
        slt[k, k + 1:] = 1.0
    toki = (np.arange(128)[:, None] + 128 * np.arange(NTT)[None, :]).astype(np.int32)
    iota8 = np.tile(np.arange(E, dtype=np.float32), (128, 1))
    onesr = np.ones((1, 128), np.float32)
    in_maps = []
    for j in range(NC):
        retr = np.concatenate([retps[i][j * CLOC:(j + 1) * CLOC] for i in range(NC)], 0)
        selc = np.zeros((8, 1), np.float32)
        selc[j, 0] = 1.0
        in_maps.append(dict(
            RETR=np.ascontiguousarray(retr),
            EIDX=np.asarray(l2res[j]["EIDX"]), LPOS=np.asarray(l2res[j]["LPOS"]),
            GATE=np.asarray(l2res[j]["GATE"]), CNTA=cnta,
            S2P=np.asarray(l3res[j]["S2P"]),
            SELC=selc, SLT=slt, IOTA8=iota8, TOKI=toki, ONESR=onesr,
        ))
    res = run_bass_kernel_spmd(nc4, in_maps, core_ids=list(range(NC)))
    return res.results


def kernel(L0_attn_x, L0_attn_W_qkv, L0_attn_W_out, L0_ffn_Wg, L0_ffn_W1,
           L0_ffn_W3, L0_ffn_W2, L0_ffn_sw1, L0_ffn_sw3, L0_ffn_sw2):
    xs = np.ascontiguousarray(np.asarray(L0_attn_x, np.float32).reshape(T, D))
    WqkvT = np.ascontiguousarray(np.asarray(L0_attn_W_qkv, np.float32).T)
    WoutT = np.ascontiguousarray(np.asarray(L0_attn_W_out, np.float32).T)
    Wg = np.ascontiguousarray(np.asarray(L0_ffn_Wg, np.float32))
    cos, sin = _rope_tables()
    r1 = _run_l1(xs, WqkvT, np.ascontiguousarray(cos.T), np.ascontiguousarray(sin.T))
    r2 = _run_l2(r1, xs, WoutT, Wg)
    r3 = _run_l3(r2, np.asarray(L0_ffn_W1, np.float32), np.asarray(L0_ffn_W3, np.float32),
                 np.asarray(L0_ffn_W2, np.float32), np.ascontiguousarray(np.asarray(L0_ffn_sw1, np.float32)),
                 np.ascontiguousarray(np.asarray(L0_ffn_sw3, np.float32)),
                 np.ascontiguousarray(np.asarray(L0_ffn_sw2, np.float32)))
    r4 = _run_l4(r2, r3)
    out = np.concatenate([np.asarray(r4[c]["OUT"], np.float32) for c in range(NC)], 0)
    return out.reshape(1, T, D)


# version 02
# speedup vs baseline: 1.3527x; 1.3527x over previous
import numpy as np
import ml_dtypes
import contextlib
import concourse.bass as bass
import concourse.mybir as mybir
from concourse.bass_utils import run_bass_kernel_spmd
from concourse.bass_core_rust import IndirectOffsetOnAxis

F32 = mybir.dt.float32
BF16 = mybir.dt.bfloat16
I32 = mybir.dt.int32
AF = mybir.ActivationFunctionType
ALU = mybir.AluOpType
AX = mybir.AxisListType

NC = 8
T = 16384
D = 2048
NH = 16
HD = 128
CH = 256
E = 8
HF = 2816
CLOC = 512
TPC = T // NC
CAP = 2048
NTT = TPC // 128
NCH = TPC // CH


def _rope_tables():
    inv = 1.0 / (10000.0 ** (np.arange(0, HD, 2, dtype=np.float32) / HD))
    pos = np.arange(T, dtype=np.float32)
    emb = np.concatenate([np.outer(pos, inv)] * 2, -1)
    return np.cos(emb).astype(np.float32), np.sin(emb).astype(np.float32)


def _p_signed():
    P = np.zeros((HD, HD), np.float32)
    for i in range(HD // 2):
        P[2 * i + 1, 2 * i] = -1.0
        P[2 * i, 2 * i + 1] = 1.0
    return P.astype(ml_dtypes.bfloat16)


def build_l1():
    nc = bass.Bass("TRN2", target_bir_lowering=False)
    X = nc.dram_tensor("X", [TPC, D], F32, kind="ExternalInput").ap()
    WQKVT = nc.dram_tensor("WQKVT", [D, 3 * D], F32, kind="ExternalInput").ap()
    COSF = nc.dram_tensor("COSF", [HD, TPC], F32, kind="ExternalInput").ap()
    SINF = nc.dram_tensor("SINF", [HD, TPC], F32, kind="ExternalInput").ap()
    PSGN = nc.dram_tensor("PSGN", [HD, HD], BF16, kind="ExternalInput").ap()
    IDN = nc.dram_tensor("IDN", [128, 128], BF16, kind="ExternalInput").ap()

    QF = nc.dram_tensor("QF", [D, TPC], BF16, kind="ExternalOutput").ap()
    KF = nc.dram_tensor("KF", [D, TPC], BF16, kind="ExternalOutput").ap()
    KT = nc.dram_tensor("KT", [TPC, D], BF16, kind="ExternalOutput").ap()
    VT = nc.dram_tensor("VT", [TPC, D], BF16, kind="ExternalOutput").ap()
    SZT = nc.dram_tensor("SZT", [NH, HD, HD + 1], F32, kind="ExternalOutput").ap()

    X0BF = nc.dram_tensor("X0BF", [TPC, D], BF16, kind="Internal").ap()

    Tn = nc.alloc_sbuf_tensor
    P = nc.alloc_psum_tensor
    x0fm = Tn("x0fm", [128, 16 * TPC], BF16)
    cosf = Tn("cosf", [128, TPC], F32)
    sinf = Tn("sinf", [128, TPC], F32)
    psgn = Tn("psgn", [128, 128], BF16)
    idn = Tn("idn", [128, 128], BF16)
    wqk = Tn("wqk", [128, 2 * 2048], BF16)
    wv = Tn("wv", [128, 2 * 8192], BF16)
    af32 = Tn("af32", [128, 512], F32)
    abf = Tn("abf", [128, 512], BF16)
    t1 = Tn("t1", [128, 512], F32)
    t2 = Tn("t2", [128, 512], F32)
    escr = Tn("escr", [128, 512], F32)
    qkout = Tn("qkout", [128, TPC], BF16)
    ktok = Tn("ktok", [128, 16 * 128], BF16)
    vtok = Tn("vtok", [128, 16 * 512], BF16)
    szev = Tn("szev", [128, HD + 1], F32)
    ones1 = Tn("ones1", [128, 1], BF16)

    pq = P("pq", [128, 512], F32)
    psw = P("psw", [128, 512], F32)
    pv = [P("pv0", [128, 512], F32), P("pv1", [128, 512], F32)]
    ptp = [P("ptp0", [128, 128], BF16), P("ptp1", [128, 128], BF16)]
    psz = [P("psz0", [128, HD + 1], F32), P("psz1", [128, HD + 1], F32)]

    es = contextlib.ExitStack()
    sgp = es.enter_context(nc.semaphore("sgp"))
    sld = es.enter_context(nc.semaphore("sld"))
    spe = es.enter_context(nc.semaphore("spe"))
    sdv = es.enter_context(nc.semaphore("sdv"))
    sac = es.enter_context(nc.semaphore("sac"))
    sst = es.enter_context(nc.semaphore("sst"))

    cnt = {"gp": 0, "ld": 0, "pe": 0, "dv": 0, "ac": 0, "st": 0}
    M = {}

    with nc.Block() as block:
        @block.gpsimd
        def _(gp):
            gp.dma_start(out=X0BF, in_=X).then_inc(sgp, 16)
            cnt["gp"] += 16
            M["x0bf"] = cnt["gp"]

        def load_weights(kind, idx):
            @block.gpsimd
            def _(gp, kind=kind, idx=idx):
                slot = 0 if kind != "k" else 1
                key = ("wfree", kind, idx)
                if key in M:
                    gp.wait_ge(spe, M[key])
                if kind == "v":
                    s = WQKVT[:, 2 * D + idx * 512: 2 * D + (idx + 1) * 512]
                    s = s.rearrange("(c p) f -> p c f", p=128)
                    d_ = wv[:, (idx % 2) * 8192:(idx % 2 + 1) * 8192]
                    d_ = d_.rearrange("p (c f) -> p c f", f=512)
                else:
                    base = 0 if kind == "q" else D
                    s = WQKVT[:, base + idx * 128: base + (idx + 1) * 128]
                    s = s.rearrange("(c p) f -> p c f", p=128)
                    d_ = wqk[:, slot * 2048:(slot + 1) * 2048]
                    d_ = d_.rearrange("p (c f) -> p c f", f=128)
                gp.dma_start(out=d_, in_=s).then_inc(sgp, 16)
                cnt["gp"] += 16
                M[("wld", kind, idx)] = cnt["gp"]

        @block.sync
        def _(sy):
            for ap_s, ap_d in ((COSF, cosf), (SINF, sinf), (PSGN, psgn), (IDN, idn)):
                sy.dma_start(out=ap_d[:], in_=ap_s).then_inc(sld, 16)
                cnt["ld"] += 16
            sy.wait_ge(sgp, M["x0bf"])
            for c in range(16):
                sy.dma_start_transpose(
                    out=x0fm[:, c * TPC:(c + 1) * TPC],
                    in_=X0BF[:, c * 128:(c + 1) * 128],
                ).then_inc(sld, 16)
                cnt["ld"] += 16
            M["ld_all"] = cnt["ld"]

        @block.vector
        def _(dv):
            dv.memset(ones1[:], 1.0).then_inc(sdv, 1)
            cnt["dv"] += 1

        for g in range(4):
            load_weights("v", g)
            pe0 = cnt["pe"]
            dv0 = cnt["dv"]

            @block.tensor
            def _(te, g=g, pe0=pe0, dv0=dv0):
                te.wait_ge(sgp, M[("wld", "v", g)])
                te.wait_ge(sld, M["ld_all"])
                for tt in range(16):
                    if tt >= 2:
                        te.wait_ge(sdv, dv0 + tt - 1)
                    last = None
                    for c in range(16):
                        last = te.matmul(
                            out=pv[tt % 2][:],
                            lhsT=x0fm[:, c * TPC + tt * 128: c * TPC + (tt + 1) * 128],
                            rhs=wv[:, (g % 2) * 8192 + c * 512:(g % 2) * 8192 + (c + 1) * 512],
                            start=(c == 0), stop=(c == 15),
                        )
                    last.then_inc(spe, 1)
                    cnt["pe"] += 1
                M[("vmm", g)] = cnt["pe"]
                M[("wfree", "v", g + 2)] = cnt["pe"]

            @block.vector
            def _(dv, g=g, pe0=pe0):
                for tt in range(16):
                    dv.wait_ge(spe, pe0 + tt + 1)
                    if g > 0:
                        dv.wait_ge(spe, M[("szdone", g - 1)])
                        dv.wait_ge(sst, M[("vstash", g - 1)])
                    dv.tensor_copy(
                        out=vtok[:, tt * 512:(tt + 1) * 512], in_=pv[tt % 2][:]
                    ).then_inc(sdv, 1)
                    cnt["dv"] += 1
                M[("vev", g)] = cnt["dv"]

            @block.sync
            def _(sy, g=g, dv0=dv0):
                for tt in range(16):
                    sy.wait_ge(sdv, dv0 + tt + 1)
                    sy.dma_start(
                        out=VT[tt * 128:(tt + 1) * 128, g * 512:(g + 1) * 512],
                        in_=vtok[:, tt * 512:(tt + 1) * 512],
                    ).then_inc(sst, 16)
                    cnt["st"] += 16
                M[("vstash", g)] = cnt["st"]

            for hh in range(4):
                h = 4 * g + hh
                for part in ("q", "k"):
                    load_weights(part, h)
                    slot = 0 if part == "q" else 1
                    dst = QF if part == "q" else KF
                    pe0 = cnt["pe"]
                    dv0 = cnt["dv"]
                    ac0 = cnt["ac"]
                    st_prev = cnt["st"]

                    @block.tensor
                    def _(te, h=h, part=part, slot=slot, pe0=pe0, dv0=dv0):
                        te.wait_ge(sgp, M[("wld", part, h)])
                        te.wait_ge(sst, M.get("qkstash", 0))
                        for tc in range(4):
                            if tc > 0:
                                te.wait_ge(sdv, dv0 + 3 * (tc - 1) + 1)
                            last = None
                            for c in range(16):
                                last = te.matmul(
                                    out=pq[:],
                                    lhsT=wqk[:, slot * 2048 + c * 128: slot * 2048 + (c + 1) * 128],
                                    rhs=x0fm[:, c * TPC + tc * 512: c * TPC + (tc + 1) * 512],
                                    start=(c == 0), stop=(c == 15),
                                )
                            last.then_inc(spe, 1)
                            cnt["pe"] += 1
                            te.wait_ge(sdv, dv0 + 3 * tc + 1)
                            if tc > 0:
                                te.wait_ge(sdv, dv0 + 3 * (tc - 1) + 2)
                            te.matmul(out=psw[:], lhsT=psgn[:], rhs=abf[:],
                                      start=True, stop=True).then_inc(spe, 1)
                            cnt["pe"] += 1
                        M[("wfree", part, h + (2 if part == "v" else 1))] = cnt["pe"]

                    @block.vector
                    def _(dv, pe0=pe0, ac0=ac0, st_prev=st_prev):
                        for tc in range(4):
                            dv.wait_ge(spe, pe0 + 2 * tc + 1)
                            if tc == 0:
                                dv.wait_ge(sst, M.get("qkstash", 0))
                            dv.tensor_copy(out=af32[:], in_=pq[:])
                            dv.tensor_copy(out=abf[:], in_=af32[:]).then_inc(sdv, 1)
                            cnt["dv"] += 1
                            dv.wait_ge(spe, pe0 + 2 * tc + 2)
                            sl = slice(tc * 512, (tc + 1) * 512)
                            dv.tensor_mul(out=t1[:], in0=af32[:], in1=cosf[:, sl])
                            dv.tensor_mul(out=t2[:], in0=psw[:], in1=sinf[:, sl])
                            dv.tensor_add(out=t1[:], in0=t1[:], in1=t2[:]).then_inc(sdv, 1)
                            cnt["dv"] += 1
                            dv.wait_ge(sac, ac0 + tc + 1)
                            dv.tensor_scalar_min(out=t2[:], in0=escr[:], scalar1=1.0)
                            dv.scalar_tensor_tensor(
                                out=qkout[:, sl], in0=t1[:], scalar=0.0, in1=t2[:],
                                op0=ALU.max, op1=ALU.add,
                            ).then_inc(sdv, 1)
                            cnt["dv"] += 1
                        M["qkready"] = cnt["dv"]

                    @block.scalar
                    def _(ac, dv0=dv0):
                        for tc in range(4):
                            ac.wait_ge(sdv, dv0 + 3 * tc + 2)
                            ac.activation(out=escr[:], in_=t1[:], func=AF.Exp).then_inc(sac, 1)
                            cnt["ac"] += 1

                    if part == "k":
                        pe1 = cnt["pe"]
                        dv1 = cnt["dv"]

                        @block.tensor
                        def _(te, dv1=dv1):
                            te.wait_ge(sdv, M["qkready"])
                            for tt in range(16):
                                if tt >= 2:
                                    te.wait_ge(sdv, dv1 + tt - 1)
                                te.transpose(
                                    out=ptp[tt % 2][:],
                                    in_=qkout[:, tt * 128:(tt + 1) * 128],
                                    identity=idn[:],
                                ).then_inc(spe, 1)
                                cnt["pe"] += 1

                        @block.vector
                        def _(dv, pe1=pe1, h=h):
                            dv.wait_ge(sst, M.get("ktstash", 0))
                            for tt in range(16):
                                dv.wait_ge(spe, pe1 + tt + 1)
                                dv.tensor_copy(
                                    out=ktok[:, tt * 128:(tt + 1) * 128],
                                    in_=ptp[tt % 2][:],
                                ).then_inc(sdv, 1)
                                cnt["dv"] += 1
                            M["ktokready"] = cnt["dv"]

                    @block.sync
                    def _(sy, h=h, part=part, dst=dst):
                        sy.wait_ge(sdv, M["qkready"])
                        sy.dma_start(out=dst[h * 128:(h + 1) * 128, :],
                                     in_=qkout[:]).then_inc(sst, 16)
                        cnt["st"] += 16
                        M["qkstash"] = cnt["st"]
                        if part == "k":
                            sy.wait_ge(sdv, M["ktokready"])
                            sy.dma_start(
                                out=KT[:, h * 128:(h + 1) * 128].rearrange(
                                    "(tt p) d -> p tt d", p=128),
                                in_=ktok[:].rearrange("p (tt d) -> p tt d", d=128),
                            ).then_inc(sst, 16)
                            cnt["st"] += 16
                            M["ktstash"] = cnt["st"]

                @block.tensor
                def _(te, g=g, hh=hh, h=h):
                    te.wait_ge(sdv, M["ktokready"])
                    te.wait_ge(sdv, M[("vev", g)])
                    te.wait_ge(sdv, 1)
                    b = psz[h % 2]
                    te.wait_ge(sdv, M.get(("szfree", h % 2), 0))
                    last = None
                    for tt in range(16):
                        last = te.matmul(
                            out=b[:, 0:HD],
                            lhsT=ktok[:, tt * 128:(tt + 1) * 128],
                            rhs=vtok[:, tt * 512 + hh * 128: tt * 512 + (hh + 1) * 128],
                            start=(tt == 0), stop=False,
                        )
                    for tt in range(16):
                        last = te.matmul(
                            out=b[:, HD:HD + 1],
                            lhsT=ktok[:, tt * 128:(tt + 1) * 128],
                            rhs=ones1[:],
                            start=False, stop=(tt == 15),
                        )
                    last.then_inc(spe, 1)
                    cnt["pe"] += 1
                    M["szmm"] = cnt["pe"]
                    if hh == 3:
                        M[("szdone", g)] = cnt["pe"]

                @block.vector
                def _(dv, h=h, g=g, hh=hh):
                    dv.wait_ge(spe, M["szmm"])
                    dv.wait_ge(sst, M.get("szstash", 0))
                    dv.tensor_copy(out=szev[:], in_=psz[h % 2][:]).then_inc(sdv, 1)
                    cnt["dv"] += 1
                    M[("szfree", h % 2)] = cnt["dv"]
                    if hh == 3:
                        M[("vconsumed", g)] = cnt["dv"]

                @block.sync
                def _(sy, h=h):
                    sy.wait_ge(sdv, M[("szfree", h % 2)])
                    sy.dma_start(out=SZT[h], in_=szev[:]).then_inc(sst, 16)
                    cnt["st"] += 16
                    M["szstash"] = cnt["st"]

        @block.sync
        def _(sy):
            sy.wait_ge(sst, cnt["st"])

    es.close()
    return nc



def build_l2(debug=False):
    nc = bass.Bass("TRN2", target_bir_lowering=False)
    QF = nc.dram_tensor("QF", [D, TPC], BF16, kind="ExternalInput").ap()
    KF = nc.dram_tensor("KF", [D, TPC], BF16, kind="ExternalInput").ap()
    KT = nc.dram_tensor("KT", [TPC, D], BF16, kind="ExternalInput").ap()
    VT = nc.dram_tensor("VT", [TPC, D], BF16, kind="ExternalInput").ap()
    SZALL = nc.dram_tensor("SZALL", [NC, NH, HD, HD + 1], F32, kind="ExternalInput").ap()
    X = nc.dram_tensor("X", [TPC, D], F32, kind="ExternalInput").ap()
    WOUTT = nc.dram_tensor("WOUTT", [D, D], F32, kind="ExternalInput").ap()
    WG = nc.dram_tensor("WG", [D, E], F32, kind="ExternalInput").ap()
    MASK = nc.dram_tensor("MASK", [128, 256], BF16, kind="ExternalInput").ap()
    IDN = nc.dram_tensor("IDN", [128, 128], BF16, kind="ExternalInput").ap()
    IDNF = nc.dram_tensor("IDNF", [128, 128], F32, kind="ExternalInput").ap()
    UT = nc.dram_tensor("UT", [128, 128], F32, kind="ExternalInput").ap()
    ONESR = nc.dram_tensor("ONESR", [1, 128], F32, kind="ExternalInput").ap()
    IOTA8 = nc.dram_tensor("IOTA8", [128, E], F32, kind="ExternalInput").ap()
    CPREFB = nc.dram_tensor("CPREFB", [128, NC], F32, kind="ExternalInput").ap()

    X1T = nc.dram_tensor("X1T", [TPC, D], BF16, kind="ExternalOutput").ap()
    X1F = nc.dram_tensor("X1F", [D, TPC], BF16, kind="ExternalOutput").ap()
    PBUF = nc.dram_tensor("PBUF", [E * CLOC + 1, D], BF16, kind="ExternalOutput").ap()
    CNT = nc.dram_tensor("CNT", [1, E], F32, kind="ExternalOutput").ap()
    EIDX = nc.dram_tensor("EIDX", [128, NTT], F32, kind="ExternalOutput").ap()
    LPOS = nc.dram_tensor("LPOS", [128, NTT], F32, kind="ExternalOutput").ap()
    GATE = nc.dram_tensor("GATE", [128, NTT], F32, kind="ExternalOutput").ap()
    if debug:
        DB_SZB = nc.dram_tensor("DB_SZB", [128, NH * 129], BF16, kind="ExternalOutput").ap()
        DB_S1 = nc.dram_tensor("DB_S1", [128, 2 * D], F32, kind="ExternalOutput").ap()
        DB_ATT = nc.dram_tensor("DB_ATT", [128, 384], BF16, kind="ExternalOutput").ap()
        DB_NUM = nc.dram_tensor("DB_NUM", [128, 258], F32, kind="ExternalOutput").ap()
        DB_RD = nc.dram_tensor("DB_RD", [128, 2], F32, kind="ExternalOutput").ap()
        DB_NUMN = nc.dram_tensor("DB_NUMN", [128, 256], BF16, kind="ExternalOutput").ap()
        DB_ZSB = nc.dram_tensor("DB_ZSB", [8, TPC], F32, kind="ExternalOutput").ap()
        DB_NUMF = nc.dram_tensor("DB_NUMF", [128, NH * 256], BF16, kind="ExternalOutput").ap()

    Tn = nc.alloc_sbuf_tensor
    P = nc.alloc_psum_tensor
    qf = Tn("qf", [128, NH * 256], BF16)
    kf = Tn("kf", [128, NH * 256], BF16)
    kt = Tn("kt", [128, 2 * D], BF16)
    vext = [Tn("vext0", [128, NH * 129], BF16), Tn("vext1", [128, NH * 129], BF16)]
    x0 = Tn("x0", [128, 2 * D], F32)
    wout = Tn("wout", [128, 2 * NH * 512], BF16)
    numfm = Tn("numfm", [128, NH * 256], BF16)
    attsb0 = Tn("attsb0", [128, 256], BF16)
    attsb1 = Tn("attsb1", [128, 128], BF16)
    numn = Tn("numn", [128, 256], BF16)
    szf = Tn("szf", [128, NH * 129], F32)
    szb = Tn("szb", [128, NH * 129], BF16)
    szj = Tn("szj", [128, NH * 129], F32)
    s1 = Tn("s1", [128, 2 * D], F32)
    sqscr = Tn("sqscr", [128, D], F32)
    x1tf = Tn("x1tf", [128, D], F32)
    x1tb = Tn("x1tb", [128, D], BF16)
    x1ff = Tn("x1ff", [128, 16 * 256], F32)
    x1fb = Tn("x1fb", [128, 16 * 256], BF16)
    wg = Tn("wg", [128, 16 * E], F32)
    zsb = Tn("zsb", [8, TPC], F32)
    mask = Tn("mask", [128, 256], BF16)
    idn = Tn("idn", [128, 128], BF16)
    idnf = Tn("idnf", [128, 128], F32)
    ut = Tn("ut", [128, 128], F32)
    onesr = Tn("onesr", [1, 128], F32)
    iota8 = Tn("iota8", [128, E], F32)
    cprefb = Tn("cprefb", [128, NC], F32)
    rd = [Tn("rd0", [128, 1], F32), Tn("rd1", [128, 1], F32)]
    dencp = Tn("dencp", [128, 4], F32)
    stats = Tn("stats", [128, 8], F32)
    ztile = Tn("ztile", [128, D], BF16)
    xrow = Tn("xrow", [128, D], BF16)
    zt = Tn("zt", [128, E], F32)
    oh0 = Tn("oh0", [128, E], F32)
    ohsc = Tn("ohsc", [128, E], F32)
    base8 = Tn("base8", [1, E], F32)
    negm = Tn("negm", [128, 1], F32)
    esum = Tn("esum", [128, 1], F32)
    escr8 = Tn("escr8", [128, E], F32)
    possel = Tn("possel", [128, 1], F32)
    q1 = Tn("q1", [128, 1], F32)
    okm = Tn("okm", [128, 1], F32)
    gate16 = Tn("gate16", [128, 16], F32)
    eidx16 = Tn("eidx16", [128, 16], F32)
    lpos16 = Tn("lpos16", [128, 16], F32)
    pidx16 = Tn("pidx16", [128, 16], I32)
    if debug:
        dbatt = Tn("dbatt", [128, 384], BF16)
        dbnum = Tn("dbnum", [128, 258], F32)
        dbrd = Tn("dbrd", [128, 2], F32)
        dbnumn = Tn("dbnumn", [128, 256], BF16)

    patt = P("patt", [128, 384], F32)
    patt0 = patt[:, 0:256]
    patt1 = patt[:, 256:384]
    pnum = P("pnum", [128, 2 * (HD + 1)], F32)
    pnum0 = pnum[:, 0:HD + 1]
    pnum1 = pnum[:, HD + 1:2 * (HD + 1)]
    ptpt = P("ptpt", [128, 256], BF16)
    ptp = [ptpt[:, 0:128], ptpt[:, 128:256]]
    pout = [P("pout0", [128, 512], F32), P("pout1", [128, 512], F32)]
    px1t = [P("px1t0", [128, 128], F32), P("px1t1", [128, 128], F32)]
    pbig = P("pbig", [128, 512], F32)
    pz = pbig[0:8, 0:256]
    pszd = pbig[:, 256:256 + HD + 1]
    pcum = pbig[:, 392:392 + E]
    pt8 = pbig[0:1, 400:400 + E]

    es = contextlib.ExitStack()
    sgp = es.enter_context(nc.semaphore("sgp"))
    sld = es.enter_context(nc.semaphore("sld"))
    spe = es.enter_context(nc.semaphore("spe"))
    sdv = es.enter_context(nc.semaphore("sdv"))
    sac = es.enter_context(nc.semaphore("sac"))
    sst = es.enter_context(nc.semaphore("sst"))

    cnt = {"gp": 0, "ld": 0, "pe": 0, "dv": 0, "ac": 0, "st": 0}
    M = {}

    def bar(dv, inst):
        inst.then_inc(sdv, 1)
        cnt["dv"] += 1
        dv.wait_ge(sdv, cnt["dv"])

    with nc.Block() as block:
        @block.sync
        def _(sy):
            for ap_s, ap_d in ((MASK, mask), (IDN, idn), (IDNF, idnf), (UT, ut),
                               (ONESR, onesr), (IOTA8, iota8), (CPREFB, cprefb)):
                sy.dma_start(out=ap_d[:], in_=ap_s).then_inc(sld, 16)
                cnt["ld"] += 16
            sy.dma_start(out=wg[:], in_=WG.rearrange("(c p) e -> p c e", p=128)).then_inc(sld, 16)
            cnt["ld"] += 16
            M["consts"] = cnt["ld"]
            for j in range(NC):
                if j > 0:
                    sy.wait_ge(sdv, 1 + 16 * j)
                sy.dma_start(out=szj[:],
                             in_=SZALL[j].rearrange("h p e -> p h e")).then_inc(sld, 16)
                cnt["ld"] += 16
                M[("szjld", j)] = cnt["ld"]

        @block.vector
        def _(dv):
            dv.memset(szf[:], 0.0).then_inc(sdv, 1)
            cnt["dv"] += 1
            for j in range(NC):
                dv.wait_ge(sld, M[("szjld", j)])
                for h in range(NH):
                    sl = slice(h * 129, (h + 1) * 129)
                    dv.tensor_scalar(out=szj[:, sl], in0=szj[:, sl],
                                     scalar1=cprefb[:, j:j + 1], scalar2=None,
                                     op0=ALU.mult)
                    dv.tensor_add(out=szf[:, sl], in0=szf[:, sl], in1=szj[:, sl]).then_inc(sdv, 1)
                    cnt["dv"] += 1
            dv.tensor_copy(out=szb[:], in_=szf[:]).then_inc(sdv, 1)
            cnt["dv"] += 1
            dv.memset(vext[0][:], 1.0)
            dv.memset(vext[1][:], 1.0)
            dv.memset(ztile[:], 0.0)
            dv.memset(base8[:], 0.0).then_inc(sdv, 1)
            cnt["dv"] += 1
            M["szinit"] = cnt["dv"]

        if debug:
            @block.sync
            def _(sy):
                sy.wait_ge(sdv, M["szinit"])
                sy.dma_start(out=DB_SZB, in_=szb[:]).then_inc(sst, 16)
                cnt["st"] += 16
                M["dbszb"] = cnt["st"]

        @block.gpsimd
        def _(gp):
            gp.wait_ge(sdv, M["szinit"])
            for b in range(32):
                gp.dma_start(out=PBUF[b * 128:(b + 1) * 128, :], in_=ztile[:]).then_inc(sgp, 16)
                cnt["gp"] += 16
            gp.dma_start(out=PBUF[E * CLOC:E * CLOC + 1, :], in_=ztile[0:1, :]).then_inc(sgp, 16)
            cnt["gp"] += 16
            M["pbufz"] = cnt["gp"]

        for c in range(NCH):
            @block.sync
            def _(sy, c=c):
                if c > 0:
                    sy.wait_ge(sdv, M[("chunkdone", c - 1)])
                    sy.wait_ge(sst, cnt["st"])
                else:
                    sy.wait_ge(sdv, M["szinit"])
                sy.dma_start(out=qf[:].rearrange("p (h t) -> p h t", t=256),
                             in_=QF[:, c * 256:(c + 1) * 256].rearrange("(h p) t -> p h t", p=128)).then_inc(sld, 16)
                sy.dma_start(out=kf[:].rearrange("p (h t) -> p h t", t=256),
                             in_=KF[:, c * 256:(c + 1) * 256].rearrange("(h p) t -> p h t", p=128)).then_inc(sld, 16)
                sy.dma_start(out=kt[:].rearrange("p (m d) -> p m d", d=D),
                             in_=KT[c * 256:(c + 1) * 256, :].rearrange("(m p) d -> p m d", p=128)).then_inc(sld, 16)
                cnt["ld"] += 48
                for mt in range(2):
                    sy.dma_start(
                        out=vext[mt][:].rearrange("p (h e) -> p h e", e=129)[:, :, 0:HD],
                        in_=VT[c * 256 + mt * 128: c * 256 + (mt + 1) * 128, :].rearrange("p (h d) -> p h d", d=HD),
                    ).then_inc(sld, 16)
                    cnt["ld"] += 16
                sy.dma_start(out=x0[:].rearrange("p (m d) -> p m d", d=D),
                             in_=X[c * 256:(c + 1) * 256, :].rearrange("(m p) d -> p m d", p=128)).then_inc(sld, 16)
                cnt["ld"] += 16
                M[("chld", c)] = cnt["ld"]

            for h in range(NH):
                pe0 = cnt["pe"]
                dv0 = cnt["dv"]
                ac0 = cnt["ac"]

                @block.tensor
                def _(te, c=c, h=h, pe0=pe0, dv0=dv0, ac0=ac0):
                    qsl = slice(h * 256, h * 256 + 256)
                    q0 = slice(h * 256, h * 256 + 128)
                    q1s = slice(h * 256 + 128, h * 256 + 256)
                    vsl = slice(h * 129, (h + 1) * 129)
                    te.wait_ge(sld, M[("chld", c)])
                    te.wait_ge(sdv, M["szinit"])
                    if c > 0:
                        te.wait_ge(sdv, M[("szcast", c - 1, h)])
                    prevmask = M.get(("maskev", c, h - 1), M.get(("maskev", c - 1, NH - 1), 0))
                    te.wait_ge(sdv, prevmask)
                    te.matmul(out=patt0, lhsT=kf[:, q0], rhs=qf[:, qsl],
                              start=True, stop=True).then_inc(spe, 1)
                    te.matmul(out=patt1, lhsT=kf[:, q1s], rhs=qf[:, q1s],
                              start=True, stop=True).then_inc(spe, 1)
                    cnt["pe"] += 2
                    te.wait_ge(sdv, dv0 + 1)
                    te.wait_ge(sac, ac0)
                    te.matmul(out=pnum0, lhsT=attsb0[:, 0:128], rhs=vext[0][:, vsl],
                              start=True, stop=False)
                    te.matmul(out=pnum0, lhsT=qf[:, q0], rhs=szb[:, vsl],
                              start=False, stop=True).then_inc(spe, 1)
                    te.matmul(out=pnum1, lhsT=attsb0[:, 128:256], rhs=vext[0][:, vsl],
                              start=True, stop=False)
                    te.matmul(out=pnum1, lhsT=attsb1[:], rhs=vext[1][:, vsl],
                              start=False, stop=False)
                    te.matmul(out=pnum1, lhsT=qf[:, q1s], rhs=szb[:, vsl],
                              start=False, stop=True).then_inc(spe, 1)
                    cnt["pe"] += 2
                    prevszc = M.get(("szcast", c, h - 1), M.get(("szcast", c - 1, NH - 1), 0))
                    te.wait_ge(sdv, prevszc)
                    te.matmul(out=pszd, lhsT=kt[:, h * 128:(h + 1) * 128],
                              rhs=vext[0][:, vsl], start=True, stop=False)
                    te.matmul(out=pszd, lhsT=kt[:, D + h * 128:D + (h + 1) * 128],
                              rhs=vext[1][:, vsl], start=False, stop=True).then_inc(spe, 1)
                    cnt["pe"] += 1
                    te.wait_ge(sac, ac0 + 2)
                    te.transpose(out=ptp[0], in_=numn[:, 0:128], identity=idn[:]).then_inc(spe, 1)
                    te.transpose(out=ptp[1], in_=numn[:, 128:256], identity=idn[:]).then_inc(spe, 1)
                    cnt["pe"] += 2

                @block.vector
                def _(dv, c=c, h=h, pe0=pe0):
                    dv.wait_ge(spe, pe0 + 2)
                    dv.tensor_mul(out=attsb0[:], in0=patt0, in1=mask[:])
                    dv.tensor_mul(out=attsb1[:], in0=patt1, in1=mask[:, 0:128]).then_inc(sdv, 1)
                    cnt["dv"] += 1
                    M[("maskev", c, h)] = cnt["dv"]
                    if debug and c == 0 and h == 0:
                        dv.tensor_copy(out=dbatt[:, 0:256], in_=attsb0[:])
                        dv.tensor_copy(out=dbatt[:, 256:384], in_=attsb1[:])
                    dv.wait_ge(spe, pe0 + 4)
                    dv.tensor_copy(out=dencp[:, 0:2], in_=pnum0[:, HD - 1:HD + 1])
                    bar(dv, dv.tensor_copy(out=dencp[:, 2:4], in_=pnum1[:, HD - 1:HD + 1]))
                    dv.tensor_scalar_max(out=rd[0][:], in0=dencp[:, 1:2], scalar1=1e-6)
                    bar(dv, dv.tensor_scalar_max(out=rd[1][:], in0=dencp[:, 3:4], scalar1=1e-6))
                    dv.reciprocal(out=rd[0][:], in_=rd[0][:])
                    dv.reciprocal(out=rd[1][:], in_=rd[1][:]).then_inc(sdv, 1)
                    cnt["dv"] += 1
                    M[("denrd", c, h)] = cnt["dv"]
                    if debug and c == 0 and h == 0:
                        dv.tensor_copy(out=dbnum[:, 0:129], in_=pnum0)
                        dv.tensor_copy(out=dbnum[:, 129:258], in_=pnum1)
                        dv.tensor_copy(out=dbrd[:, 0:1], in_=rd[0][:])
                        dv.tensor_copy(out=dbrd[:, 1:2], in_=rd[1][:])
                    if c == 0 and h == 0 and "dbszb" in M:
                        dv.wait_ge(sst, M["dbszb"])
                    dv.wait_ge(spe, pe0 + 5)
                    vsl = slice(h * 129, (h + 1) * 129)
                    bar(dv, dv.tensor_add(out=szf[:, vsl], in0=szf[:, vsl], in1=pszd))
                    dv.tensor_copy(out=szb[:, vsl], in_=szf[:, vsl]).then_inc(sdv, 1)
                    cnt["dv"] += 1
                    M[("szcast", c, h)] = cnt["dv"]
                    dv.wait_ge(spe, pe0 + 7)
                    if debug and c == 0 and h == 0:
                        dv.tensor_copy(out=dbnumn[:], in_=numn[:])
                    dv.tensor_copy(out=numfm[:, h * 256:h * 256 + 128], in_=ptp[0])
                    dv.tensor_copy(out=numfm[:, h * 256 + 128:h * 256 + 256], in_=ptp[1]).then_inc(sdv, 1)
                    cnt["dv"] += 1
                    M[("numfm", c, h)] = cnt["dv"]

                @block.scalar
                def _(ac, c=c, h=h):
                    ac.wait_ge(sdv, M[("denrd", c, h)])
                    ac.activation(out=numn[:, 0:128], in_=pnum0[:, 0:HD], func=AF.Copy,
                                  scale=rd[0][:, 0:1]).then_inc(sac, 1)
                    ac.activation(out=numn[:, 128:256], in_=pnum1[:, 0:HD], func=AF.Copy,
                                  scale=rd[1][:, 0:1]).then_inc(sac, 1)
                    cnt["ac"] += 2

            if debug and c == 0:
                @block.sync
                def _(sy):
                    sy.wait_ge(sdv, M[("numfm", 0, 0)])
                    sy.dma_start(out=DB_ATT, in_=dbatt[:]).then_inc(sst, 16)
                    sy.dma_start(out=DB_NUM, in_=dbnum[:]).then_inc(sst, 16)
                    sy.dma_start(out=DB_RD, in_=dbrd[:]).then_inc(sst, 16)
                    sy.dma_start(out=DB_NUMN, in_=dbnumn[:]).then_inc(sst, 16)
                    cnt["st"] += 64

                @block.sync
                def _(sy):
                    sy.wait_ge(sdv, M[("numfm", 0, NH - 1)])
                    sy.dma_start(out=DB_NUMF, in_=numfm[:]).then_inc(sst, 16)
                    cnt["st"] += 16

            dv1 = cnt["dv"]
            pe1 = cnt["pe"]
            for oc in range(4):
                @block.gpsimd
                def _(gp, c=c, oc=oc):
                    slot = oc % 2
                    key = ("woutfree", slot)
                    if key in M:
                        gp.wait_ge(spe, M[key])
                    gp.dma_start(
                        out=wout[:, slot * NH * 512:(slot + 1) * NH * 512].rearrange("p (hb f) -> p hb f", f=512),
                        in_=WOUTT[:, oc * 512:(oc + 1) * 512].rearrange("(hb p) f -> p hb f", p=128),
                    ).then_inc(sgp, 16)
                    cnt["gp"] += 16
                    M[("woutld", c, oc)] = cnt["gp"]

                @block.tensor
                def _(te, c=c, oc=oc, pe1=pe1, dv1=dv1):
                    te.wait_ge(sdv, M[("numfm", c, NH - 1)])
                    te.wait_ge(sgp, M[("woutld", c, oc)])
                    slot = oc % 2
                    for lt in range(2):
                        k = 2 * oc + lt
                        if k >= 2:
                            te.wait_ge(sdv, dv1 + k - 1)
                        last = None
                        for h in range(NH):
                            last = te.matmul(
                                out=pout[lt][:],
                                lhsT=numfm[:, h * 256 + lt * 128: h * 256 + (lt + 1) * 128],
                                rhs=wout[:, slot * NH * 512 + h * 512: slot * NH * 512 + (h + 1) * 512],
                                start=(h == 0), stop=(h == NH - 1),
                            )
                        last.then_inc(spe, 1)
                        cnt["pe"] += 1
                    M[("woutfree", slot)] = cnt["pe"]

            @block.vector
            def _(dv, c=c, pe1=pe1):
                for oc in range(4):
                    for lt in range(2):
                        k = 2 * oc + lt
                        dv.wait_ge(spe, pe1 + k + 1)
                        dv.scalar_tensor_tensor(
                            out=s1[:, lt * D + oc * 512: lt * D + (oc + 1) * 512],
                            in0=pout[lt][:], scalar=1.0,
                            in1=x0[:, lt * D + oc * 512: lt * D + (oc + 1) * 512],
                            op0=ALU.mult, op1=ALU.add,
                        ).then_inc(sdv, 1)
                        cnt["dv"] += 1
                M[("s1done", c)] = cnt["dv"]

            if debug and c == 0:
                @block.sync
                def _(sy):
                    sy.wait_ge(sdv, M[("s1done", 0)])
                    sy.dma_start(out=DB_S1, in_=s1[:]).then_inc(sst, 16)
                    cnt["st"] += 16

            for lt in range(2):
                dv2 = cnt["dv"]
                ac2 = cnt["ac"]
                pe2 = cnt["pe"]

                @block.vector
                def _(dv, c=c, lt=lt, ac2=ac2, pe2=pe2):
                    dv.wait_ge(sdv, M[("s1done", c)])
                    dv.reduce_sum(out=stats[:, 0:1], in_=s1[:, lt * D:(lt + 1) * D], axis=AX.X).then_inc(sdv, 1)
                    cnt["dv"] += 1
                    dv.wait_ge(sac, ac2 + 1)
                    dv.tensor_scalar_mul(out=stats[:, 0:1], in0=stats[:, 0:1], scalar1=1.0 / D)
                    bar(dv, dv.tensor_scalar_mul(out=stats[:, 1:2], in0=stats[:, 1:2], scalar1=1.0 / D))
                    bar(dv, dv.tensor_mul(out=stats[:, 6:7], in0=stats[:, 0:1], in1=stats[:, 0:1]))
                    bar(dv, dv.tensor_sub(out=stats[:, 2:3], in0=stats[:, 1:2], in1=stats[:, 6:7]))
                    dv.tensor_scalar_add(out=stats[:, 2:3], in0=stats[:, 2:3], scalar1=1e-5).then_inc(sdv, 1)
                    cnt["dv"] += 1
                    M[("var", c, lt)] = cnt["dv"]
                    dv.wait_ge(sac, ac2 + 2)
                    dv.tensor_copy(out=stats[:, 4:5], in_=stats[:, 3:4]).then_inc(sdv, 1)
                    cnt["dv"] += 1
                    M[("lnv2", c, lt)] = cnt["dv"]
                    dv.wait_ge(sac, ac2 + 3)
                    dv.wait_ge(sdv, cnt["dv"])
                    dv.tensor_scalar(out=x1tf[:], in0=s1[:, lt * D:(lt + 1) * D],
                                     scalar1=stats[:, 0:1], scalar2=stats[:, 5:6],
                                     op0=ALU.subtract, op1=ALU.mult).then_inc(sdv, 1)
                    cnt["dv"] += 1
                    dv.wait_ge(sst, M.get("x1stash", 0))
                    dv.tensor_copy(out=x1tb[:], in_=x1tf[:]).then_inc(sdv, 1)
                    cnt["dv"] += 1
                    M[("x1tb", c, lt)] = cnt["dv"]
                    for c16 in range(16):
                        dv.wait_ge(spe, pe2 + c16 + 1)
                        dv.tensor_copy(out=x1ff[:, c16 * 256 + lt * 128: c16 * 256 + lt * 128 + 128],
                                       in_=px1t[c16 % 2][:]).then_inc(sdv, 1)
                        cnt["dv"] += 1
                    M[("x1ff", c, lt)] = cnt["dv"]

                @block.scalar
                def _(ac, c=c, lt=lt, dv2=dv2):
                    ac.wait_ge(sdv, dv2 + 1)
                    ac.activation(out=sqscr[:], in_=s1[:, lt * D:(lt + 1) * D], func=AF.Square,
                                  accum_out=stats[:, 1:2]).then_inc(sac, 1)
                    ac.wait_ge(sdv, M[("var", c, lt)])
                    ac.activation(out=stats[:, 3:4], in_=stats[:, 2:3], func=AF.Ln).then_inc(sac, 1)
                    ac.wait_ge(sdv, M[("lnv2", c, lt)])
                    ac.activation(out=stats[:, 5:6], in_=stats[:, 4:5], func=AF.Exp,
                                  scale=-0.5).then_inc(sac, 1)
                    cnt["ac"] += 3

                @block.sync
                def _(sy, c=c, lt=lt):
                    sy.wait_ge(sdv, M[("x1tb", c, lt)])
                    sy.dma_start(out=X1T[c * 256 + lt * 128: c * 256 + (lt + 1) * 128, :],
                                 in_=x1tb[:]).then_inc(sst, 16)
                    cnt["st"] += 16
                    M["x1stash"] = cnt["st"]
                    M[("x1stash", c, lt)] = cnt["st"]

                @block.tensor
                def _(te, c=c, lt=lt, pe2=pe2):
                    te.wait_ge(sdv, M[("x1tb", c, lt)])
                    base_dv = M[("x1tb", c, lt)]
                    for c16 in range(16):
                        if c16 >= 2:
                            te.wait_ge(sdv, base_dv + c16 - 1)
                        te.transpose(out=px1t[c16 % 2][:],
                                     in_=x1tf[:, c16 * 128:(c16 + 1) * 128],
                                     identity=idnf[:]).then_inc(spe, 1)
                        cnt["pe"] += 1

            @block.tensor
            def _(te, c=c):
                te.wait_ge(sdv, M[("x1ff", c, 1)])
                last = None
                for c16 in range(16):
                    last = te.matmul(out=pz, lhsT=wg[:, c16 * E:(c16 + 1) * E],
                                     rhs=x1ff[:, c16 * 256:(c16 + 1) * 256],
                                     start=(c16 == 0), stop=(c16 == 15))
                last.then_inc(spe, 1)
                cnt["pe"] += 1
                M[("zmm", c)] = cnt["pe"]

            @block.vector
            def _(dv, c=c):
                dv.wait_ge(spe, M[("zmm", c)])
                dv.wait_ge(sdv, M[("x1ff", c, 1)])
                dv.tensor_copy(out=zsb[:, c * 256:(c + 1) * 256], in_=pz).then_inc(sdv, 1)
                cnt["dv"] += 1
                dv.tensor_copy(out=x1fb[:], in_=x1ff[:]).then_inc(sdv, 1)
                cnt["dv"] += 1
                M[("x1fb", c)] = cnt["dv"]
                M[("chunkdone", c)] = cnt["dv"]

            @block.sync
            def _(sy, c=c):
                sy.wait_ge(sdv, M[("x1fb", c)])
                sy.dma_start(out=X1F[:, c * 256:(c + 1) * 256].rearrange("(c16 p) t -> p c16 t", p=128),
                             in_=x1fb[:].rearrange("p (c16 t) -> p c16 t", t=256)).then_inc(sst, 16)
                cnt["st"] += 16
                M[("x1fstash", c)] = cnt["st"]

        if debug:
            @block.sync
            def _(sy):
                sy.wait_ge(sdv, M[("chunkdone", NCH - 1)])
                sy.dma_start(out=DB_ZSB, in_=zsb[:]).then_inc(sst, 16)
                cnt["st"] += 16

        for tt in range(NTT):
            pe4 = cnt["pe"]
            dv4 = cnt["dv"]
            ac4 = cnt["ac"]

            @block.tensor
            def _(te, tt=tt):
                te.wait_ge(sdv, M[("chunkdone", NCH - 1)])
                if tt > 0:
                    te.wait_ge(sdv, M[("pidx", tt - 1)])
                te.transpose(out=pcum, in_=zsb[0:8, tt * 128:(tt + 1) * 128],
                             identity=idnf[0:8, 0:8]).then_inc(spe, 1)
                cnt["pe"] += 1

            @block.vector
            def _(dv, tt=tt, pe4=pe4, ac4=ac4):
                dv.wait_ge(spe, pe4 + 1)
                bar(dv, dv.tensor_copy(out=zt[:], in_=pcum))
                cnt["dv"] += 0
                M[("ztev", tt)] = cnt["dv"]
                bar(dv, dv.reduce_max(out=negm[:], in_=zt[:], axis=AX.X, negate=True))
                sdvmark = cnt["dv"]
                dv.then_maybe_inc(None) if False else None
                ncnt = cnt["dv"] + 1
                cnt["dv"] = ncnt - 1
                dv.sem_inc(sdv, 1)
                cnt["dv"] += 1
                M[("negmready", tt)] = cnt["dv"]
                dv.wait_ge(sac, ac4 + 1)
                bar(dv, dv.reciprocal(out=gate16[:, tt:tt + 1], in_=esum[:]))
                bar(dv, dv.tensor_scalar(out=oh0[:], in0=zt[:], scalar1=negm[:, 0:1], scalar2=0.0,
                                         op0=ALU.add, op1=ALU.is_ge))
                bar(dv, dv.tensor_tensor_scan(out=ohsc[:], data0=oh0[:], data1=oh0[:],
                                              initial=0.0, op0=ALU.add, op1=ALU.bypass))
                bar(dv, dv.tensor_single_scalar(out=ohsc[:], in_=ohsc[:], scalar=1.0, op=ALU.is_equal))
                bar(dv, dv.tensor_mul(out=oh0[:], in0=oh0[:], in1=ohsc[:]))
                bar(dv, dv.tensor_mul(out=ohsc[:], in0=oh0[:], in1=iota8[:]))
                dv.reduce_sum(out=eidx16[:, tt:tt + 1], in_=ohsc[:], axis=AX.X).then_inc(sdv, 1)
                cnt["dv"] += 1
                M[("ohready", tt)] = cnt["dv"]

            @block.scalar
            def _(ac, tt=tt):
                ac.wait_ge(sdv, M[("negmready", tt)])
                ac.activation(out=escr8[:], in_=zt[:], func=AF.Exp, bias=negm[:, 0:1],
                              accum_out=esum[:, 0:1]).then_inc(sac, 1)
                cnt["ac"] += 1

            @block.tensor
            def _(te, tt=tt):
                te.wait_ge(sdv, M[("ohready", tt)])
                te.matmul(out=pcum, lhsT=ut[:], rhs=oh0[:], start=True, stop=False)
                te.matmul(out=pcum, lhsT=onesr[:], rhs=base8[:], start=False,
                          stop=True).then_inc(spe, 1)
                te.matmul(out=pt8, lhsT=ut[:, 127:128], rhs=oh0[:], start=True,
                          stop=True).then_inc(spe, 1)
                cnt["pe"] += 2
                M[("cummm", tt)] = cnt["pe"]

            @block.vector
            def _(dv, tt=tt):
                dv.wait_ge(spe, M[("cummm", tt)])
                bar(dv, dv.tensor_mul(out=ohsc[:], in0=oh0[:], in1=pcum))
                bar(dv, dv.reduce_sum(out=possel[:], in_=ohsc[:], axis=AX.X))
                bar(dv, dv.tensor_add(out=base8[:], in0=base8[:], in1=pt8))
                bar(dv, dv.tensor_scalar_sub(out=lpos16[:, tt:tt + 1], in0=possel[:], scalar1=1.0))
                bar(dv, dv.tensor_single_scalar(out=okm[:], in_=possel[:], scalar=float(CLOC), op=ALU.is_le))
                bar(dv, dv.scalar_tensor_tensor(out=q1[:], in0=eidx16[:, tt:tt + 1], scalar=float(CLOC),
                                                in1=possel[:], op0=ALU.mult, op1=ALU.add))
                bar(dv, dv.tensor_scalar_sub(out=q1[:], in0=q1[:], scalar1=float(E * CLOC + 1)))
                bar(dv, dv.tensor_mul(out=q1[:], in0=q1[:], in1=okm[:]))
                bar(dv, dv.tensor_scalar_add(out=q1[:], in0=q1[:], scalar1=float(E * CLOC)))
                dv.tensor_copy(out=pidx16[:, tt:tt + 1], in_=q1[:]).then_inc(sdv, 1)
                cnt["dv"] += 1
                M[("pidx", tt)] = cnt["dv"]

        @block.gpsimd
        def _(gp):
            gp.wait_ge(sgp, M["pbufz"])
            gp.wait_ge(sst, M[("x1stash", NCH - 1, 1)])
            for tt in range(NTT):
                gp.wait_ge(sdv, M[("pidx", tt)])
                gp.dma_start(out=xrow[:], in_=X1T[tt * 128:(tt + 1) * 128, :]).then_inc(sgp, 16)
                cnt["gp"] += 16
                gp.wait_ge(sgp, cnt["gp"])
                gp.indirect_dma_start(
                    out=PBUF, out_offset=IndirectOffsetOnAxis(pidx16[:, tt:tt + 1], 0),
                    in_=xrow[:], in_offset=None,
                ).then_inc(sgp, 16)
                cnt["gp"] += 16
                gp.wait_ge(sgp, cnt["gp"])
            M["scatdone"] = cnt["gp"]

        @block.sync
        def _(sy):
            sy.wait_ge(sdv, M[("pidx", NTT - 1)])
            sy.dma_start(out=CNT, in_=base8[:]).then_inc(sst, 16)
            sy.dma_start(out=EIDX, in_=eidx16[:]).then_inc(sst, 16)
            sy.dma_start(out=LPOS, in_=lpos16[:]).then_inc(sst, 16)
            sy.dma_start(out=GATE, in_=gate16[:]).then_inc(sst, 16)
            cnt["st"] += 64
            sy.wait_ge(sst, cnt["st"])
            sy.wait_ge(sgp, M["scatdone"])

    es.close()
    return nc


def build_l3():
    nc = bass.Bass("TRN2", target_bir_lowering=False)
    RUNS = nc.dram_tensor("RUNS", [NC * CLOC, D], BF16, kind="ExternalInput").ap()
    CNTA = nc.dram_tensor("CNTA", [NC, E], F32, kind="ExternalInput").ap()
    X1F = nc.dram_tensor("X1F", [D, TPC], BF16, kind="ExternalInput").ap()
    X1T = nc.dram_tensor("X1T", [TPC, D], BF16, kind="ExternalInput").ap()
    W13E = nc.dram_tensor("W13E", [D, 2 * HF], F32, kind="ExternalInput").ap()
    W2E = nc.dram_tensor("W2E", [HF, D], F32, kind="ExternalInput").ap()
    SW13 = nc.dram_tensor("SW13", [D, 2 * HF], F32, kind="ExternalInput").ap()
    SW2 = nc.dram_tensor("SW2", [HF, D], F32, kind="ExternalInput").ap()
    IDN = nc.dram_tensor("IDN", [128, 128], BF16, kind="ExternalInput").ap()
    EJC = nc.dram_tensor("EJC", [8, NC * 128], F32, kind="ExternalInput").ap()
    SELB = nc.dram_tensor("SELB", [8, E], F32, kind="ExternalInput").ap()
    SLT = nc.dram_tensor("SLT", [8, 8], F32, kind="ExternalInput").ap()
    LPC = nc.dram_tensor("LPC", [128, 4], F32, kind="ExternalInput").ap()
    SLC = nc.dram_tensor("SLC", [128, 16], F32, kind="ExternalInput").ap()

    RETP = nc.dram_tensor("RETP", [NC * CLOC + 1, D], BF16, kind="ExternalOutput").ap()
    S2P = nc.dram_tensor("S2P", [TPC, D], F32, kind="ExternalOutput").ap()

    XD = nc.dram_tensor("XD", [CAP + 1, D], BF16, kind="Internal").ap()

    NHC = HF // 128
    SCW = 1024
    NSC = CAP // SCW
    RT = SCW // 128
    NOC = D // 256

    Tn = nc.alloc_sbuf_tensor
    P = nc.alloc_psum_tensor
    cnta = Tn("cnta", [8, E], F32)
    selb = Tn("selb", [8, E], F32)
    slt = Tn("slt", [8, 8], F32)
    ejc = Tn("ejc", [8, NC * 128], F32)
    lpc = Tn("lpc", [128, 4], F32)
    slc = Tn("slc", [128, 16], F32)
    idn = Tn("idn", [128, 128], BF16)
    scr8 = Tn("scr8", [8, E], F32)
    gcol = Tn("gcol", [8, 2], F32)
    gbb = Tn("gbb", [128, 16], F32)
    ubb = Tn("ubb", [128, 8], F32)
    t1 = Tn("t1", [128, 1], F32)
    t2 = Tn("t2", [128, 1], F32)
    t3 = Tn("t3", [128, 1], F32)
    acc = Tn("acc", [128, 1], F32)
    vany = Tn("vany", [128, 1], F32)
    desti = Tn("desti", [128, 32], I32)
    retidx = Tn("retidx", [128, 16], I32)
    ztile = Tn("ztile", [128, D], BF16)
    rbuf = Tn("rbuf", [128, D], BF16)
    xsf = Tn("xsf", [128, 2 * D], BF16)
    xdf = Tn("xdf", [128, 16 * SCW], BF16)
    hfm = Tn("hfm", [128, NHC * SCW], BF16)
    hs = Tn("hs", [128, 512], BF16)
    w13 = Tn("w13", [128, 2 * 16 * 256], BF16)
    w2s = Tn("w2s", [128, 2 * NHC * 256], BF16)
    oesb = Tn("oesb", [128, RT * D], BF16)
    s2oc = Tn("s2oc", [128, RT * 256], F32)
    x1r = Tn("x1r", [128, 2 * D], BF16)

    paux = P("paux", [128, 512], F32)
    pgb = paux[0:8, 0:8]
    pbc = paux[:, 8:24]
    h1p = P("h1p", [128, 512], F32)
    h3p = P("h3p", [128, 512], F32)
    oep = [P("oep0", [128, 256], F32), P("oep1", [128, 256], F32)]
    ptpa = P("ptpa", [128, 128], BF16)
    ptpb = P("ptpb", [128, 128], BF16)
    ptp2 = [ptpa, ptpb]

    es = contextlib.ExitStack()
    sgp = es.enter_context(nc.semaphore("sgp"))
    sld = es.enter_context(nc.semaphore("sld"))
    spe = es.enter_context(nc.semaphore("spe"))
    sdv = es.enter_context(nc.semaphore("sdv"))
    sac = es.enter_context(nc.semaphore("sac"))
    sst = es.enter_context(nc.semaphore("sst"))

    cnt = {"gp": 0, "ld": 0, "pe": 0, "dv": 0, "ac": 0, "st": 0}
    M = {}

    def bar(dv, inst):
        inst.then_inc(sdv, 1)
        cnt["dv"] += 1
        dv.wait_ge(sdv, cnt["dv"])

    with nc.Block() as block:
        @block.sync
        def _(sy):
            for ap_s, ap_d in ((CNTA, cnta), (SELB, selb), (SLT, slt), (EJC, ejc),
                               (LPC, lpc), (SLC, slc), (IDN, idn)):
                sy.dma_start(out=ap_d[:], in_=ap_s).then_inc(sld, 16)
                cnt["ld"] += 16
            M["consts"] = cnt["ld"]

        @block.tensor
        def _(te):
            te.wait_ge(sld, M["consts"])
            te.matmul(out=pgb, lhsT=slt[:], rhs=cnta[:], start=True, stop=True).then_inc(spe, 1)
            cnt["pe"] += 1
            M["gbmm"] = cnt["pe"]

        @block.vector
        def _(dv):
            dv.wait_ge(spe, M["gbmm"])
            bar(dv, dv.tensor_mul(out=scr8[:], in0=pgb, in1=selb[:]))
            bar(dv, dv.reduce_sum(out=gcol[0:8, 0:1], in_=scr8[:], axis=AX.X))
            bar(dv, dv.tensor_mul(out=scr8[:], in0=cnta[:], in1=selb[:]))
            bar(dv, dv.reduce_sum(out=gcol[0:8, 1:2], in_=scr8[:], axis=AX.X))
            M["gcol"] = cnt["dv"]

        @block.tensor
        def _(te):
            te.wait_ge(sdv, M["gcol"])
            last = None
            for j in range(NC):
                te.matmul(out=pbc[:, j:j + 1], lhsT=ejc[:, j * 128:(j + 1) * 128],
                          rhs=gcol[0:8, 0:1], start=True, stop=True)
                last = te.matmul(out=pbc[:, 8 + j: 9 + j], lhsT=ejc[:, j * 128:(j + 1) * 128],
                                 rhs=gcol[0:8, 1:2], start=True, stop=True)
            last.then_inc(spe, 1)
            cnt["pe"] += 1
            M["bcmm"] = cnt["pe"]

        @block.vector
        def _(dv):
            dv.wait_ge(spe, M["bcmm"])
            bar(dv, dv.tensor_copy(out=gbb[:], in_=pbc))
            bar(dv, dv.tensor_scalar_min(out=ubb[:], in0=gbb[:, 8:16], scalar1=float(CLOC)))
            bar(dv, dv.tensor_add(out=ubb[:], in0=ubb[:], in1=gbb[:, 0:8]))
            for tt2 in range(32):
                j, q = tt2 // 4, tt2 % 4
                bar(dv, dv.tensor_add(out=t1[:], in0=lpc[:, q:q + 1], in1=gbb[:, j:j + 1]))
                bar(dv, dv.tensor_tensor(out=t2[:], in0=lpc[:, q:q + 1], in1=gbb[:, 8 + j:9 + j],
                                         op=ALU.is_lt))
                bar(dv, dv.tensor_single_scalar(out=t3[:], in_=t1[:], scalar=float(CAP), op=ALU.is_lt))
                bar(dv, dv.tensor_mul(out=t2[:], in0=t2[:], in1=t3[:]))
                bar(dv, dv.tensor_scalar_sub(out=t1[:], in0=t1[:], scalar1=float(CAP)))
                bar(dv, dv.tensor_mul(out=t1[:], in0=t1[:], in1=t2[:]))
                bar(dv, dv.tensor_scalar_add(out=t1[:], in0=t1[:], scalar1=float(CAP)))
                bar(dv, dv.tensor_copy(out=desti[:, tt2:tt2 + 1], in_=t1[:]))
            M["desti"] = cnt["dv"]
            for stt in range(16):
                bar(dv, dv.memset(acc[:], 0.0))
                bar(dv, dv.memset(vany[:], 0.0))
                for j in range(NC):
                    bar(dv, dv.tensor_tensor(out=t1[:], in0=slc[:, stt:stt + 1],
                                             in1=gbb[:, j:j + 1], op=ALU.is_ge))
                    bar(dv, dv.tensor_tensor(out=t2[:], in0=slc[:, stt:stt + 1],
                                             in1=ubb[:, j:j + 1], op=ALU.is_lt))
                    bar(dv, dv.tensor_mul(out=t1[:], in0=t1[:], in1=t2[:]))
                    bar(dv, dv.tensor_sub(out=t2[:], in0=slc[:, stt:stt + 1], in1=gbb[:, j:j + 1]))
                    bar(dv, dv.tensor_scalar_add(out=t2[:], in0=t2[:], scalar1=float(j * CLOC)))
                    bar(dv, dv.tensor_mul(out=t2[:], in0=t2[:], in1=t1[:]))
                    bar(dv, dv.tensor_add(out=acc[:], in0=acc[:], in1=t2[:]))
                    bar(dv, dv.tensor_add(out=vany[:], in0=vany[:], in1=t1[:]))
                bar(dv, dv.tensor_scalar(out=t1[:], in0=vany[:], scalar1=float(-NC * CLOC),
                                         scalar2=float(NC * CLOC), op0=ALU.mult, op1=ALU.add))
                bar(dv, dv.tensor_add(out=acc[:], in0=acc[:], in1=t1[:]))
                bar(dv, dv.tensor_copy(out=retidx[:, stt:stt + 1], in_=acc[:]))
            dv.memset(ztile[:], 0.0).then_inc(sdv, 1)
            cnt["dv"] += 1
            M["retidx"] = cnt["dv"]

        @block.gpsimd
        def _(gp):
            gp.wait_ge(sdv, M["retidx"])
            for b in range(16):
                gp.dma_start(out=XD[b * 128:(b + 1) * 128, :], in_=ztile[:]).then_inc(sgp, 16)
                cnt["gp"] += 16
            gp.dma_start(out=XD[CAP:CAP + 1, :], in_=ztile[0:1, :]).then_inc(sgp, 16)
            cnt["gp"] += 16
            for b in range(32):
                gp.dma_start(out=RETP[b * 128:(b + 1) * 128, :], in_=ztile[:]).then_inc(sgp, 16)
                cnt["gp"] += 16
            gp.dma_start(out=RETP[NC * CLOC:NC * CLOC + 1, :], in_=ztile[0:1, :]).then_inc(sgp, 16)
            cnt["gp"] += 16
            for tt2 in range(32):
                gp.dma_start(out=rbuf[:], in_=RUNS[tt2 * 128:(tt2 + 1) * 128, :]).then_inc(sgp, 16)
                cnt["gp"] += 16
                gp.wait_ge(sgp, cnt["gp"])
                gp.indirect_dma_start(
                    out=XD, out_offset=IndirectOffsetOnAxis(desti[:, tt2:tt2 + 1], 0),
                    in_=rbuf[:], in_offset=None,
                ).then_inc(sgp, 16)
                cnt["gp"] += 16
                gp.wait_ge(sgp, cnt["gp"])
            M["asm"] = cnt["gp"]

        def quadrant(tag, w13ap, w2ap, x1f_src):
            for sc in range(NSC):
                if tag == "e":
                    for rt in range(RT):
                        @block.gpsimd
                        def _(gp, sc=sc, rt=rt):
                            if rt == 0 and sc == 0:
                                gp.wait_ge(sgp, M["asm"])
                            if ("e", "rtc", sc, rt - 2) in M:
                                gp.wait_ge(spe, M[("e", "rtc", sc, rt - 2)])
                            if ("e", "xdffree", sc - 1) in M and rt == 0:
                                gp.wait_ge(sdv, M[("e", "xdffree", sc - 1)])
                            gp.dma_start(out=xsf[:, (rt % 2) * D:(rt % 2 + 1) * D],
                                         in_=XD[sc * SCW + rt * 128: sc * SCW + (rt + 1) * 128, :]
                                         ).then_inc(sgp, 16)
                            cnt["gp"] += 16
                            M[("e", "row", sc, rt)] = cnt["gp"]

                        dv0 = cnt["dv"]

                        @block.tensor
                        def _(te, sc=sc, rt=rt, dv0=dv0):
                            te.wait_ge(sgp, M[("e", "row", sc, rt)])
                            te.wait_ge(sdv, dv0)
                            if ("e", "xdffree", sc - 1) in M and rt == 0:
                                te.wait_ge(sdv, M[("e", "xdffree", sc - 1)])
                            for c16 in range(16):
                                if c16 >= 2:
                                    te.wait_ge(sdv, dv0 + c16 - 1)
                                te.transpose(out=ptp2[c16 % 2][:],
                                             in_=xsf[:, (rt % 2) * D + c16 * 128: (rt % 2) * D + (c16 + 1) * 128],
                                             identity=idn[:]).then_inc(spe, 1)
                                cnt["pe"] += 1
                            M[("e", "rtc", sc, rt)] = cnt["pe"]

                        pe0 = M[("e", "rtc", sc, rt)] - 16

                        @block.vector
                        def _(dv, sc=sc, rt=rt, pe0=pe0):
                            for c16 in range(16):
                                dv.wait_ge(spe, pe0 + c16 + 1)
                                dv.tensor_copy(
                                    out=xdf[:, c16 * SCW + rt * 128: c16 * SCW + (rt + 1) * 128],
                                    in_=ptp2[c16 % 2][:],
                                ).then_inc(sdv, 1)
                                cnt["dv"] += 1
                            M[(tag, "xdf", sc)] = cnt["dv"]
                else:
                    @block.gpsimd
                    def _(gp, sc=sc):
                        if ("s", "xdffree", sc - 1) in M:
                            gp.wait_ge(sdv, M[("s", "xdffree", sc - 1)])
                        elif ("e", "xdffree", NSC - 1) in M:
                            gp.wait_ge(sdv, M[("e", "xdffree", NSC - 1)])
                        gp.dma_start(
                            out=xdf[:].rearrange("p (c t) -> p c t", t=SCW),
                            in_=x1f_src[:, sc * SCW:(sc + 1) * SCW].rearrange("(c p) t -> p c t", p=128),
                        ).then_inc(sgp, 16)
                        cnt["gp"] += 16
                        M[("s", "xdfld", sc)] = cnt["gp"]

                    @block.vector
                    def _(dv, sc=sc):
                        dv.wait_ge(sgp, M[("s", "xdfld", sc)])
                        dv.sem_inc(sdv, 1)
                        cnt["dv"] += 1
                        M[("s", "xdf", sc)] = cnt["dv"]

                for hc in range(NHC):
                    @block.gpsimd
                    def _(gp, tag=tag, sc=sc, hc=hc):
                        slot = hc % 2
                        key = ("w13cons", slot)
                        if key in M:
                            gp.wait_ge(spe, M[key])
                        gp.dma_start(
                            out=w13[:, slot * 4096:(slot + 1) * 4096].rearrange("p (c f) -> p c f", f=256),
                            in_=w13ap[:, hc * 256:(hc + 1) * 256].rearrange("(c p) f -> p c f", p=128),
                        ).then_inc(sgp, 16)
                        cnt["gp"] += 16
                        M[("w13", tag, sc, hc)] = cnt["gp"]

                    @block.tensor
                    def _(te, tag=tag, sc=sc, hc=hc):
                        slot = hc % 2
                        te.wait_ge(sgp, M[("w13", tag, sc, hc)])
                        te.wait_ge(sdv, M[(tag, "xdf", sc)])
                        for hf in range(2):
                            if ("hmul",) in M:
                                te.wait_ge(sdv, M[("hmul",)])
                                te.wait_ge(sac, M[("hsil",)])
                            last = None
                            for c16 in range(16):
                                last = te.matmul(
                                    out=h1p[:],
                                    lhsT=w13[:, slot * 4096 + c16 * 256: slot * 4096 + c16 * 256 + 128],
                                    rhs=xdf[:, c16 * SCW + hf * 512: c16 * SCW + (hf + 1) * 512],
                                    start=(c16 == 0), stop=(c16 == 15))
                            last.then_inc(spe, 1)
                            cnt["pe"] += 1
                            M[("h1mm", tag, sc, hc, hf)] = cnt["pe"]
                            for c16 in range(16):
                                last = te.matmul(
                                    out=h3p[:],
                                    lhsT=w13[:, slot * 4096 + c16 * 256 + 128: slot * 4096 + (c16 + 1) * 256],
                                    rhs=xdf[:, c16 * SCW + hf * 512: c16 * SCW + (hf + 1) * 512],
                                    start=(c16 == 0), stop=(c16 == 15))
                            last.then_inc(spe, 1)
                            cnt["pe"] += 1
                            M[("h3mm", tag, sc, hc, hf)] = cnt["pe"]
                        M[("w13cons", slot)] = cnt["pe"]

                    for hf in range(2):
                        @block.scalar
                        def _(ac, tag=tag, sc=sc, hc=hc, hf=hf):
                            ac.wait_ge(spe, M[("h1mm", tag, sc, hc, hf)])
                            ac.activation(out=hs[:], in_=h1p[:], func=AF.Silu).then_inc(sac, 1)
                            cnt["ac"] += 1
                            M[("hsil",)] = cnt["ac"]
                            M[("hsilk", tag, sc, hc, hf)] = cnt["ac"]

                        @block.vector
                        def _(dv, tag=tag, sc=sc, hc=hc, hf=hf):
                            dv.wait_ge(sac, M[("hsilk", tag, sc, hc, hf)])
                            dv.wait_ge(spe, M[("h3mm", tag, sc, hc, hf)])
                            dv.tensor_mul(out=hfm[:, hc * SCW + hf * 512: hc * SCW + (hf + 1) * 512],
                                          in0=hs[:], in1=h3p[:]).then_inc(sdv, 1)
                            cnt["dv"] += 1
                            M[("hmul",)] = cnt["dv"]
                M[(tag, "hdone", sc)] = M[("hmul",)]
                M[(tag, "xdffree", sc)] = M[("hmul",)]

                for oc2 in range(NOC):
                    @block.gpsimd
                    def _(gp, tag=tag, sc=sc, oc2=oc2):
                        slot = oc2 % 2
                        key = ("w2cons", slot)
                        if key in M:
                            gp.wait_ge(spe, M[key])
                        gp.dma_start(
                            out=w2s[:, slot * NHC * 256:(slot + 1) * NHC * 256].rearrange("p (c f) -> p c f", f=256),
                            in_=w2ap[:, oc2 * 256:(oc2 + 1) * 256].rearrange("(c p) f -> p c f", p=128),
                        ).then_inc(sgp, 16)
                        cnt["gp"] += 16
                        M[("w2", tag, sc, oc2)] = cnt["gp"]

                    dv1 = cnt["dv"]

                    @block.tensor
                    def _(te, tag=tag, sc=sc, oc2=oc2, dv1=dv1):
                        slot = oc2 % 2
                        te.wait_ge(sgp, M[("w2", tag, sc, oc2)])
                        te.wait_ge(sdv, M[(tag, "hdone", sc)])
                        for st2 in range(RT):
                            if st2 >= 2:
                                te.wait_ge(sdv, dv1 + st2 - 1)
                            elif ("oevprev",) in M:
                                te.wait_ge(sdv, M[("oevprev",)])
                            last = None
                            for hc in range(NHC):
                                last = te.matmul(
                                    out=oep[st2 % 2][:],
                                    lhsT=hfm[:, hc * SCW + st2 * 128: hc * SCW + (st2 + 1) * 128],
                                    rhs=w2s[:, slot * NHC * 256 + hc * 256: slot * NHC * 256 + (hc + 1) * 256],
                                    start=(hc == 0), stop=(hc == NHC - 1))
                            last.then_inc(spe, 1)
                            cnt["pe"] += 1
                            M[("omm", tag, sc, oc2, st2)] = cnt["pe"]
                        M[("w2cons", slot)] = cnt["pe"]

                    for st2 in range(RT):
                        if tag == "s":
                            @block.gpsimd
                            def _(gp, sc=sc, oc2=oc2, st2=st2):
                                if ("x1cons", sc, oc2, st2 - 2) in M:
                                    gp.wait_ge(sdv, M[("x1cons", sc, oc2, st2 - 2)])
                                elif st2 < 2 and ("x1cons", sc, oc2 - 1, RT - 2 + st2) in M:
                                    gp.wait_ge(sdv, M[("x1cons", sc, oc2 - 1, RT - 2 + st2)])
                                gp.dma_start(out=x1r[:, (st2 % 2) * D:(st2 % 2 + 1) * D],
                                             in_=X1T[sc * SCW + st2 * 128: sc * SCW + (st2 + 1) * 128, :]
                                             ).then_inc(sgp, 16)
                                cnt["gp"] += 16
                                M[("x1r", sc, oc2, st2)] = cnt["gp"]

                        @block.vector
                        def _(dv, tag=tag, sc=sc, oc2=oc2, st2=st2):
                            dv.wait_ge(spe, M[("omm", tag, sc, oc2, st2)])
                            if tag == "e":
                                if ("e", "scat", sc - 1) in M and oc2 == 0 and st2 == 0:
                                    dv.wait_ge(sgp, M[("e", "scat", sc - 1)])
                                dv.tensor_copy(
                                    out=oesb[:, st2 * D + oc2 * 256: st2 * D + (oc2 + 1) * 256],
                                    in_=oep[st2 % 2][:]).then_inc(sdv, 1)
                            else:
                                dv.wait_ge(sgp, M[("x1r", sc, oc2, st2)])
                                if ("s2st", sc, oc2 - 1) in M and st2 == 0:
                                    dv.wait_ge(sst, M[("s2st", sc, oc2 - 1)])
                                dv.scalar_tensor_tensor(
                                    out=s2oc[:, st2 * 256:(st2 + 1) * 256],
                                    in0=oep[st2 % 2][:], scalar=1.0,
                                    in1=x1r[:, (st2 % 2) * D + oc2 * 256: (st2 % 2) * D + (oc2 + 1) * 256],
                                    op0=ALU.mult, op1=ALU.add).then_inc(sdv, 1)
                            cnt["dv"] += 1
                            M[("oev", tag, sc, oc2, st2)] = cnt["dv"]
                            if tag == "s":
                                M[("x1cons", sc, oc2, st2)] = cnt["dv"]
                            M[("oevprev",)] = cnt["dv"]

                    if tag == "s":
                        @block.sync
                        def _(sy, sc=sc, oc2=oc2):
                            sy.wait_ge(sdv, M[("oev", "s", sc, oc2, RT - 1)])
                            sy.dma_start(
                                out=S2P[sc * SCW:(sc + 1) * SCW, oc2 * 256:(oc2 + 1) * 256]
                                    .rearrange("(s p) f -> p s f", p=128),
                                in_=s2oc[:].rearrange("p (s f) -> p s f", f=256),
                            ).then_inc(sst, 16)
                            cnt["st"] += 16
                            M[("s2st", sc, oc2)] = cnt["st"]

                if tag == "e":
                    @block.gpsimd
                    def _(gp, sc=sc):
                        for st2 in range(RT):
                            gp.wait_ge(sdv, M[("oev", "e", sc, NOC - 1, st2)])
                            gp.indirect_dma_start(
                                out=RETP,
                                out_offset=IndirectOffsetOnAxis(
                                    retidx[:, sc * RT + st2: sc * RT + st2 + 1], 0),
                                in_=oesb[:, st2 * D:(st2 + 1) * D], in_offset=None,
                            ).then_inc(sgp, 16)
                            cnt["gp"] += 16
                            gp.wait_ge(sgp, cnt["gp"])
                        M[("e", "scat", sc)] = cnt["gp"]

        quadrant("e", W13E, W2E, None)
        quadrant("s", SW13, SW2, X1F)

        @block.sync
        def _(sy):
            sy.wait_ge(sst, cnt["st"])
            sy.wait_ge(sgp, M[("e", "scat", NSC - 1)])

    es.close()
    return nc


def build_l4():
    nc = bass.Bass("TRN2", target_bir_lowering=False)
    RETR = nc.dram_tensor("RETR", [E * CLOC, D], BF16, kind="ExternalInput").ap()
    EIDX = nc.dram_tensor("EIDX", [128, NTT], F32, kind="ExternalInput").ap()
    LPOS = nc.dram_tensor("LPOS", [128, NTT], F32, kind="ExternalInput").ap()
    GATE = nc.dram_tensor("GATE", [128, NTT], F32, kind="ExternalInput").ap()
    CNTA = nc.dram_tensor("CNTA", [NC, E], F32, kind="ExternalInput").ap()
    S2P = nc.dram_tensor("S2P", [TPC, D], F32, kind="ExternalInput").ap()
    SELC = nc.dram_tensor("SELC", [8, 1], F32, kind="ExternalInput").ap()
    SLT = nc.dram_tensor("SLT", [8, 8], F32, kind="ExternalInput").ap()
    IOTA8 = nc.dram_tensor("IOTA8", [128, E], F32, kind="ExternalInput").ap()
    TOKI = nc.dram_tensor("TOKI", [128, NTT], I32, kind="ExternalInput").ap()
    ONESR = nc.dram_tensor("ONESR", [1, 128], F32, kind="ExternalInput").ap()

    OUT = nc.dram_tensor("OUT", [TPC, D], F32, kind="ExternalOutput").ap()

    RTD = nc.dram_tensor("RTD", [TPC + 1, D], BF16, kind="Internal").ap()
    TKM = nc.dram_tensor("TKM", [E * CLOC + 1, 1], I32, kind="Internal").ap()

    Tn = nc.alloc_sbuf_tensor
    P = nc.alloc_psum_tensor
    eidx = Tn("eidx", [128, NTT], F32)
    lpos = Tn("lpos", [128, NTT], F32)
    gate = Tn("gate", [128, NTT], F32)
    cnta = Tn("cnta", [8, E], F32)
    selc = Tn("selc", [8, 1], F32)
    slt = Tn("slt", [8, 8], F32)
    iota8 = Tn("iota8", [128, E], F32)
    toki = Tn("toki", [128, NTT], I32)
    onesr = Tn("onesr", [1, 128], F32)
    gbme = Tn("gbme", [128, E], F32)
    oh8 = Tn("oh8", [128, E], F32)
    t1 = Tn("t1", [128, 1], F32)
    t2 = Tn("t2", [128, 1], F32)
    gk16 = Tn("gk16", [128, NTT], F32)
    pidx4 = Tn("pidx4", [128, NTT], I32)
    zti = Tn("zti", [128, 32], I32)
    ztile = Tn("ztile", [128, D], BF16)
    tmsb = Tn("tmsb", [128, 1], I32)
    rb = Tn("rb", [128, D], BF16)
    rgf = Tn("rgf", [128, D], F32)
    s2f = Tn("s2f", [128, D], F32)
    s2pr = Tn("s2pr", [128, D], F32)
    sqscr = Tn("sqscr", [128, D], F32)
    stats = Tn("stats", [128, 8], F32)
    outf = Tn("outf", [128, D], F32)

    pg4 = P("pg4", [128, 512], F32)
    pgb = pg4[0:8, 0:8]
    pgr = pg4[0:1, 8:16]
    pgm = pg4[:, 16:24]

    es = contextlib.ExitStack()
    sgp = es.enter_context(nc.semaphore("sgp"))
    sld = es.enter_context(nc.semaphore("sld"))
    spe = es.enter_context(nc.semaphore("spe"))
    sdv = es.enter_context(nc.semaphore("sdv"))
    sac = es.enter_context(nc.semaphore("sac"))
    sst = es.enter_context(nc.semaphore("sst"))

    cnt = {"gp": 0, "ld": 0, "pe": 0, "dv": 0, "ac": 0, "st": 0}
    M = {}

    def bar(dv, inst):
        inst.then_inc(sdv, 1)
        cnt["dv"] += 1
        dv.wait_ge(sdv, cnt["dv"])

    with nc.Block() as block:
        @block.sync
        def _(sy):
            for ap_s, ap_d in ((EIDX, eidx), (LPOS, lpos), (GATE, gate), (CNTA, cnta),
                               (SELC, selc), (SLT, slt), (IOTA8, iota8), (TOKI, toki),
                               (ONESR, onesr)):
                sy.dma_start(out=ap_d[:], in_=ap_s).then_inc(sld, 16)
                cnt["ld"] += 16
            M["consts"] = cnt["ld"]

        @block.tensor
        def _(te):
            te.wait_ge(sld, M["consts"])
            te.matmul(out=pgb, lhsT=slt[:], rhs=cnta[:], start=True, stop=True)
            te.matmul(out=pgr, lhsT=selc[:], rhs=pgb if False else cnta[:],
                      start=True, stop=True).then_inc(spe, 1)
            cnt["pe"] += 1
            M["g1"] = cnt["pe"]

        @block.vector
        def _(dv):
            dv.wait_ge(spe, M["g1"])
            bar(dv, dv.tensor_copy(out=gbme[0:8, :], in_=pgb))
            M["gbev"] = cnt["dv"]

        @block.tensor
        def _(te):
            te.wait_ge(sdv, M["gbev"])
            te.matmul(out=pgr, lhsT=selc[:], rhs=gbme[0:8, :], start=True, stop=True).then_inc(spe, 1)
            cnt["pe"] += 1
            M["g2"] = cnt["pe"]

        @block.vector
        def _(dv):
            dv.wait_ge(spe, M["g2"])
            bar(dv, dv.tensor_copy(out=gbme[0:1, :], in_=pgr))
            M["grow"] = cnt["dv"]

        @block.tensor
        def _(te):
            te.wait_ge(sdv, M["grow"])
            te.matmul(out=pgm, lhsT=onesr[:], rhs=gbme[0:1, :], start=True, stop=True).then_inc(spe, 1)
            cnt["pe"] += 1
            M["g3"] = cnt["pe"]

        @block.vector
        def _(dv):
            dv.wait_ge(spe, M["g3"])
            bar(dv, dv.tensor_copy(out=gbme[:], in_=pgm))
            for tt in range(NTT):
                bar(dv, dv.tensor_scalar(out=oh8[:], in0=iota8[:], scalar1=eidx[:, tt:tt + 1],
                                         scalar2=None, op0=ALU.is_equal))
                bar(dv, dv.tensor_mul(out=oh8[:], in0=oh8[:], in1=gbme[:]))
                bar(dv, dv.reduce_sum(out=t1[:], in_=oh8[:], axis=AX.X))
                bar(dv, dv.tensor_add(out=t1[:], in0=t1[:], in1=lpos[:, tt:tt + 1]))
                bar(dv, dv.tensor_single_scalar(out=t1[:], in_=t1[:], scalar=float(CAP), op=ALU.is_lt))
                bar(dv, dv.tensor_mul(out=gk16[:, tt:tt + 1] if False else t2[:],
                                      in0=t1[:], in1=gate[:, tt:tt + 1]))
                bar(dv, dv.tensor_copy(out=gk16[:, tt:tt + 1], in_=t2[:]))
                bar(dv, dv.tensor_single_scalar(out=t2[:], in_=lpos[:, tt:tt + 1],
                                                scalar=float(CLOC), op=ALU.is_lt))
                bar(dv, dv.scalar_tensor_tensor(out=t1[:], in0=eidx[:, tt:tt + 1],
                                                scalar=float(CLOC), in1=lpos[:, tt:tt + 1],
                                                op0=ALU.mult, op1=ALU.add))
                bar(dv, dv.tensor_scalar_sub(out=t1[:], in0=t1[:], scalar1=float(E * CLOC)))
                bar(dv, dv.tensor_mul(out=t1[:], in0=t1[:], in1=t2[:]))
                bar(dv, dv.tensor_scalar_add(out=t1[:], in0=t1[:], scalar1=float(E * CLOC)))
                bar(dv, dv.tensor_copy(out=pidx4[:, tt:tt + 1], in_=t1[:]))
            dv.memset(ztile[:], 0.0)
            bar(dv, dv.memset(zti[:], CAP))
            M["routmath"] = cnt["dv"]

        @block.gpsimd
        def _(gp):
            gp.wait_ge(sdv, M["routmath"])
            for b in range(16):
                gp.dma_start(out=RTD[b * 128:(b + 1) * 128, :], in_=ztile[:]).then_inc(sgp, 16)
                cnt["gp"] += 16
            gp.dma_start(out=RTD[TPC:TPC + 1, :], in_=ztile[0:1, :]).then_inc(sgp, 16)
            cnt["gp"] += 16
            gp.dma_start(out=TKM[0:E * CLOC, 0:1].rearrange("(p a) c -> p a c", a=32),
                         in_=zti[:].rearrange("p (a c) -> p a c", c=1)).then_inc(sgp, 16)
            cnt["gp"] += 16
            gp.dma_start(out=TKM[E * CLOC:E * CLOC + 1, 0:1], in_=zti[0:1, 0:1]).then_inc(sgp, 16)
            cnt["gp"] += 16
            gp.wait_ge(sgp, cnt["gp"])
            for tt in range(NTT):
                gp.indirect_dma_start(
                    out=TKM, out_offset=IndirectOffsetOnAxis(pidx4[:, tt:tt + 1], 0),
                    in_=toki[:, tt:tt + 1], in_offset=None,
                ).then_inc(sgp, 16)
                cnt["gp"] += 16
            gp.wait_ge(sgp, cnt["gp"])
            for rt2 in range(32):
                gp.dma_start(out=tmsb[:], in_=TKM[rt2 * 128:(rt2 + 1) * 128, 0:1]).then_inc(sgp, 16)
                gp.dma_start(out=rb[:], in_=RETR[rt2 * 128:(rt2 + 1) * 128, :]).then_inc(sgp, 16)
                cnt["gp"] += 32
                gp.wait_ge(sgp, cnt["gp"])
                gp.indirect_dma_start(
                    out=RTD, out_offset=IndirectOffsetOnAxis(tmsb[:, 0:1], 0),
                    in_=rb[:], in_offset=None,
                ).then_inc(sgp, 16)
                cnt["gp"] += 16
                gp.wait_ge(sgp, cnt["gp"])
            M["rscat"] = cnt["gp"]

        for tt in range(NTT):
            @block.sync
            def _(sy, tt=tt):
                sy.wait_ge(sgp, M["rscat"])
                if tt > 0:
                    sy.wait_ge(sdv, M[("tiledone", tt - 1)])
                sy.dma_start(out=rb[:], in_=RTD[tt * 128:(tt + 1) * 128, :]).then_inc(sld, 16)
                cnt["ld"] += 16
                sy.dma_start(out=s2pr[:], in_=S2P[tt * 128:(tt + 1) * 128, :]).then_inc(sld, 16)
                cnt["ld"] += 16
                M[("tl", tt)] = cnt["ld"]

            ac0 = cnt["ac"]
            dv0 = cnt["dv"]

            @block.scalar
            def _(ac, tt=tt, dv0=dv0):
                ac.wait_ge(sld, M[("tl", tt)])
                if tt > 0:
                    ac.wait_ge(sdv, M[("s2f", tt - 1)])
                ac.activation(out=rgf[:], in_=rb[:], func=AF.Copy,
                              scale=gk16[:, tt:tt + 1]).then_inc(sac, 1)
                cnt["ac"] += 1
                M[("rg", tt)] = cnt["ac"]
                ac.wait_ge(sdv, dv0 + 2)
                ac.activation(out=sqscr[:], in_=s2f[:], func=AF.Square,
                              accum_out=stats[:, 1:2]).then_inc(sac, 1)
                cnt["ac"] += 1
                M[("sq", tt)] = cnt["ac"]

            @block.vector
            def _(dv, tt=tt):
                dv.wait_ge(sac, M[("rg", tt)])
                dv.tensor_add(out=s2f[:], in0=s2pr[:], in1=rgf[:]).then_inc(sdv, 1)
                cnt["dv"] += 1
                M[("s2f", tt)] = cnt["dv"]
                dv.reduce_sum(out=stats[:, 0:1], in_=s2f[:], axis=AX.X).then_inc(sdv, 1)
                cnt["dv"] += 1
                dv.wait_ge(sac, M[("sq", tt)])
                dv.tensor_scalar_mul(out=stats[:, 0:1], in0=stats[:, 0:1], scalar1=1.0 / D)
                bar(dv, dv.tensor_scalar_mul(out=stats[:, 1:2], in0=stats[:, 1:2], scalar1=1.0 / D))
                bar(dv, dv.tensor_mul(out=stats[:, 6:7], in0=stats[:, 0:1], in1=stats[:, 0:1]))
                bar(dv, dv.tensor_sub(out=stats[:, 2:3], in0=stats[:, 1:2], in1=stats[:, 6:7]))
                dv.tensor_scalar_add(out=stats[:, 2:3], in0=stats[:, 2:3], scalar1=1e-5).then_inc(sdv, 1)
                cnt["dv"] += 1
                M[("var", tt)] = cnt["dv"]
                dv.wait_ge(sac, M[("sq", tt)] + 1)
                bar(dv, dv.tensor_copy(out=stats[:, 4:5], in_=stats[:, 3:4]))
                M[("lnv2", tt)] = cnt["dv"]
                dv.wait_ge(sac, M[("sq", tt)] + 2)
                dv.wait_ge(sdv, cnt["dv"])
                if tt > 0:
                    dv.wait_ge(sst, M[("ost", tt - 1)])
                dv.tensor_scalar(out=outf[:], in0=s2f[:], scalar1=stats[:, 0:1],
                                 scalar2=stats[:, 5:6], op0=ALU.subtract,
                                 op1=ALU.mult).then_inc(sdv, 1)
                cnt["dv"] += 1
                M[("outf", tt)] = cnt["dv"]

            @block.scalar
            def _(ac, tt=tt):
                ac.wait_ge(sdv, M[("var", tt)])
                ac.activation(out=stats[:, 3:4], in_=stats[:, 2:3], func=AF.Ln).then_inc(sac, 1)
                cnt["ac"] += 1
                ac.wait_ge(sdv, M[("lnv2", tt)])
                ac.activation(out=stats[:, 5:6], in_=stats[:, 4:5], func=AF.Exp,
                              scale=-0.5).then_inc(sac, 1)
                cnt["ac"] += 1

            @block.sync
            def _(sy, tt=tt):
                sy.wait_ge(sdv, M[("outf", tt)])
                sy.dma_start(out=OUT[tt * 128:(tt + 1) * 128, :], in_=outf[:]).then_inc(sst, 16)
                cnt["st"] += 16
                M[("ost", tt)] = cnt["st"]
                M[("tiledone", tt)] = cnt["dv"]

        @block.sync
        def _(sy):
            sy.wait_ge(sst, cnt["st"])

    es.close()
    return nc


_PROGS = {}


def _get(name, builder):
    if name not in _PROGS:
        _PROGS[name] = builder()
    return _PROGS[name]


def _run_l1(xs, WqkvT, cosT, sinT):
    nc1 = _get("l1", build_l1)
    psgn = np.asarray(_p_signed())
    idn = np.eye(128, dtype=ml_dtypes.bfloat16)
    in_maps = []
    for c in range(NC):
        sl = slice(c * TPC, (c + 1) * TPC)
        in_maps.append(dict(
            X=np.ascontiguousarray(xs[sl]),
            WQKVT=WqkvT,
            COSF=np.ascontiguousarray(cosT[:, sl]),
            SINF=np.ascontiguousarray(sinT[:, sl]),
            PSGN=psgn, IDN=idn,
        ))
    res = run_bass_kernel_spmd(nc1, in_maps, core_ids=list(range(NC)))
    return res.results




def _consts_l2():
    mask = np.zeros((128, 256), ml_dtypes.bfloat16)
    for m in range(128):
        mask[m, m:] = 1.0
    ut = np.zeros((128, 128), np.float32)
    for k in range(128):
        ut[k, k:] = 1.0
    return dict(
        MASK=mask,
        IDN=np.eye(128, dtype=ml_dtypes.bfloat16),
        IDNF=np.eye(128, dtype=np.float32),
        UT=ut,
        ONESR=np.ones((1, 128), np.float32),
        IOTA8=np.tile(np.arange(E, dtype=np.float32), (128, 1)),
    )


def _run_l2(l1res, xs, WoutT, Wg):
    nc2 = _get("l2", build_l2)
    consts = _consts_l2()
    szall = np.stack([np.asarray(l1res[c]["SZT"]) for c in range(NC)])
    in_maps = []
    for c in range(NC):
        sl = slice(c * TPC, (c + 1) * TPC)
        cpref = np.zeros((128, NC), np.float32)
        cpref[:, :c] = 1.0
        in_maps.append(dict(
            QF=np.asarray(l1res[c]["QF"]), KF=np.asarray(l1res[c]["KF"]),
            KT=np.asarray(l1res[c]["KT"]), VT=np.asarray(l1res[c]["VT"]),
            SZALL=szall, X=np.ascontiguousarray(xs[sl]),
            WOUTT=WoutT, WG=Wg, CPREFB=cpref, **consts,
        ))
    res = run_bass_kernel_spmd(nc2, in_maps, core_ids=list(range(NC)))
    return res.results




def _consts_l3():
    ejc = np.zeros((8, NC * 128), np.float32)
    for j in range(NC):
        ejc[j, j * 128:(j + 1) * 128] = 1.0
    slt = np.zeros((8, 8), np.float32)
    for k in range(8):
        slt[k, k + 1:] = 1.0
    lpc = (np.arange(128)[:, None] + 128 * np.arange(4)[None, :]).astype(np.float32)
    slc = (np.arange(128)[:, None] + 128 * np.arange(16)[None, :]).astype(np.float32)
    return dict(EJC=ejc, SLT=slt, LPC=lpc, SLC=slc,
                IDN=np.eye(128, dtype=ml_dtypes.bfloat16))


def _interleave13(w1, w3):
    a = w1.reshape(D, HF // 128, 128)
    b = w3.reshape(D, HF // 128, 128)
    return np.ascontiguousarray(np.concatenate([a, b], axis=2).reshape(D, 2 * HF))


def _run_l3(l2res, W1, W3, W2, sw1, sw3, sw2):
    nc3 = _get("l3", build_l3)
    consts = _consts_l3()
    cnta = np.concatenate([np.asarray(l2res[c]["CNT"]) for c in range(NC)], 0)
    pbufs = [np.asarray(l2res[c]["PBUF"]) for c in range(NC)]
    sw13 = _interleave13(sw1, sw3)
    in_maps = []
    for i in range(NC):
        runs = np.concatenate([pbufs[j][i * CLOC:(i + 1) * CLOC] for j in range(NC)], 0)
        selb = np.zeros((8, E), np.float32)
        selb[:, i] = 1.0
        in_maps.append(dict(
            RUNS=np.ascontiguousarray(runs), CNTA=cnta,
            X1F=np.asarray(l2res[i]["X1F"]), X1T=np.asarray(l2res[i]["X1T"]),
            W13E=_interleave13(W1[i], W3[i]),
            W2E=np.ascontiguousarray(W2[i]),
            SW13=sw13, SW2=np.ascontiguousarray(sw2), SELB=selb, **consts,
        ))
    res = run_bass_kernel_spmd(nc3, in_maps, core_ids=list(range(NC)))
    return res.results




def _run_l4(l2res, l3res):
    nc4 = _get("l4", build_l4)
    cnta = np.concatenate([np.asarray(l2res[c]["CNT"]) for c in range(NC)], 0)
    retps = [np.asarray(l3res[i]["RETP"]) for i in range(NC)]
    slt = np.zeros((8, 8), np.float32)
    for k in range(8):
        slt[k, k + 1:] = 1.0
    toki = (np.arange(128)[:, None] + 128 * np.arange(NTT)[None, :]).astype(np.int32)
    iota8 = np.tile(np.arange(E, dtype=np.float32), (128, 1))
    onesr = np.ones((1, 128), np.float32)
    in_maps = []
    for j in range(NC):
        retr = np.concatenate([retps[i][j * CLOC:(j + 1) * CLOC] for i in range(NC)], 0)
        selc = np.zeros((8, 1), np.float32)
        selc[j, 0] = 1.0
        in_maps.append(dict(
            RETR=np.ascontiguousarray(retr),
            EIDX=np.asarray(l2res[j]["EIDX"]), LPOS=np.asarray(l2res[j]["LPOS"]),
            GATE=np.asarray(l2res[j]["GATE"]), CNTA=cnta,
            S2P=np.asarray(l3res[j]["S2P"]),
            SELC=selc, SLT=slt, IOTA8=iota8, TOKI=toki, ONESR=onesr,
        ))
    res = run_bass_kernel_spmd(nc4, in_maps, core_ids=list(range(NC)))
    return res.results


def kernel(L0_attn_x, L0_attn_W_qkv, L0_attn_W_out, L0_ffn_Wg, L0_ffn_W1,
           L0_ffn_W3, L0_ffn_W2, L0_ffn_sw1, L0_ffn_sw3, L0_ffn_sw2):
    xs = np.ascontiguousarray(np.asarray(L0_attn_x, np.float32).reshape(T, D))
    WqkvT = np.ascontiguousarray(np.asarray(L0_attn_W_qkv, np.float32).T)
    WoutT = np.ascontiguousarray(np.asarray(L0_attn_W_out, np.float32).T)
    Wg = np.ascontiguousarray(np.asarray(L0_ffn_Wg, np.float32))
    cos, sin = _rope_tables()
    r1 = _run_l1(xs, WqkvT, np.ascontiguousarray(cos.T), np.ascontiguousarray(sin.T))
    r2 = _run_l2(r1, xs, WoutT, Wg)
    r3 = _run_l3(r2, np.asarray(L0_ffn_W1, np.float32), np.asarray(L0_ffn_W3, np.float32),
                 np.asarray(L0_ffn_W2, np.float32), np.ascontiguousarray(np.asarray(L0_ffn_sw1, np.float32)),
                 np.ascontiguousarray(np.asarray(L0_ffn_sw3, np.float32)),
                 np.ascontiguousarray(np.asarray(L0_ffn_sw2, np.float32)))
    r4 = _run_l4(r2, r3)
    out = np.concatenate([np.asarray(r4[c]["OUT"], np.float32) for c in range(NC)], 0)
    return out.reshape(1, T, D)


# version 03
# speedup vs baseline: 1.4296x; 1.0569x over previous
import numpy as np
import ml_dtypes
import contextlib
import concourse.bass as bass
import concourse.mybir as mybir
from concourse.bass_utils import run_bass_kernel_spmd
from concourse.bass_core_rust import IndirectOffsetOnAxis

F32 = mybir.dt.float32
BF16 = mybir.dt.bfloat16
I32 = mybir.dt.int32
AF = mybir.ActivationFunctionType
ALU = mybir.AluOpType
AX = mybir.AxisListType

NC = 8
T = 16384
D = 2048
NH = 16
HD = 128
CH = 256
E = 8
HF = 2816
CLOC = 512
TPC = T // NC
CAP = 2048
NTT = TPC // 128
NCH = TPC // CH


def _rope_tables():
    inv = 1.0 / (10000.0 ** (np.arange(0, HD, 2, dtype=np.float32) / HD))
    pos = np.arange(T, dtype=np.float32)
    emb = np.concatenate([np.outer(pos, inv)] * 2, -1)
    return np.cos(emb).astype(np.float32), np.sin(emb).astype(np.float32)


def _p_signed():
    P = np.zeros((HD, HD), np.float32)
    for i in range(HD // 2):
        P[2 * i + 1, 2 * i] = -1.0
        P[2 * i, 2 * i + 1] = 1.0
    return P.astype(ml_dtypes.bfloat16)


def build_l1():
    nc = bass.Bass("TRN2", target_bir_lowering=False, detect_race_conditions=False)
    X = nc.dram_tensor("X", [TPC, D], F32, kind="ExternalInput").ap()
    WQKVT = nc.dram_tensor("WQKVT", [D, 3 * D], F32, kind="ExternalInput").ap()
    COSF = nc.dram_tensor("COSF", [HD, TPC], F32, kind="ExternalInput").ap()
    SINF = nc.dram_tensor("SINF", [HD, TPC], F32, kind="ExternalInput").ap()
    PSGN = nc.dram_tensor("PSGN", [HD, HD], BF16, kind="ExternalInput").ap()
    IDN = nc.dram_tensor("IDN", [128, 128], BF16, kind="ExternalInput").ap()

    QF = nc.dram_tensor("QF", [D, TPC], BF16, kind="ExternalOutput").ap()
    KF = nc.dram_tensor("KF", [D, TPC], BF16, kind="ExternalOutput").ap()
    KT = nc.dram_tensor("KT", [TPC, D], BF16, kind="ExternalOutput").ap()
    VT = nc.dram_tensor("VT", [TPC, D], BF16, kind="ExternalOutput").ap()
    SZT = nc.dram_tensor("SZT", [NH, HD, HD + 1], F32, kind="ExternalOutput").ap()

    X0BF = nc.dram_tensor("X0BF", [TPC, D], BF16, kind="Internal").ap()

    Tn = nc.alloc_sbuf_tensor
    P = nc.alloc_psum_tensor
    x0fm = Tn("x0fm", [128, 16 * TPC], BF16)
    cosf = Tn("cosf", [128, TPC], F32)
    sinf = Tn("sinf", [128, TPC], F32)
    psgn = Tn("psgn", [128, 128], BF16)
    idn = Tn("idn", [128, 128], BF16)
    wqk = Tn("wqk", [128, 2 * 2048], BF16)
    wv = Tn("wv", [128, 2 * 8192], BF16)
    af32 = Tn("af32", [128, 512], F32)
    abf = Tn("abf", [128, 512], BF16)
    t1 = Tn("t1", [128, 512], F32)
    t2 = Tn("t2", [128, 512], F32)
    escr = Tn("escr", [128, 512], F32)
    qkout = Tn("qkout", [128, TPC], BF16)
    ktok = Tn("ktok", [128, 16 * 128], BF16)
    vtok = Tn("vtok", [128, 16 * 512], BF16)
    szev = Tn("szev", [128, HD + 1], F32)
    ones1 = Tn("ones1", [128, 1], BF16)

    pq = P("pq", [128, 512], F32)
    psw = P("psw", [128, 512], F32)
    pv = [P("pv0", [128, 512], F32), P("pv1", [128, 512], F32)]
    ptp = [P("ptp0", [128, 128], BF16), P("ptp1", [128, 128], BF16)]
    psz = [P("psz0", [128, HD + 1], F32), P("psz1", [128, HD + 1], F32)]

    es = contextlib.ExitStack()
    sgp = es.enter_context(nc.semaphore("sgp"))
    sld = es.enter_context(nc.semaphore("sld"))
    spe = es.enter_context(nc.semaphore("spe"))
    sdv = es.enter_context(nc.semaphore("sdv"))
    sac = es.enter_context(nc.semaphore("sac"))
    sst = es.enter_context(nc.semaphore("sst"))

    cnt = {"gp": 0, "ld": 0, "pe": 0, "dv": 0, "ac": 0, "st": 0}
    M = {}

    with nc.Block() as block:
        @block.gpsimd
        def _(gp):
            gp.dma_start(out=X0BF, in_=X).then_inc(sgp, 16)
            cnt["gp"] += 16
            M["x0bf"] = cnt["gp"]

        def load_weights(kind, idx):
            @block.gpsimd
            def _(gp, kind=kind, idx=idx):
                slot = 0 if kind != "k" else 1
                key = ("wfree", kind, idx)
                if key in M:
                    gp.wait_ge(spe, M[key])
                if kind == "v":
                    s = WQKVT[:, 2 * D + idx * 512: 2 * D + (idx + 1) * 512]
                    s = s.rearrange("(c p) f -> p c f", p=128)
                    d_ = wv[:, (idx % 2) * 8192:(idx % 2 + 1) * 8192]
                    d_ = d_.rearrange("p (c f) -> p c f", f=512)
                else:
                    base = 0 if kind == "q" else D
                    s = WQKVT[:, base + idx * 128: base + (idx + 1) * 128]
                    s = s.rearrange("(c p) f -> p c f", p=128)
                    d_ = wqk[:, slot * 2048:(slot + 1) * 2048]
                    d_ = d_.rearrange("p (c f) -> p c f", f=128)
                gp.dma_start(out=d_, in_=s).then_inc(sgp, 16)
                cnt["gp"] += 16
                M[("wld", kind, idx)] = cnt["gp"]

        @block.sync
        def _(sy):
            for ap_s, ap_d in ((COSF, cosf), (SINF, sinf), (PSGN, psgn), (IDN, idn)):
                sy.dma_start(out=ap_d[:], in_=ap_s).then_inc(sld, 16)
                cnt["ld"] += 16
            sy.wait_ge(sgp, M["x0bf"])
            for c in range(16):
                sy.dma_start_transpose(
                    out=x0fm[:, c * TPC:(c + 1) * TPC],
                    in_=X0BF[:, c * 128:(c + 1) * 128],
                ).then_inc(sld, 16)
                cnt["ld"] += 16
            M["ld_all"] = cnt["ld"]

        @block.vector
        def _(dv):
            dv.memset(ones1[:], 1.0).then_inc(sdv, 1)
            cnt["dv"] += 1

        for g in range(4):
            load_weights("v", g)
            pe0 = cnt["pe"]
            dv0 = cnt["dv"]

            @block.tensor
            def _(te, g=g, pe0=pe0, dv0=dv0):
                te.wait_ge(sgp, M[("wld", "v", g)])
                te.wait_ge(sld, M["ld_all"])
                for tt in range(16):
                    if tt >= 2:
                        te.wait_ge(sdv, dv0 + tt - 1)
                    last = None
                    for c in range(16):
                        last = te.matmul(
                            out=pv[tt % 2][:],
                            lhsT=x0fm[:, c * TPC + tt * 128: c * TPC + (tt + 1) * 128],
                            rhs=wv[:, (g % 2) * 8192 + c * 512:(g % 2) * 8192 + (c + 1) * 512],
                            start=(c == 0), stop=(c == 15),
                        )
                    last.then_inc(spe, 1)
                    cnt["pe"] += 1
                M[("vmm", g)] = cnt["pe"]
                M[("wfree", "v", g + 2)] = cnt["pe"]

            @block.vector
            def _(dv, g=g, pe0=pe0):
                for tt in range(16):
                    dv.wait_ge(spe, pe0 + tt + 1)
                    if g > 0:
                        dv.wait_ge(spe, M[("szdone", g - 1)])
                        dv.wait_ge(sst, M[("vstash", g - 1)])
                    dv.tensor_copy(
                        out=vtok[:, tt * 512:(tt + 1) * 512], in_=pv[tt % 2][:]
                    ).then_inc(sdv, 1)
                    cnt["dv"] += 1
                M[("vev", g)] = cnt["dv"]

            @block.sync
            def _(sy, g=g, dv0=dv0):
                for tt in range(16):
                    sy.wait_ge(sdv, dv0 + tt + 1)
                    sy.dma_start(
                        out=VT[tt * 128:(tt + 1) * 128, g * 512:(g + 1) * 512],
                        in_=vtok[:, tt * 512:(tt + 1) * 512],
                    ).then_inc(sst, 16)
                    cnt["st"] += 16
                M[("vstash", g)] = cnt["st"]

            for hh in range(4):
                h = 4 * g + hh
                for part in ("q", "k"):
                    load_weights(part, h)
                    slot = 0 if part == "q" else 1
                    dst = QF if part == "q" else KF
                    pe0 = cnt["pe"]
                    dv0 = cnt["dv"]
                    ac0 = cnt["ac"]
                    st_prev = cnt["st"]

                    @block.tensor
                    def _(te, h=h, part=part, slot=slot, pe0=pe0, dv0=dv0):
                        te.wait_ge(sgp, M[("wld", part, h)])
                        te.wait_ge(sst, M.get("qkstash", 0))
                        for tc in range(4):
                            if tc > 0:
                                te.wait_ge(sdv, dv0 + 3 * (tc - 1) + 1)
                            last = None
                            for c in range(16):
                                last = te.matmul(
                                    out=pq[:],
                                    lhsT=wqk[:, slot * 2048 + c * 128: slot * 2048 + (c + 1) * 128],
                                    rhs=x0fm[:, c * TPC + tc * 512: c * TPC + (tc + 1) * 512],
                                    start=(c == 0), stop=(c == 15),
                                )
                            last.then_inc(spe, 1)
                            cnt["pe"] += 1
                            te.wait_ge(sdv, dv0 + 3 * tc + 1)
                            if tc > 0:
                                te.wait_ge(sdv, dv0 + 3 * (tc - 1) + 2)
                            te.matmul(out=psw[:], lhsT=psgn[:], rhs=abf[:],
                                      start=True, stop=True).then_inc(spe, 1)
                            cnt["pe"] += 1
                        M[("wfree", part, h + (2 if part == "v" else 1))] = cnt["pe"]

                    @block.vector
                    def _(dv, pe0=pe0, ac0=ac0, st_prev=st_prev):
                        for tc in range(4):
                            dv.wait_ge(spe, pe0 + 2 * tc + 1)
                            if tc == 0:
                                dv.wait_ge(sst, M.get("qkstash", 0))
                            dv.tensor_copy(out=af32[:], in_=pq[:])
                            dv.tensor_copy(out=abf[:], in_=af32[:]).then_inc(sdv, 1)
                            cnt["dv"] += 1
                            dv.wait_ge(spe, pe0 + 2 * tc + 2)
                            sl = slice(tc * 512, (tc + 1) * 512)
                            dv.tensor_mul(out=t1[:], in0=af32[:], in1=cosf[:, sl])
                            dv.tensor_mul(out=t2[:], in0=psw[:], in1=sinf[:, sl])
                            dv.tensor_add(out=t1[:], in0=t1[:], in1=t2[:]).then_inc(sdv, 1)
                            cnt["dv"] += 1
                            dv.wait_ge(sac, ac0 + tc + 1)
                            dv.tensor_scalar_min(out=t2[:], in0=escr[:], scalar1=1.0)
                            dv.scalar_tensor_tensor(
                                out=qkout[:, sl], in0=t1[:], scalar=0.0, in1=t2[:],
                                op0=ALU.max, op1=ALU.add,
                            ).then_inc(sdv, 1)
                            cnt["dv"] += 1
                        M["qkready"] = cnt["dv"]

                    @block.scalar
                    def _(ac, dv0=dv0):
                        for tc in range(4):
                            ac.wait_ge(sdv, dv0 + 3 * tc + 2)
                            ac.activation(out=escr[:], in_=t1[:], func=AF.Exp).then_inc(sac, 1)
                            cnt["ac"] += 1

                    if part == "k":
                        pe1 = cnt["pe"]
                        dv1 = cnt["dv"]

                        @block.tensor
                        def _(te, dv1=dv1):
                            te.wait_ge(sdv, M["qkready"])
                            for tt in range(16):
                                if tt >= 2:
                                    te.wait_ge(sdv, dv1 + tt - 1)
                                te.transpose(
                                    out=ptp[tt % 2][:],
                                    in_=qkout[:, tt * 128:(tt + 1) * 128],
                                    identity=idn[:],
                                ).then_inc(spe, 1)
                                cnt["pe"] += 1

                        @block.vector
                        def _(dv, pe1=pe1, h=h):
                            dv.wait_ge(sst, M.get("ktstash", 0))
                            for tt in range(16):
                                dv.wait_ge(spe, pe1 + tt + 1)
                                dv.tensor_copy(
                                    out=ktok[:, tt * 128:(tt + 1) * 128],
                                    in_=ptp[tt % 2][:],
                                ).then_inc(sdv, 1)
                                cnt["dv"] += 1
                            M["ktokready"] = cnt["dv"]

                    @block.sync
                    def _(sy, h=h, part=part, dst=dst):
                        sy.wait_ge(sdv, M["qkready"])
                        sy.dma_start(out=dst[h * 128:(h + 1) * 128, :],
                                     in_=qkout[:]).then_inc(sst, 16)
                        cnt["st"] += 16
                        M["qkstash"] = cnt["st"]
                        if part == "k":
                            sy.wait_ge(sdv, M["ktokready"])
                            sy.dma_start(
                                out=KT[:, h * 128:(h + 1) * 128].rearrange(
                                    "(tt p) d -> p tt d", p=128),
                                in_=ktok[:].rearrange("p (tt d) -> p tt d", d=128),
                            ).then_inc(sst, 16)
                            cnt["st"] += 16
                            M["ktstash"] = cnt["st"]

                @block.tensor
                def _(te, g=g, hh=hh, h=h):
                    te.wait_ge(sdv, M["ktokready"])
                    te.wait_ge(sdv, M[("vev", g)])
                    te.wait_ge(sdv, 1)
                    b = psz[h % 2]
                    te.wait_ge(sdv, M.get(("szfree", h % 2), 0))
                    last = None
                    for tt in range(16):
                        last = te.matmul(
                            out=b[:, 0:HD],
                            lhsT=ktok[:, tt * 128:(tt + 1) * 128],
                            rhs=vtok[:, tt * 512 + hh * 128: tt * 512 + (hh + 1) * 128],
                            start=(tt == 0), stop=False,
                        )
                    for tt in range(16):
                        last = te.matmul(
                            out=b[:, HD:HD + 1],
                            lhsT=ktok[:, tt * 128:(tt + 1) * 128],
                            rhs=ones1[:],
                            start=False, stop=(tt == 15),
                        )
                    last.then_inc(spe, 1)
                    cnt["pe"] += 1
                    M["szmm"] = cnt["pe"]
                    if hh == 3:
                        M[("szdone", g)] = cnt["pe"]

                @block.vector
                def _(dv, h=h, g=g, hh=hh):
                    dv.wait_ge(spe, M["szmm"])
                    dv.wait_ge(sst, M.get("szstash", 0))
                    dv.tensor_copy(out=szev[:], in_=psz[h % 2][:]).then_inc(sdv, 1)
                    cnt["dv"] += 1
                    M[("szfree", h % 2)] = cnt["dv"]
                    if hh == 3:
                        M[("vconsumed", g)] = cnt["dv"]

                @block.sync
                def _(sy, h=h):
                    sy.wait_ge(sdv, M[("szfree", h % 2)])
                    sy.dma_start(out=SZT[h], in_=szev[:]).then_inc(sst, 16)
                    cnt["st"] += 16
                    M["szstash"] = cnt["st"]

        @block.sync
        def _(sy):
            sy.wait_ge(sst, cnt["st"])

    es.close()
    return nc



def build_l2(debug=False, serial=0, trunc=0, nch=NCH, phase=99):
    nc = bass.Bass("TRN2", target_bir_lowering=False, detect_race_conditions=False)
    QF = nc.dram_tensor("QF", [D, TPC], BF16, kind="ExternalInput").ap()
    KF = nc.dram_tensor("KF", [D, TPC], BF16, kind="ExternalInput").ap()
    KT = nc.dram_tensor("KT", [TPC, D], BF16, kind="ExternalInput").ap()
    VT = nc.dram_tensor("VT", [TPC, D], BF16, kind="ExternalInput").ap()
    SZALL = nc.dram_tensor("SZALL", [NC, NH, HD, HD + 1], F32, kind="ExternalInput").ap()
    X = nc.dram_tensor("X", [TPC, D], F32, kind="ExternalInput").ap()
    WOUTT = nc.dram_tensor("WOUTT", [D, D], F32, kind="ExternalInput").ap()
    WG = nc.dram_tensor("WG", [D, E], F32, kind="ExternalInput").ap()
    MASK = nc.dram_tensor("MASK", [128, 256], BF16, kind="ExternalInput").ap()
    IDN = nc.dram_tensor("IDN", [128, 128], BF16, kind="ExternalInput").ap()
    IDNF = nc.dram_tensor("IDNF", [128, 128], F32, kind="ExternalInput").ap()
    UT = nc.dram_tensor("UT", [128, 128], F32, kind="ExternalInput").ap()
    ONESR = nc.dram_tensor("ONESR", [1, 128], F32, kind="ExternalInput").ap()
    IOTA8 = nc.dram_tensor("IOTA8", [128, E], F32, kind="ExternalInput").ap()
    CPREFB = nc.dram_tensor("CPREFB", [128, NC], F32, kind="ExternalInput").ap()

    X1T = nc.dram_tensor("X1T", [TPC, D], BF16, kind="ExternalOutput").ap()
    X1F = nc.dram_tensor("X1F", [D, TPC], BF16, kind="ExternalOutput").ap()
    PBUF = nc.dram_tensor("PBUF", [E * CLOC + 1, D], BF16, kind="ExternalOutput").ap()
    CNT = nc.dram_tensor("CNT", [1, E], F32, kind="ExternalOutput").ap()
    EIDX = nc.dram_tensor("EIDX", [128, NTT], F32, kind="ExternalOutput").ap()
    LPOS = nc.dram_tensor("LPOS", [128, NTT], F32, kind="ExternalOutput").ap()
    GATE = nc.dram_tensor("GATE", [128, NTT], F32, kind="ExternalOutput").ap()
    if debug:
        DB_SZB = nc.dram_tensor("DB_SZB", [128, NH * 129], BF16, kind="ExternalOutput").ap()
        DB_S1 = nc.dram_tensor("DB_S1", [128, 2 * D], F32, kind="ExternalOutput").ap()
        DB_ATT = nc.dram_tensor("DB_ATT", [128, 384], BF16, kind="ExternalOutput").ap()
        DB_NUM = nc.dram_tensor("DB_NUM", [128, 258], F32, kind="ExternalOutput").ap()
        DB_RD = nc.dram_tensor("DB_RD", [128, 2], F32, kind="ExternalOutput").ap()
        DB_NUMN = nc.dram_tensor("DB_NUMN", [128, 256], BF16, kind="ExternalOutput").ap()
        DB_ZSB = nc.dram_tensor("DB_ZSB", [8, TPC], F32, kind="ExternalOutput").ap()
        DB_NUMF = nc.dram_tensor("DB_NUMF", [128, NH * 256], BF16, kind="ExternalOutput").ap()

    Tn = nc.alloc_sbuf_tensor
    P = nc.alloc_psum_tensor
    qf = Tn("qf", [128, NH * 256], BF16)
    kf = Tn("kf", [128, NH * 256], BF16)
    kt = Tn("kt", [128, 2 * D], BF16)
    vext = [Tn("vext0", [128, NH * 129], BF16), Tn("vext1", [128, NH * 129], BF16)]
    x0 = Tn("x0", [128, 2 * D], F32)
    wout = Tn("wout", [128, 2 * NH * 512], BF16)
    numfm = Tn("numfm", [128, NH * 256], BF16)
    attsb = Tn("attsb", [128, NH * 384], BF16)
    numn = Tn("numn", [128, NH * 256], BF16)
    szf = Tn("szf", [128, NH * 129], F32)
    szb = Tn("szb", [128, NH * 129], BF16)
    szj = Tn("szj", [128, NH * 129], F32)
    s1 = Tn("s1", [128, 2 * D], F32)
    sqscr = Tn("sqscr", [128, D], F32)
    x1tf = Tn("x1tf", [128, D], F32)
    x1tb = Tn("x1tb", [128, D], BF16)
    x1ff = Tn("x1ff", [128, 16 * 256], F32)
    x1fb = Tn("x1fb", [128, 16 * 256], BF16)
    wg = Tn("wg", [128, 16 * E], F32)
    zsb = Tn("zsb", [8, TPC], F32)
    mask = Tn("mask", [128, 256], BF16)
    idn = Tn("idn", [128, 128], BF16)
    idnf = Tn("idnf", [128, 128], F32)
    ut = Tn("ut", [128, 128], F32)
    onesr = Tn("onesr", [1, 128], F32)
    iota8 = Tn("iota8", [128, E], F32)
    cprefb = Tn("cprefb", [128, NC], F32)
    rd = [Tn("rd0", [128, 2], F32), Tn("rd1", [128, 2], F32)]
    dencp = Tn("dencp", [128, 4], F32)
    stats = Tn("stats", [128, 8], F32)
    ztile = Tn("ztile", [128, D], BF16)
    xrow = Tn("xrow", [128, D], BF16)
    zt = Tn("zt", [128, E], F32)
    oh0 = Tn("oh0", [128, E], F32)
    ohsc = Tn("ohsc", [128, E], F32)
    base8 = Tn("base8", [1, E], F32)
    negm = Tn("negm", [128, 1], F32)
    esum = Tn("esum", [128, 1], F32)
    escr8 = Tn("escr8", [128, E], F32)
    possel = Tn("possel", [128, 1], F32)
    q1 = Tn("q1", [128, 1], F32)
    okm = Tn("okm", [128, 1], F32)
    gate16 = Tn("gate16", [128, 16], F32)
    eidx16 = Tn("eidx16", [128, 16], F32)
    lpos16 = Tn("lpos16", [128, 16], F32)
    pidx16 = Tn("pidx16", [128, 16], I32)
    if debug:
        dbatt = Tn("dbatt", [128, 384], BF16)
        dbnum = Tn("dbnum", [128, 258], F32)
        dbrd = Tn("dbrd", [128, 2], F32)
        dbnumn = Tn("dbnumn", [128, 256], BF16)

    pattA = P("pattA", [128, 384], F32)
    pattB = P("pattB", [128, 384], F32)
    patt = [pattA, pattB]
    pnumA = P("pnumA", [128, HD + 1], F32)
    pnumB = P("pnumB", [128, HD + 1], F32)
    pnum = [pnumA, pnumB]
    ptpA = P("ptpA", [128, 128], BF16)
    ptpB = P("ptpB", [128, 128], BF16)
    ptp = [ptpA, ptpB]
    pout = P("pout", [128, 512], F32)
    px1t = [pattA[:, 0:128], pattB[:, 0:128]]
    pbig = P("pbig", [128, 512], F32)
    pz = pbig[0:8, 0:256]
    pszd = [pbig[:, 0:HD + 1], pout[:, 256:256 + HD + 1]]
    pcum = pbig[:, 386:386 + E]
    pt8 = pbig[0:1, 394:394 + E]

    es = contextlib.ExitStack()
    sgp = es.enter_context(nc.semaphore("sgp"))
    sld = es.enter_context(nc.semaphore("sld"))
    spe = es.enter_context(nc.semaphore("spe"))
    sdv = es.enter_context(nc.semaphore("sdv"))
    sac = es.enter_context(nc.semaphore("sac"))
    sst = es.enter_context(nc.semaphore("sst"))

    cnt = {"gp": 0, "ld": 0, "pe": 0, "dv": 0, "ac": 0, "st": 0}
    M = {}

    def bar(dv, inst):
        inst.then_inc(sdv, 1)
        cnt["dv"] += 1
        dv.wait_ge(sdv, cnt["dv"])

    with nc.Block() as block:
        @block.sync
        def _(sy):
            for ap_s, ap_d in ((MASK, mask), (IDN, idn), (IDNF, idnf), (UT, ut),
                               (ONESR, onesr), (IOTA8, iota8), (CPREFB, cprefb)):
                sy.dma_start(out=ap_d[:], in_=ap_s).then_inc(sld, 16)
                cnt["ld"] += 16
            sy.dma_start(out=wg[:], in_=WG.rearrange("(c p) e -> p c e", p=128)).then_inc(sld, 16)
            cnt["ld"] += 16
            M["consts"] = cnt["ld"]
            for j in range(NC):
                if j > 0:
                    sy.wait_ge(sdv, 1 + 16 * j)
                sy.dma_start(out=szj[:],
                             in_=SZALL[j].rearrange("h p e -> p h e")).then_inc(sld, 16)
                cnt["ld"] += 16
                M[("szjld", j)] = cnt["ld"]

        @block.vector
        def _(dv):
            dv.memset(szf[:], 0.0).then_inc(sdv, 1)
            cnt["dv"] += 1
            for j in range(NC):
                dv.wait_ge(sld, M[("szjld", j)])
                for h in range(NH):
                    sl = slice(h * 129, (h + 1) * 129)
                    dv.tensor_scalar(out=szj[:, sl], in0=szj[:, sl],
                                     scalar1=cprefb[:, j:j + 1], scalar2=None,
                                     op0=ALU.mult)
                    dv.tensor_add(out=szf[:, sl], in0=szf[:, sl], in1=szj[:, sl]).then_inc(sdv, 1)
                    cnt["dv"] += 1
            dv.tensor_copy(out=szb[:], in_=szf[:]).then_inc(sdv, 1)
            cnt["dv"] += 1
            dv.memset(vext[0][:], 1.0)
            dv.memset(vext[1][:], 1.0)
            dv.memset(ztile[:], 0.0)
            dv.memset(base8[:], 0.0).then_inc(sdv, 1)
            cnt["dv"] += 1
            M["szinit"] = cnt["dv"]

        if debug:
            @block.sync
            def _(sy):
                sy.wait_ge(sdv, M["szinit"])
                sy.dma_start(out=DB_SZB, in_=szb[:]).then_inc(sst, 16)
                cnt["st"] += 16
                M["dbszb"] = cnt["st"]

        @block.gpsimd
        def _(gp):
            gp.wait_ge(sdv, M["szinit"])
            for b in range(32):
                gp.dma_start(out=PBUF[b * 128:(b + 1) * 128, :], in_=ztile[:]).then_inc(sgp, 16)
                cnt["gp"] += 16
            gp.dma_start(out=PBUF[E * CLOC:E * CLOC + 1, :], in_=ztile[0:1, :]).then_inc(sgp, 16)
            cnt["gp"] += 16
            M["pbufz"] = cnt["gp"]

        for c in range(nch):
            @block.sync
            def _(sy, c=c):
                if c > 0:
                    sy.wait_ge(sdv, M[("chunkdone", c - 1)])
                    sy.wait_ge(sst, cnt["st"])
                else:
                    sy.wait_ge(sdv, M["szinit"])
                sy.dma_start(out=qf[:].rearrange("p (h t) -> p h t", t=256),
                             in_=QF[:, c * 256:(c + 1) * 256].rearrange("(h p) t -> p h t", p=128)).then_inc(sld, 16)
                sy.dma_start(out=kf[:].rearrange("p (h t) -> p h t", t=256),
                             in_=KF[:, c * 256:(c + 1) * 256].rearrange("(h p) t -> p h t", p=128)).then_inc(sld, 16)
                sy.dma_start(out=kt[:].rearrange("p (m d) -> p m d", d=D),
                             in_=KT[c * 256:(c + 1) * 256, :].rearrange("(m p) d -> p m d", p=128)).then_inc(sld, 16)
                cnt["ld"] += 48
                for mt in range(2):
                    sy.dma_start(
                        out=vext[mt][:].rearrange("p (h e) -> p h e", e=129)[:, :, 0:HD],
                        in_=VT[c * 256 + mt * 128: c * 256 + (mt + 1) * 128, :].rearrange("p (h d) -> p h d", d=HD),
                    ).then_inc(sld, 16)
                    cnt["ld"] += 16
                sy.dma_start(out=x0[:].rearrange("p (m d) -> p m d", d=D),
                             in_=X[c * 256:(c + 1) * 256, :].rearrange("(m p) d -> p m d", p=128)).then_inc(sld, 16)
                cnt["ld"] += 16
                M[("chld", c)] = cnt["ld"]

            peA = cnt["pe"]
            dvA = cnt["dv"]

            @block.tensor
            def _(te, c=c, peA=peA, dvA=dvA):
                if phase < 1: return
                if serial & 1:
                    te.wait_ge(sdv, cnt["dv"])
                    te.wait_ge(sac, cnt["ac"])
                te.wait_ge(sld, M[("chld", c)])
                te.wait_ge(sdv, M["szinit"])
                for h in range(NH):
                    qsl = slice(h * 256, h * 256 + 256)
                    q0 = slice(h * 256, h * 256 + 128)
                    q1s = slice(h * 256 + 128, h * 256 + 256)
                    if h >= 2:
                        te.wait_ge(sdv, dvA + h - 1)
                    b = patt[h % 2]
                    te.matmul(out=b[:, 0:256], lhsT=kf[:, q0], rhs=qf[:, qsl],
                              start=True, stop=True)
                    te.matmul(out=b[:, 256:384], lhsT=kf[:, q1s], rhs=qf[:, q1s],
                              start=True, stop=True).then_inc(spe, 1)
                    cnt["pe"] += 1
                M[("attmm", c)] = cnt["pe"]

            @block.vector
            def _(dv, c=c, peA=peA):
                if phase < 1: return
                for h in range(NH):
                    dv.wait_ge(spe, peA + h + 1)
                    b = patt[h % 2]
                    dv.tensor_mul(out=attsb[:, h * 384:h * 384 + 256], in0=b[:, 0:256], in1=mask[:])
                    dv.tensor_mul(out=attsb[:, h * 384 + 256:(h + 1) * 384], in0=b[:, 256:384],
                                  in1=mask[:, 0:128]).then_inc(sdv, 1)
                    cnt["dv"] += 1
                M[("maskev", c)] = cnt["dv"]

            peB = cnt["pe"]
            dvB = cnt["dv"]
            acB = cnt["ac"]

            @block.tensor
            def _(te, c=c, dvA=dvA, acB=acB):
                if phase < 2: return
                if serial & 2:
                    te.wait_ge(sdv, cnt["dv"])
                    te.wait_ge(sac, cnt["ac"])
                for h in range(NH):
                    q0 = slice(h * 256, h * 256 + 128)
                    q1s = slice(h * 256 + 128, h * 256 + 256)
                    vsl = slice(h * 129, (h + 1) * 129)
                    te.wait_ge(sdv, dvA + h + 1)
                    asl0 = slice(h * 384, h * 384 + 128)
                    asl0b = slice(h * 384 + 128, h * 384 + 256)
                    asl1 = slice(h * 384 + 256, h * 384 + 384)
                    if h >= 1:
                        te.wait_ge(sac, acB + 2 * (h - 1) + 1)
                    te.matmul(out=pnum[0][:], lhsT=attsb[:, asl0], rhs=vext[0][:, vsl],
                              start=True, stop=False)
                    te.matmul(out=pnum[0][:], lhsT=qf[:, q0], rhs=szb[:, vsl],
                              start=False, stop=True).then_inc(spe, 1)
                    cnt["pe"] += 1
                    if h >= 1:
                        te.wait_ge(sac, acB + 2 * (h - 1) + 2)
                    te.matmul(out=pnum[1][:], lhsT=attsb[:, asl0b], rhs=vext[0][:, vsl],
                              start=True, stop=False)
                    te.matmul(out=pnum[1][:], lhsT=attsb[:, asl1], rhs=vext[1][:, vsl],
                              start=False, stop=False)
                    te.matmul(out=pnum[1][:], lhsT=qf[:, q1s], rhs=szb[:, vsl],
                              start=False, stop=True).then_inc(spe, 1)
                    cnt["pe"] += 1
                M[("nummm", c)] = cnt["pe"]

            @block.vector
            def _(dv, c=c, peB=peB, acB=acB):
                if phase < 2: return
                for h in range(NH):
                    r = rd[h % 2]
                    if h >= 2:
                        dv.wait_ge(sac, acB + 2 * (h - 2) + 2)
                    dv.wait_ge(spe, peB + 2 * h + 1)
                    dv.tensor_copy(out=dencp[:, 0:2], in_=pnum[0][:, HD - 1:HD + 1]).then_inc(sdv, 1)
                    cnt["dv"] += 1
                    dv.wait_ge(sdv, cnt["dv"])
                    bar(dv, dv.tensor_scalar_max(out=r[:, 0:1], in0=dencp[:, 1:2], scalar1=1e-6))
                    dv.reciprocal(out=r[:, 0:1], in_=r[:, 0:1]).then_inc(sdv, 1)
                    cnt["dv"] += 1
                    M[("den0", c, h)] = cnt["dv"]
                    dv.wait_ge(spe, peB + 2 * h + 2)
                    dv.tensor_copy(out=dencp[:, 2:4], in_=pnum[1][:, HD - 1:HD + 1]).then_inc(sdv, 1)
                    cnt["dv"] += 1
                    dv.wait_ge(sdv, cnt["dv"])
                    bar(dv, dv.tensor_scalar_max(out=r[:, 1:2], in0=dencp[:, 3:4], scalar1=1e-6))
                    dv.reciprocal(out=r[:, 1:2], in_=r[:, 1:2]).then_inc(sdv, 1)
                    cnt["dv"] += 1
                    M[("den1", c, h)] = cnt["dv"]

            @block.scalar
            def _(ac, c=c):
                if phase < 2: return
                for h in range(NH):
                    r = rd[h % 2]
                    ac.wait_ge(sdv, M[("den0", c, h)])
                    ac.activation(out=numn[:, h * 256:h * 256 + 128], in_=pnum[0][:, 0:HD],
                                  func=AF.Copy, scale=r[:, 0:1]).then_inc(sac, 1)
                    cnt["ac"] += 1
                    ac.wait_ge(sdv, M[("den1", c, h)])
                    ac.activation(out=numn[:, h * 256 + 128:(h + 1) * 256], in_=pnum[1][:, 0:HD],
                                  func=AF.Copy, scale=r[:, 1:2]).then_inc(sac, 1)
                    cnt["ac"] += 1
                M[("numnac", c)] = cnt["ac"]

            peC = cnt["pe"]
            dvC = cnt["dv"]

            @block.tensor
            def _(te, c=c, dvC=dvC):
                if phase < 3: return
                if serial & 4:
                    te.wait_ge(sdv, cnt["dv"])
                    te.wait_ge(sac, cnt["ac"])
                if ("chunkdone", c - 1) in M:
                    te.wait_ge(sdv, M[("chunkdone", c - 1)])
                for h in range(NH):
                    vsl = slice(h * 129, (h + 1) * 129)
                    if h >= 2:
                        te.wait_ge(sdv, dvC + 2 * (h - 2) + 1)
                    b = pszd[h % 2]
                    te.matmul(out=b, lhsT=kt[:, h * 128:(h + 1) * 128],
                              rhs=vext[0][:, vsl], start=True, stop=False)
                    te.matmul(out=b, lhsT=kt[:, D + h * 128:D + (h + 1) * 128],
                              rhs=vext[1][:, vsl], start=False, stop=True).then_inc(spe, 1)
                    cnt["pe"] += 1
                M[("szmm", c)] = cnt["pe"]

            @block.vector
            def _(dv, c=c, peC=peC):
                if phase < 3: return
                for h in range(NH):
                    vsl = slice(h * 129, (h + 1) * 129)
                    dv.wait_ge(spe, peC + h + 1)
                    bar(dv, dv.tensor_add(out=szf[:, vsl], in0=szf[:, vsl], in1=pszd[h % 2]))
                    dv.tensor_copy(out=szb[:, vsl], in_=szf[:, vsl]).then_inc(sdv, 1)
                    cnt["dv"] += 1
                    M[("szcast", c, h)] = cnt["dv"]

            peD = cnt["pe"]
            dvD = cnt["dv"]

            @block.tensor
            def _(te, c=c, dvD=dvD):
                if phase < 4: return
                if serial & 8:
                    te.wait_ge(sdv, cnt["dv"])
                    te.wait_ge(sac, cnt["ac"])
                te.wait_ge(sac, M[("numnac", c)])
                k = 0
                for h in range(NH):
                    for lt in range(2):
                        if k >= 2:
                            te.wait_ge(sdv, dvD + k - 1)
                        te.transpose(out=ptp[k % 2][:],
                                     in_=numn[:, h * 256 + lt * 128: h * 256 + (lt + 1) * 128],
                                     identity=idn[:]).then_inc(spe, 1)
                        cnt["pe"] += 1
                        k += 1

            @block.vector
            def _(dv, c=c, peD=peD):
                if phase < 4: return
                k = 0
                for h in range(NH):
                    for lt in range(2):
                        dv.wait_ge(spe, peD + k + 1)
                        dv.tensor_copy(
                            out=numfm[:, h * 256 + lt * 128: h * 256 + (lt + 1) * 128],
                            in_=ptp[k % 2][:]).then_inc(sdv, 1)
                        cnt["dv"] += 1
                        k += 1
                M[("numfm", c, NH - 1)] = cnt["dv"]

            dv1 = cnt["dv"]
            pe1 = cnt["pe"]
            for oc in range(4):
                @block.gpsimd
                def _(gp, c=c, oc=oc):
                    if phase < 5: return
                    slot = oc % 2
                    key = ("woutfree", slot)
                    if key in M:
                        gp.wait_ge(spe, M[key])
                    gp.dma_start(
                        out=wout[:, slot * NH * 512:(slot + 1) * NH * 512].rearrange("p (hb f) -> p hb f", f=512),
                        in_=WOUTT[:, oc * 512:(oc + 1) * 512].rearrange("(hb p) f -> p hb f", p=128),
                    ).then_inc(sgp, 16)
                    cnt["gp"] += 16
                    M[("woutld", c, oc)] = cnt["gp"]

                @block.tensor
                def _(te, c=c, oc=oc, pe1=pe1, dv1=dv1):
                    if phase < 5: return
                    if serial & 16:
                        te.wait_ge(sdv, cnt["dv"])
                        te.wait_ge(sac, cnt["ac"])
                    te.wait_ge(sdv, M[("numfm", c, NH - 1)])
                    te.wait_ge(sgp, M[("woutld", c, oc)])
                    slot = oc % 2
                    for lt in range(2):
                        k = 2 * oc + lt
                        if k >= 1:
                            te.wait_ge(sdv, dv1 + k)
                        last = None
                        for h in range(NH):
                            last = te.matmul(
                                out=pout[:],
                                lhsT=numfm[:, h * 256 + lt * 128: h * 256 + (lt + 1) * 128],
                                rhs=wout[:, slot * NH * 512 + h * 512: slot * NH * 512 + (h + 1) * 512],
                                start=(h == 0), stop=(h == NH - 1),
                            )
                        last.then_inc(spe, 1)
                        cnt["pe"] += 1
                    M[("woutfree", slot)] = cnt["pe"]

            @block.vector
            def _(dv, c=c, pe1=pe1):
                if phase < 5: return
                for oc in range(4):
                    for lt in range(2):
                        k = 2 * oc + lt
                        dv.wait_ge(spe, pe1 + k + 1)
                        dv.scalar_tensor_tensor(
                            out=s1[:, lt * D + oc * 512: lt * D + (oc + 1) * 512],
                            in0=pout[:], scalar=1.0,
                            in1=x0[:, lt * D + oc * 512: lt * D + (oc + 1) * 512],
                            op0=ALU.mult, op1=ALU.add,
                        ).then_inc(sdv, 1)
                        cnt["dv"] += 1
                M[("s1done", c)] = cnt["dv"]

            if debug and c == 0:
                @block.sync
                def _(sy):
                    sy.wait_ge(sdv, M[("s1done", 0)])
                    sy.dma_start(out=DB_S1, in_=s1[:]).then_inc(sst, 16)
                    cnt["st"] += 16

            for lt in range(2):
                dv2 = cnt["dv"]
                ac2 = cnt["ac"]
                pe2 = cnt["pe"]

                @block.vector
                def _(dv, c=c, lt=lt, ac2=ac2, pe2=pe2):
                    if phase < 6: return
                    dv.wait_ge(sdv, M[("s1done", c)])
                    dv.reduce_sum(out=stats[:, 0:1], in_=s1[:, lt * D:(lt + 1) * D], axis=AX.X).then_inc(sdv, 1)
                    cnt["dv"] += 1
                    dv.wait_ge(sac, ac2 + 1)
                    dv.tensor_scalar_mul(out=stats[:, 0:1], in0=stats[:, 0:1], scalar1=1.0 / D)
                    bar(dv, dv.tensor_scalar_mul(out=stats[:, 1:2], in0=stats[:, 1:2], scalar1=1.0 / D))
                    bar(dv, dv.tensor_mul(out=stats[:, 6:7], in0=stats[:, 0:1], in1=stats[:, 0:1]))
                    bar(dv, dv.tensor_sub(out=stats[:, 2:3], in0=stats[:, 1:2], in1=stats[:, 6:7]))
                    dv.tensor_scalar_add(out=stats[:, 2:3], in0=stats[:, 2:3], scalar1=1e-5).then_inc(sdv, 1)
                    cnt["dv"] += 1
                    M[("var", c, lt)] = cnt["dv"]
                    dv.wait_ge(sac, ac2 + 2)
                    dv.tensor_copy(out=stats[:, 4:5], in_=stats[:, 3:4]).then_inc(sdv, 1)
                    cnt["dv"] += 1
                    M[("lnv2", c, lt)] = cnt["dv"]
                    dv.wait_ge(sac, ac2 + 3)
                    dv.wait_ge(sdv, cnt["dv"])
                    dv.tensor_scalar(out=x1tf[:], in0=s1[:, lt * D:(lt + 1) * D],
                                     scalar1=stats[:, 0:1], scalar2=stats[:, 5:6],
                                     op0=ALU.subtract, op1=ALU.mult).then_inc(sdv, 1)
                    cnt["dv"] += 1
                    dv.wait_ge(sst, M.get("x1stash", 0))
                    dv.tensor_copy(out=x1tb[:], in_=x1tf[:]).then_inc(sdv, 1)
                    cnt["dv"] += 1
                    M[("x1tb", c, lt)] = cnt["dv"]
                    for c16 in range(16):
                        dv.wait_ge(spe, pe2 + c16 + 1)
                        dv.tensor_copy(out=x1ff[:, c16 * 256 + lt * 128: c16 * 256 + lt * 128 + 128],
                                       in_=px1t[c16 % 2][:]).then_inc(sdv, 1)
                        cnt["dv"] += 1
                    M[("x1ff", c, lt)] = cnt["dv"]

                @block.scalar
                def _(ac, c=c, lt=lt, dv2=dv2):
                    if phase < 6: return
                    ac.wait_ge(sdv, dv2 + 1)
                    ac.activation(out=sqscr[:], in_=s1[:, lt * D:(lt + 1) * D], func=AF.Square,
                                  accum_out=stats[:, 1:2]).then_inc(sac, 1)
                    ac.wait_ge(sdv, M[("var", c, lt)])
                    ac.activation(out=stats[:, 3:4], in_=stats[:, 2:3], func=AF.Ln).then_inc(sac, 1)
                    ac.wait_ge(sdv, M[("lnv2", c, lt)])
                    ac.activation(out=stats[:, 5:6], in_=stats[:, 4:5], func=AF.Exp,
                                  scale=-0.5).then_inc(sac, 1)
                    cnt["ac"] += 3

                @block.sync
                def _(sy, c=c, lt=lt):
                    if phase < 6: return
                    sy.wait_ge(sdv, M[("x1tb", c, lt)])
                    sy.dma_start(out=X1T[c * 256 + lt * 128: c * 256 + (lt + 1) * 128, :],
                                 in_=x1tb[:]).then_inc(sst, 16)
                    cnt["st"] += 16
                    M["x1stash"] = cnt["st"]
                    M[("x1stash", c, lt)] = cnt["st"]

                @block.tensor
                def _(te, c=c, lt=lt, pe2=pe2):
                    if phase < 6: return
                    if serial & 32:
                        te.wait_ge(sdv, cnt["dv"])
                        te.wait_ge(sac, cnt["ac"])
                    te.wait_ge(sdv, M[("x1tb", c, lt)])
                    base_dv = M[("x1tb", c, lt)]
                    for c16 in range(16):
                        if c16 >= 2:
                            te.wait_ge(sdv, base_dv + c16 - 1)
                        te.transpose(out=px1t[c16 % 2][:],
                                     in_=x1tf[:, c16 * 128:(c16 + 1) * 128],
                                     identity=idnf[:]).then_inc(spe, 1)
                        cnt["pe"] += 1

            @block.tensor
            def _(te, c=c):
                if phase < 7: return
                te.wait_ge(sdv, M[("x1ff", c, 1)])
                last = None
                for c16 in range(16):
                    last = te.matmul(out=pz, lhsT=wg[:, c16 * E:(c16 + 1) * E],
                                     rhs=x1ff[:, c16 * 256:(c16 + 1) * 256],
                                     start=(c16 == 0), stop=(c16 == 15))
                last.then_inc(spe, 1)
                cnt["pe"] += 1
                M[("zmm", c)] = cnt["pe"]

            @block.vector
            def _(dv, c=c):
                if phase < 7: return
                dv.wait_ge(spe, M[("zmm", c)])
                dv.wait_ge(sdv, M[("x1ff", c, 1)])
                dv.tensor_copy(out=zsb[:, c * 256:(c + 1) * 256], in_=pz).then_inc(sdv, 1)
                cnt["dv"] += 1
                dv.tensor_copy(out=x1fb[:], in_=x1ff[:]).then_inc(sdv, 1)
                cnt["dv"] += 1
                M[("x1fb", c)] = cnt["dv"]
                M[("chunkdone", c)] = cnt["dv"]

            @block.sync
            def _(sy, c=c):
                if phase < 7: return
                sy.wait_ge(sdv, M[("x1fb", c)])
                sy.dma_start(out=X1F[:, c * 256:(c + 1) * 256].rearrange("(c16 p) t -> p c16 t", p=128),
                             in_=x1fb[:].rearrange("p (c16 t) -> p c16 t", t=256)).then_inc(sst, 16)
                cnt["st"] += 16
                M[("x1fstash", c)] = cnt["st"]

        if trunc or nch < NCH:
            @block.sync
            def _(sy):
                if phase >= 7:
                    sy.wait_ge(sdv, M[("chunkdone", nch - 1)])
                    sy.wait_ge(sst, M[("x1fstash", nch - 1)])
                elif phase == 6:
                    sy.wait_ge(sdv, M[("x1ff", nch - 1, 1)])
                elif phase == 5:
                    sy.wait_ge(sdv, M[("s1done", nch - 1)])
                elif phase == 4:
                    sy.wait_ge(sdv, M[("numfm", nch - 1, NH - 1)])
                elif phase == 3:
                    sy.wait_ge(sdv, M[("szcast", nch - 1, NH - 1)])
                elif phase == 2:
                    sy.wait_ge(sac, M[("numnac", nch - 1)])
                elif phase == 1:
                    sy.wait_ge(sdv, M[("maskev", nch - 1)])
                sy.wait_ge(sgp, cnt["gp"])
        else:
            if debug:
                @block.sync
                def _(sy):
                    sy.wait_ge(sdv, M[("chunkdone", NCH - 1)])
                    sy.dma_start(out=DB_ZSB, in_=zsb[:]).then_inc(sst, 16)
                    cnt["st"] += 16

            for tt in range(NTT):
                pe4 = cnt["pe"]
                dv4 = cnt["dv"]
                ac4 = cnt["ac"]

                @block.tensor
                def _(te, tt=tt):
                    te.wait_ge(sdv, M[("chunkdone", NCH - 1)])
                    if tt > 0:
                        te.wait_ge(sdv, M[("pidx", tt - 1)])
                    te.transpose(out=pcum, in_=zsb[0:8, tt * 128:(tt + 1) * 128],
                                 identity=idnf[0:8, 0:8]).then_inc(spe, 1)
                    cnt["pe"] += 1

                @block.vector
                def _(dv, tt=tt, pe4=pe4, ac4=ac4):
                    dv.wait_ge(spe, pe4 + 1)
                    bar(dv, dv.tensor_copy(out=zt[:], in_=pcum))
                    cnt["dv"] += 0
                    M[("ztev", tt)] = cnt["dv"]
                    bar(dv, dv.reduce_max(out=negm[:], in_=zt[:], axis=AX.X, negate=True))
                    sdvmark = cnt["dv"]
                    dv.then_maybe_inc(None) if False else None
                    ncnt = cnt["dv"] + 1
                    cnt["dv"] = ncnt - 1
                    dv.sem_inc(sdv, 1)
                    cnt["dv"] += 1
                    M[("negmready", tt)] = cnt["dv"]
                    dv.wait_ge(sac, ac4 + 1)
                    bar(dv, dv.reciprocal(out=gate16[:, tt:tt + 1], in_=esum[:]))
                    bar(dv, dv.tensor_scalar(out=oh0[:], in0=zt[:], scalar1=negm[:, 0:1], scalar2=0.0,
                                             op0=ALU.add, op1=ALU.is_ge))
                    bar(dv, dv.tensor_tensor_scan(out=ohsc[:], data0=oh0[:], data1=oh0[:],
                                                  initial=0.0, op0=ALU.add, op1=ALU.bypass))
                    bar(dv, dv.tensor_single_scalar(out=ohsc[:], in_=ohsc[:], scalar=1.0, op=ALU.is_equal))
                    bar(dv, dv.tensor_mul(out=oh0[:], in0=oh0[:], in1=ohsc[:]))
                    bar(dv, dv.tensor_mul(out=ohsc[:], in0=oh0[:], in1=iota8[:]))
                    dv.reduce_sum(out=eidx16[:, tt:tt + 1], in_=ohsc[:], axis=AX.X).then_inc(sdv, 1)
                    cnt["dv"] += 1
                    M[("ohready", tt)] = cnt["dv"]

                @block.scalar
                def _(ac, tt=tt):
                    ac.wait_ge(sdv, M[("negmready", tt)])
                    ac.activation(out=escr8[:], in_=zt[:], func=AF.Exp, bias=negm[:, 0:1],
                                  accum_out=esum[:, 0:1]).then_inc(sac, 1)
                    cnt["ac"] += 1

                @block.tensor
                def _(te, tt=tt):
                    te.wait_ge(sdv, M[("ohready", tt)])
                    te.matmul(out=pcum, lhsT=ut[:], rhs=oh0[:], start=True, stop=False)
                    te.matmul(out=pcum, lhsT=onesr[:], rhs=base8[:], start=False,
                              stop=True).then_inc(spe, 1)
                    te.matmul(out=pt8, lhsT=ut[:, 127:128], rhs=oh0[:], start=True,
                              stop=True).then_inc(spe, 1)
                    cnt["pe"] += 2
                    M[("cummm", tt)] = cnt["pe"]

                @block.vector
                def _(dv, tt=tt):
                    dv.wait_ge(spe, M[("cummm", tt)])
                    bar(dv, dv.tensor_mul(out=ohsc[:], in0=oh0[:], in1=pcum))
                    bar(dv, dv.reduce_sum(out=possel[:], in_=ohsc[:], axis=AX.X))
                    bar(dv, dv.tensor_add(out=base8[:], in0=base8[:], in1=pt8))
                    bar(dv, dv.tensor_scalar_sub(out=lpos16[:, tt:tt + 1], in0=possel[:], scalar1=1.0))
                    bar(dv, dv.tensor_single_scalar(out=okm[:], in_=possel[:], scalar=float(CLOC), op=ALU.is_le))
                    bar(dv, dv.scalar_tensor_tensor(out=q1[:], in0=eidx16[:, tt:tt + 1], scalar=float(CLOC),
                                                    in1=possel[:], op0=ALU.mult, op1=ALU.add))
                    bar(dv, dv.tensor_scalar_sub(out=q1[:], in0=q1[:], scalar1=float(E * CLOC + 1)))
                    bar(dv, dv.tensor_mul(out=q1[:], in0=q1[:], in1=okm[:]))
                    bar(dv, dv.tensor_scalar_add(out=q1[:], in0=q1[:], scalar1=float(E * CLOC)))
                    dv.tensor_copy(out=pidx16[:, tt:tt + 1], in_=q1[:]).then_inc(sdv, 1)
                    cnt["dv"] += 1
                    M[("pidx", tt)] = cnt["dv"]

            @block.gpsimd
            def _(gp):
                gp.wait_ge(sgp, M["pbufz"])
                gp.wait_ge(sst, M[("x1stash", NCH - 1, 1)])
                for tt in range(NTT):
                    gp.wait_ge(sdv, M[("pidx", tt)])
                    gp.dma_start(out=xrow[:], in_=X1T[tt * 128:(tt + 1) * 128, :]).then_inc(sgp, 16)
                    cnt["gp"] += 16
                    gp.wait_ge(sgp, cnt["gp"])
                    gp.indirect_dma_start(
                        out=PBUF, out_offset=IndirectOffsetOnAxis(pidx16[:, tt:tt + 1], 0),
                        in_=xrow[:], in_offset=None,
                    ).then_inc(sgp, 16)
                    cnt["gp"] += 16
                    gp.wait_ge(sgp, cnt["gp"])
                M["scatdone"] = cnt["gp"]

            @block.sync
            def _(sy):
                sy.wait_ge(sdv, M[("pidx", NTT - 1)])
                sy.dma_start(out=CNT, in_=base8[:]).then_inc(sst, 16)
                sy.dma_start(out=EIDX, in_=eidx16[:]).then_inc(sst, 16)
                sy.dma_start(out=LPOS, in_=lpos16[:]).then_inc(sst, 16)
                sy.dma_start(out=GATE, in_=gate16[:]).then_inc(sst, 16)
                cnt["st"] += 64
                sy.wait_ge(sst, cnt["st"])
                sy.wait_ge(sgp, M["scatdone"])

    es.close()
    return nc


def build_l3():
    nc = bass.Bass("TRN2", target_bir_lowering=False, detect_race_conditions=False)
    RUNS = nc.dram_tensor("RUNS", [NC * CLOC, D], BF16, kind="ExternalInput").ap()
    CNTA = nc.dram_tensor("CNTA", [NC, E], F32, kind="ExternalInput").ap()
    X1F = nc.dram_tensor("X1F", [D, TPC], BF16, kind="ExternalInput").ap()
    X1T = nc.dram_tensor("X1T", [TPC, D], BF16, kind="ExternalInput").ap()
    W13E = nc.dram_tensor("W13E", [D, 2 * HF], F32, kind="ExternalInput").ap()
    W2E = nc.dram_tensor("W2E", [HF, D], F32, kind="ExternalInput").ap()
    SW13 = nc.dram_tensor("SW13", [D, 2 * HF], F32, kind="ExternalInput").ap()
    SW2 = nc.dram_tensor("SW2", [HF, D], F32, kind="ExternalInput").ap()
    IDN = nc.dram_tensor("IDN", [128, 128], BF16, kind="ExternalInput").ap()
    EJC = nc.dram_tensor("EJC", [8, NC * 128], F32, kind="ExternalInput").ap()
    SELB = nc.dram_tensor("SELB", [8, E], F32, kind="ExternalInput").ap()
    SLT = nc.dram_tensor("SLT", [8, 8], F32, kind="ExternalInput").ap()
    LPC = nc.dram_tensor("LPC", [128, 4], F32, kind="ExternalInput").ap()
    SLC = nc.dram_tensor("SLC", [128, 16], F32, kind="ExternalInput").ap()

    RETP = nc.dram_tensor("RETP", [NC * CLOC + 1, D], BF16, kind="ExternalOutput").ap()
    S2P = nc.dram_tensor("S2P", [TPC, D], F32, kind="ExternalOutput").ap()

    XD = nc.dram_tensor("XD", [CAP + 1, D], BF16, kind="Internal").ap()

    NHC = HF // 128
    SCW = 1024
    NSC = CAP // SCW
    RT = SCW // 128
    NOC = D // 256

    Tn = nc.alloc_sbuf_tensor
    P = nc.alloc_psum_tensor
    cnta = Tn("cnta", [8, E], F32)
    selb = Tn("selb", [8, E], F32)
    slt = Tn("slt", [8, 8], F32)
    ejc = Tn("ejc", [8, NC * 128], F32)
    lpc = Tn("lpc", [128, 4], F32)
    slc = Tn("slc", [128, 16], F32)
    idn = Tn("idn", [128, 128], BF16)
    scr8 = Tn("scr8", [8, E], F32)
    gcol = Tn("gcol", [8, 2], F32)
    gbb = Tn("gbb", [128, 16], F32)
    ubb = Tn("ubb", [128, 8], F32)
    t1 = Tn("t1", [128, 1], F32)
    t2 = Tn("t2", [128, 1], F32)
    t3 = Tn("t3", [128, 1], F32)
    acc = Tn("acc", [128, 1], F32)
    vany = Tn("vany", [128, 1], F32)
    desti = Tn("desti", [128, 32], I32)
    retidx = Tn("retidx", [128, 16], I32)
    ztile = Tn("ztile", [128, D], BF16)
    rbuf = Tn("rbuf", [128, D], BF16)
    xsf = Tn("xsf", [128, 2 * D], BF16)
    xdf = Tn("xdf", [128, 16 * SCW], BF16)
    hfm = Tn("hfm", [128, NHC * SCW], BF16)
    hs = Tn("hs", [128, 512], BF16)
    w13 = Tn("w13", [128, 2 * 16 * 256], BF16)
    w2s = Tn("w2s", [128, 2 * NHC * 256], BF16)
    oesb = Tn("oesb", [128, RT * D], BF16)
    s2oc = Tn("s2oc", [128, RT * 256], F32)
    x1r = Tn("x1r", [128, 2 * D], BF16)

    paux = P("paux", [128, 512], F32)
    pgb = paux[0:8, 0:8]
    pbc = paux[:, 8:24]
    h1p = P("h1p", [128, 512], F32)
    h3p = P("h3p", [128, 512], F32)
    oep = [P("oep0", [128, 256], F32), P("oep1", [128, 256], F32)]
    ptpa = P("ptpa", [128, 128], BF16)
    ptpb = P("ptpb", [128, 128], BF16)
    ptp2 = [ptpa, ptpb]

    es = contextlib.ExitStack()
    sgp = es.enter_context(nc.semaphore("sgp"))
    sld = es.enter_context(nc.semaphore("sld"))
    spe = es.enter_context(nc.semaphore("spe"))
    sdv = es.enter_context(nc.semaphore("sdv"))
    sac = es.enter_context(nc.semaphore("sac"))
    sst = es.enter_context(nc.semaphore("sst"))

    cnt = {"gp": 0, "ld": 0, "pe": 0, "dv": 0, "ac": 0, "st": 0}
    M = {}

    def bar(dv, inst):
        inst.then_inc(sdv, 1)
        cnt["dv"] += 1
        dv.wait_ge(sdv, cnt["dv"])

    with nc.Block() as block:
        @block.sync
        def _(sy):
            for ap_s, ap_d in ((CNTA, cnta), (SELB, selb), (SLT, slt), (EJC, ejc),
                               (LPC, lpc), (SLC, slc), (IDN, idn)):
                sy.dma_start(out=ap_d[:], in_=ap_s).then_inc(sld, 16)
                cnt["ld"] += 16
            M["consts"] = cnt["ld"]

        @block.tensor
        def _(te):
            te.wait_ge(sld, M["consts"])
            te.matmul(out=pgb, lhsT=slt[:], rhs=cnta[:], start=True, stop=True).then_inc(spe, 1)
            cnt["pe"] += 1
            M["gbmm"] = cnt["pe"]

        @block.vector
        def _(dv):
            dv.wait_ge(spe, M["gbmm"])
            bar(dv, dv.tensor_mul(out=scr8[:], in0=pgb, in1=selb[:]))
            bar(dv, dv.reduce_sum(out=gcol[0:8, 0:1], in_=scr8[:], axis=AX.X))
            bar(dv, dv.tensor_mul(out=scr8[:], in0=cnta[:], in1=selb[:]))
            bar(dv, dv.reduce_sum(out=gcol[0:8, 1:2], in_=scr8[:], axis=AX.X))
            M["gcol"] = cnt["dv"]

        @block.tensor
        def _(te):
            te.wait_ge(sdv, M["gcol"])
            last = None
            for j in range(NC):
                te.matmul(out=pbc[:, j:j + 1], lhsT=ejc[:, j * 128:(j + 1) * 128],
                          rhs=gcol[0:8, 0:1], start=True, stop=True)
                last = te.matmul(out=pbc[:, 8 + j: 9 + j], lhsT=ejc[:, j * 128:(j + 1) * 128],
                                 rhs=gcol[0:8, 1:2], start=True, stop=True)
            last.then_inc(spe, 1)
            cnt["pe"] += 1
            M["bcmm"] = cnt["pe"]

        @block.vector
        def _(dv):
            dv.wait_ge(spe, M["bcmm"])
            bar(dv, dv.tensor_copy(out=gbb[:], in_=pbc))
            bar(dv, dv.tensor_scalar_min(out=ubb[:], in0=gbb[:, 8:16], scalar1=float(CLOC)))
            bar(dv, dv.tensor_add(out=ubb[:], in0=ubb[:], in1=gbb[:, 0:8]))
            for tt2 in range(32):
                j, q = tt2 // 4, tt2 % 4
                bar(dv, dv.tensor_add(out=t1[:], in0=lpc[:, q:q + 1], in1=gbb[:, j:j + 1]))
                bar(dv, dv.tensor_tensor(out=t2[:], in0=lpc[:, q:q + 1], in1=gbb[:, 8 + j:9 + j],
                                         op=ALU.is_lt))
                bar(dv, dv.tensor_single_scalar(out=t3[:], in_=t1[:], scalar=float(CAP), op=ALU.is_lt))
                bar(dv, dv.tensor_mul(out=t2[:], in0=t2[:], in1=t3[:]))
                bar(dv, dv.tensor_scalar_sub(out=t1[:], in0=t1[:], scalar1=float(CAP)))
                bar(dv, dv.tensor_mul(out=t1[:], in0=t1[:], in1=t2[:]))
                bar(dv, dv.tensor_scalar_add(out=t1[:], in0=t1[:], scalar1=float(CAP)))
                bar(dv, dv.tensor_copy(out=desti[:, tt2:tt2 + 1], in_=t1[:]))
            M["desti"] = cnt["dv"]
            for stt in range(16):
                bar(dv, dv.memset(acc[:], 0.0))
                bar(dv, dv.memset(vany[:], 0.0))
                for j in range(NC):
                    bar(dv, dv.tensor_tensor(out=t1[:], in0=slc[:, stt:stt + 1],
                                             in1=gbb[:, j:j + 1], op=ALU.is_ge))
                    bar(dv, dv.tensor_tensor(out=t2[:], in0=slc[:, stt:stt + 1],
                                             in1=ubb[:, j:j + 1], op=ALU.is_lt))
                    bar(dv, dv.tensor_mul(out=t1[:], in0=t1[:], in1=t2[:]))
                    bar(dv, dv.tensor_sub(out=t2[:], in0=slc[:, stt:stt + 1], in1=gbb[:, j:j + 1]))
                    bar(dv, dv.tensor_scalar_add(out=t2[:], in0=t2[:], scalar1=float(j * CLOC)))
                    bar(dv, dv.tensor_mul(out=t2[:], in0=t2[:], in1=t1[:]))
                    bar(dv, dv.tensor_add(out=acc[:], in0=acc[:], in1=t2[:]))
                    bar(dv, dv.tensor_add(out=vany[:], in0=vany[:], in1=t1[:]))
                bar(dv, dv.tensor_scalar(out=t1[:], in0=vany[:], scalar1=float(-NC * CLOC),
                                         scalar2=float(NC * CLOC), op0=ALU.mult, op1=ALU.add))
                bar(dv, dv.tensor_add(out=acc[:], in0=acc[:], in1=t1[:]))
                bar(dv, dv.tensor_copy(out=retidx[:, stt:stt + 1], in_=acc[:]))
            dv.memset(ztile[:], 0.0).then_inc(sdv, 1)
            cnt["dv"] += 1
            M["retidx"] = cnt["dv"]

        @block.gpsimd
        def _(gp):
            gp.wait_ge(sdv, M["retidx"])
            for b in range(16):
                gp.dma_start(out=XD[b * 128:(b + 1) * 128, :], in_=ztile[:]).then_inc(sgp, 16)
                cnt["gp"] += 16
            gp.dma_start(out=XD[CAP:CAP + 1, :], in_=ztile[0:1, :]).then_inc(sgp, 16)
            cnt["gp"] += 16
            for b in range(32):
                gp.dma_start(out=RETP[b * 128:(b + 1) * 128, :], in_=ztile[:]).then_inc(sgp, 16)
                cnt["gp"] += 16
            gp.dma_start(out=RETP[NC * CLOC:NC * CLOC + 1, :], in_=ztile[0:1, :]).then_inc(sgp, 16)
            cnt["gp"] += 16
            for tt2 in range(32):
                gp.dma_start(out=rbuf[:], in_=RUNS[tt2 * 128:(tt2 + 1) * 128, :]).then_inc(sgp, 16)
                cnt["gp"] += 16
                gp.wait_ge(sgp, cnt["gp"])
                gp.indirect_dma_start(
                    out=XD, out_offset=IndirectOffsetOnAxis(desti[:, tt2:tt2 + 1], 0),
                    in_=rbuf[:], in_offset=None,
                ).then_inc(sgp, 16)
                cnt["gp"] += 16
                gp.wait_ge(sgp, cnt["gp"])
            M["asm"] = cnt["gp"]

        def quadrant(tag, w13ap, w2ap, x1f_src):
            for sc in range(NSC):
                if tag == "e":
                    for rt in range(RT):
                        @block.gpsimd
                        def _(gp, sc=sc, rt=rt):
                            if rt == 0 and sc == 0:
                                gp.wait_ge(sgp, M["asm"])
                            if ("e", "rtc", sc, rt - 2) in M:
                                gp.wait_ge(spe, M[("e", "rtc", sc, rt - 2)])
                            if ("e", "xdffree", sc - 1) in M and rt == 0:
                                gp.wait_ge(sdv, M[("e", "xdffree", sc - 1)])
                            gp.dma_start(out=xsf[:, (rt % 2) * D:(rt % 2 + 1) * D],
                                         in_=XD[sc * SCW + rt * 128: sc * SCW + (rt + 1) * 128, :]
                                         ).then_inc(sgp, 16)
                            cnt["gp"] += 16
                            M[("e", "row", sc, rt)] = cnt["gp"]

                        dv0 = cnt["dv"]

                        @block.tensor
                        def _(te, sc=sc, rt=rt, dv0=dv0):
                            te.wait_ge(sgp, M[("e", "row", sc, rt)])
                            te.wait_ge(sdv, dv0)
                            if ("e", "xdffree", sc - 1) in M and rt == 0:
                                te.wait_ge(sdv, M[("e", "xdffree", sc - 1)])
                            for c16 in range(16):
                                if c16 >= 2:
                                    te.wait_ge(sdv, dv0 + c16 - 1)
                                te.transpose(out=ptp2[c16 % 2][:],
                                             in_=xsf[:, (rt % 2) * D + c16 * 128: (rt % 2) * D + (c16 + 1) * 128],
                                             identity=idn[:]).then_inc(spe, 1)
                                cnt["pe"] += 1
                            M[("e", "rtc", sc, rt)] = cnt["pe"]

                        pe0 = M[("e", "rtc", sc, rt)] - 16

                        @block.vector
                        def _(dv, sc=sc, rt=rt, pe0=pe0):
                            for c16 in range(16):
                                dv.wait_ge(spe, pe0 + c16 + 1)
                                dv.tensor_copy(
                                    out=xdf[:, c16 * SCW + rt * 128: c16 * SCW + (rt + 1) * 128],
                                    in_=ptp2[c16 % 2][:],
                                ).then_inc(sdv, 1)
                                cnt["dv"] += 1
                            M[(tag, "xdf", sc)] = cnt["dv"]
                else:
                    @block.gpsimd
                    def _(gp, sc=sc):
                        if ("s", "xdffree", sc - 1) in M:
                            gp.wait_ge(sdv, M[("s", "xdffree", sc - 1)])
                        elif ("e", "xdffree", NSC - 1) in M:
                            gp.wait_ge(sdv, M[("e", "xdffree", NSC - 1)])
                        gp.dma_start(
                            out=xdf[:].rearrange("p (c t) -> p c t", t=SCW),
                            in_=x1f_src[:, sc * SCW:(sc + 1) * SCW].rearrange("(c p) t -> p c t", p=128),
                        ).then_inc(sgp, 16)
                        cnt["gp"] += 16
                        M[("s", "xdfld", sc)] = cnt["gp"]

                    @block.vector
                    def _(dv, sc=sc):
                        dv.wait_ge(sgp, M[("s", "xdfld", sc)])
                        dv.sem_inc(sdv, 1)
                        cnt["dv"] += 1
                        M[("s", "xdf", sc)] = cnt["dv"]

                for hc in range(NHC):
                    @block.gpsimd
                    def _(gp, tag=tag, sc=sc, hc=hc):
                        slot = hc % 2
                        key = ("w13cons", slot)
                        if key in M:
                            gp.wait_ge(spe, M[key])
                        gp.dma_start(
                            out=w13[:, slot * 4096:(slot + 1) * 4096].rearrange("p (c f) -> p c f", f=256),
                            in_=w13ap[:, hc * 256:(hc + 1) * 256].rearrange("(c p) f -> p c f", p=128),
                        ).then_inc(sgp, 16)
                        cnt["gp"] += 16
                        M[("w13", tag, sc, hc)] = cnt["gp"]

                    @block.tensor
                    def _(te, tag=tag, sc=sc, hc=hc):
                        slot = hc % 2
                        te.wait_ge(sgp, M[("w13", tag, sc, hc)])
                        te.wait_ge(sdv, M[(tag, "xdf", sc)])
                        for hf in range(2):
                            if ("hmul",) in M:
                                te.wait_ge(sdv, M[("hmul",)])
                                te.wait_ge(sac, M[("hsil",)])
                            last = None
                            for c16 in range(16):
                                last = te.matmul(
                                    out=h1p[:],
                                    lhsT=w13[:, slot * 4096 + c16 * 256: slot * 4096 + c16 * 256 + 128],
                                    rhs=xdf[:, c16 * SCW + hf * 512: c16 * SCW + (hf + 1) * 512],
                                    start=(c16 == 0), stop=(c16 == 15))
                            last.then_inc(spe, 1)
                            cnt["pe"] += 1
                            M[("h1mm", tag, sc, hc, hf)] = cnt["pe"]
                            for c16 in range(16):
                                last = te.matmul(
                                    out=h3p[:],
                                    lhsT=w13[:, slot * 4096 + c16 * 256 + 128: slot * 4096 + (c16 + 1) * 256],
                                    rhs=xdf[:, c16 * SCW + hf * 512: c16 * SCW + (hf + 1) * 512],
                                    start=(c16 == 0), stop=(c16 == 15))
                            last.then_inc(spe, 1)
                            cnt["pe"] += 1
                            M[("h3mm", tag, sc, hc, hf)] = cnt["pe"]
                        M[("w13cons", slot)] = cnt["pe"]

                    for hf in range(2):
                        @block.scalar
                        def _(ac, tag=tag, sc=sc, hc=hc, hf=hf):
                            ac.wait_ge(spe, M[("h1mm", tag, sc, hc, hf)])
                            ac.activation(out=hs[:], in_=h1p[:], func=AF.Silu).then_inc(sac, 1)
                            cnt["ac"] += 1
                            M[("hsil",)] = cnt["ac"]
                            M[("hsilk", tag, sc, hc, hf)] = cnt["ac"]

                        @block.vector
                        def _(dv, tag=tag, sc=sc, hc=hc, hf=hf):
                            dv.wait_ge(sac, M[("hsilk", tag, sc, hc, hf)])
                            dv.wait_ge(spe, M[("h3mm", tag, sc, hc, hf)])
                            dv.tensor_mul(out=hfm[:, hc * SCW + hf * 512: hc * SCW + (hf + 1) * 512],
                                          in0=hs[:], in1=h3p[:]).then_inc(sdv, 1)
                            cnt["dv"] += 1
                            M[("hmul",)] = cnt["dv"]
                M[(tag, "hdone", sc)] = M[("hmul",)]
                M[(tag, "xdffree", sc)] = M[("hmul",)]

                for oc2 in range(NOC):
                    @block.gpsimd
                    def _(gp, tag=tag, sc=sc, oc2=oc2):
                        slot = oc2 % 2
                        key = ("w2cons", slot)
                        if key in M:
                            gp.wait_ge(spe, M[key])
                        gp.dma_start(
                            out=w2s[:, slot * NHC * 256:(slot + 1) * NHC * 256].rearrange("p (c f) -> p c f", f=256),
                            in_=w2ap[:, oc2 * 256:(oc2 + 1) * 256].rearrange("(c p) f -> p c f", p=128),
                        ).then_inc(sgp, 16)
                        cnt["gp"] += 16
                        M[("w2", tag, sc, oc2)] = cnt["gp"]

                    dv1 = cnt["dv"]

                    @block.tensor
                    def _(te, tag=tag, sc=sc, oc2=oc2, dv1=dv1):
                        slot = oc2 % 2
                        te.wait_ge(sgp, M[("w2", tag, sc, oc2)])
                        te.wait_ge(sdv, M[(tag, "hdone", sc)])
                        for st2 in range(RT):
                            if st2 >= 2:
                                te.wait_ge(sdv, dv1 + st2 - 1)
                            elif ("oevprev",) in M:
                                te.wait_ge(sdv, M[("oevprev",)])
                            last = None
                            for hc in range(NHC):
                                last = te.matmul(
                                    out=oep[st2 % 2][:],
                                    lhsT=hfm[:, hc * SCW + st2 * 128: hc * SCW + (st2 + 1) * 128],
                                    rhs=w2s[:, slot * NHC * 256 + hc * 256: slot * NHC * 256 + (hc + 1) * 256],
                                    start=(hc == 0), stop=(hc == NHC - 1))
                            last.then_inc(spe, 1)
                            cnt["pe"] += 1
                            M[("omm", tag, sc, oc2, st2)] = cnt["pe"]
                        M[("w2cons", slot)] = cnt["pe"]

                    for st2 in range(RT):
                        if tag == "s":
                            @block.gpsimd
                            def _(gp, sc=sc, oc2=oc2, st2=st2):
                                if ("x1cons", sc, oc2, st2 - 2) in M:
                                    gp.wait_ge(sdv, M[("x1cons", sc, oc2, st2 - 2)])
                                elif st2 < 2 and ("x1cons", sc, oc2 - 1, RT - 2 + st2) in M:
                                    gp.wait_ge(sdv, M[("x1cons", sc, oc2 - 1, RT - 2 + st2)])
                                gp.dma_start(out=x1r[:, (st2 % 2) * D:(st2 % 2 + 1) * D],
                                             in_=X1T[sc * SCW + st2 * 128: sc * SCW + (st2 + 1) * 128, :]
                                             ).then_inc(sgp, 16)
                                cnt["gp"] += 16
                                M[("x1r", sc, oc2, st2)] = cnt["gp"]

                        @block.vector
                        def _(dv, tag=tag, sc=sc, oc2=oc2, st2=st2):
                            dv.wait_ge(spe, M[("omm", tag, sc, oc2, st2)])
                            if tag == "e":
                                if ("e", "scat", sc - 1) in M and oc2 == 0 and st2 == 0:
                                    dv.wait_ge(sgp, M[("e", "scat", sc - 1)])
                                dv.tensor_copy(
                                    out=oesb[:, st2 * D + oc2 * 256: st2 * D + (oc2 + 1) * 256],
                                    in_=oep[st2 % 2][:]).then_inc(sdv, 1)
                            else:
                                dv.wait_ge(sgp, M[("x1r", sc, oc2, st2)])
                                if ("s2st", sc, oc2 - 1) in M and st2 == 0:
                                    dv.wait_ge(sst, M[("s2st", sc, oc2 - 1)])
                                dv.scalar_tensor_tensor(
                                    out=s2oc[:, st2 * 256:(st2 + 1) * 256],
                                    in0=oep[st2 % 2][:], scalar=1.0,
                                    in1=x1r[:, (st2 % 2) * D + oc2 * 256: (st2 % 2) * D + (oc2 + 1) * 256],
                                    op0=ALU.mult, op1=ALU.add).then_inc(sdv, 1)
                            cnt["dv"] += 1
                            M[("oev", tag, sc, oc2, st2)] = cnt["dv"]
                            if tag == "s":
                                M[("x1cons", sc, oc2, st2)] = cnt["dv"]
                            M[("oevprev",)] = cnt["dv"]

                    if tag == "s":
                        @block.sync
                        def _(sy, sc=sc, oc2=oc2):
                            sy.wait_ge(sdv, M[("oev", "s", sc, oc2, RT - 1)])
                            sy.dma_start(
                                out=S2P[sc * SCW:(sc + 1) * SCW, oc2 * 256:(oc2 + 1) * 256]
                                    .rearrange("(s p) f -> p s f", p=128),
                                in_=s2oc[:].rearrange("p (s f) -> p s f", f=256),
                            ).then_inc(sst, 16)
                            cnt["st"] += 16
                            M[("s2st", sc, oc2)] = cnt["st"]

                if tag == "e":
                    @block.gpsimd
                    def _(gp, sc=sc):
                        for st2 in range(RT):
                            gp.wait_ge(sdv, M[("oev", "e", sc, NOC - 1, st2)])
                            gp.indirect_dma_start(
                                out=RETP,
                                out_offset=IndirectOffsetOnAxis(
                                    retidx[:, sc * RT + st2: sc * RT + st2 + 1], 0),
                                in_=oesb[:, st2 * D:(st2 + 1) * D], in_offset=None,
                            ).then_inc(sgp, 16)
                            cnt["gp"] += 16
                            gp.wait_ge(sgp, cnt["gp"])
                        M[("e", "scat", sc)] = cnt["gp"]

        quadrant("e", W13E, W2E, None)
        quadrant("s", SW13, SW2, X1F)

        @block.sync
        def _(sy):
            sy.wait_ge(sst, cnt["st"])
            sy.wait_ge(sgp, M[("e", "scat", NSC - 1)])

    es.close()
    return nc


def build_l4():
    nc = bass.Bass("TRN2", target_bir_lowering=False, detect_race_conditions=False)
    RETR = nc.dram_tensor("RETR", [E * CLOC, D], BF16, kind="ExternalInput").ap()
    EIDX = nc.dram_tensor("EIDX", [128, NTT], F32, kind="ExternalInput").ap()
    LPOS = nc.dram_tensor("LPOS", [128, NTT], F32, kind="ExternalInput").ap()
    GATE = nc.dram_tensor("GATE", [128, NTT], F32, kind="ExternalInput").ap()
    CNTA = nc.dram_tensor("CNTA", [NC, E], F32, kind="ExternalInput").ap()
    S2P = nc.dram_tensor("S2P", [TPC, D], F32, kind="ExternalInput").ap()
    SELC = nc.dram_tensor("SELC", [8, 1], F32, kind="ExternalInput").ap()
    SLT = nc.dram_tensor("SLT", [8, 8], F32, kind="ExternalInput").ap()
    IOTA8 = nc.dram_tensor("IOTA8", [128, E], F32, kind="ExternalInput").ap()
    TOKI = nc.dram_tensor("TOKI", [128, NTT], I32, kind="ExternalInput").ap()
    ONESR = nc.dram_tensor("ONESR", [1, 128], F32, kind="ExternalInput").ap()

    OUT = nc.dram_tensor("OUT", [TPC, D], F32, kind="ExternalOutput").ap()

    RTD = nc.dram_tensor("RTD", [TPC + 1, D], BF16, kind="Internal").ap()
    TKM = nc.dram_tensor("TKM", [E * CLOC + 1, 1], I32, kind="Internal").ap()

    Tn = nc.alloc_sbuf_tensor
    P = nc.alloc_psum_tensor
    eidx = Tn("eidx", [128, NTT], F32)
    lpos = Tn("lpos", [128, NTT], F32)
    gate = Tn("gate", [128, NTT], F32)
    cnta = Tn("cnta", [8, E], F32)
    selc = Tn("selc", [8, 1], F32)
    slt = Tn("slt", [8, 8], F32)
    iota8 = Tn("iota8", [128, E], F32)
    toki = Tn("toki", [128, NTT], I32)
    onesr = Tn("onesr", [1, 128], F32)
    gbme = Tn("gbme", [128, E], F32)
    oh8 = Tn("oh8", [128, E], F32)
    t1 = Tn("t1", [128, 1], F32)
    t2 = Tn("t2", [128, 1], F32)
    gk16 = Tn("gk16", [128, NTT], F32)
    pidx4 = Tn("pidx4", [128, NTT], I32)
    zti = Tn("zti", [128, 32], I32)
    ztile = Tn("ztile", [128, D], BF16)
    tmsb = Tn("tmsb", [128, 1], I32)
    rb = Tn("rb", [128, D], BF16)
    rgf = Tn("rgf", [128, D], F32)
    s2f = Tn("s2f", [128, D], F32)
    s2pr = Tn("s2pr", [128, D], F32)
    sqscr = Tn("sqscr", [128, D], F32)
    stats = Tn("stats", [128, 8], F32)
    outf = Tn("outf", [128, D], F32)

    pg4 = P("pg4", [128, 512], F32)
    pgb = pg4[0:8, 0:8]
    pgr = pg4[0:1, 8:16]
    pgm = pg4[:, 16:24]

    es = contextlib.ExitStack()
    sgp = es.enter_context(nc.semaphore("sgp"))
    sld = es.enter_context(nc.semaphore("sld"))
    spe = es.enter_context(nc.semaphore("spe"))
    sdv = es.enter_context(nc.semaphore("sdv"))
    sac = es.enter_context(nc.semaphore("sac"))
    sst = es.enter_context(nc.semaphore("sst"))

    cnt = {"gp": 0, "ld": 0, "pe": 0, "dv": 0, "ac": 0, "st": 0}
    M = {}

    def bar(dv, inst):
        inst.then_inc(sdv, 1)
        cnt["dv"] += 1
        dv.wait_ge(sdv, cnt["dv"])

    with nc.Block() as block:
        @block.sync
        def _(sy):
            for ap_s, ap_d in ((EIDX, eidx), (LPOS, lpos), (GATE, gate), (CNTA, cnta),
                               (SELC, selc), (SLT, slt), (IOTA8, iota8), (TOKI, toki),
                               (ONESR, onesr)):
                sy.dma_start(out=ap_d[:], in_=ap_s).then_inc(sld, 16)
                cnt["ld"] += 16
            M["consts"] = cnt["ld"]

        @block.tensor
        def _(te):
            te.wait_ge(sld, M["consts"])
            te.matmul(out=pgb, lhsT=slt[:], rhs=cnta[:], start=True, stop=True)
            te.matmul(out=pgr, lhsT=selc[:], rhs=pgb if False else cnta[:],
                      start=True, stop=True).then_inc(spe, 1)
            cnt["pe"] += 1
            M["g1"] = cnt["pe"]

        @block.vector
        def _(dv):
            dv.wait_ge(spe, M["g1"])
            bar(dv, dv.tensor_copy(out=gbme[0:8, :], in_=pgb))
            M["gbev"] = cnt["dv"]

        @block.tensor
        def _(te):
            te.wait_ge(sdv, M["gbev"])
            te.matmul(out=pgr, lhsT=selc[:], rhs=gbme[0:8, :], start=True, stop=True).then_inc(spe, 1)
            cnt["pe"] += 1
            M["g2"] = cnt["pe"]

        @block.vector
        def _(dv):
            dv.wait_ge(spe, M["g2"])
            bar(dv, dv.tensor_copy(out=gbme[0:1, :], in_=pgr))
            M["grow"] = cnt["dv"]

        @block.tensor
        def _(te):
            te.wait_ge(sdv, M["grow"])
            te.matmul(out=pgm, lhsT=onesr[:], rhs=gbme[0:1, :], start=True, stop=True).then_inc(spe, 1)
            cnt["pe"] += 1
            M["g3"] = cnt["pe"]

        @block.vector
        def _(dv):
            dv.wait_ge(spe, M["g3"])
            bar(dv, dv.tensor_copy(out=gbme[:], in_=pgm))
            for tt in range(NTT):
                bar(dv, dv.tensor_scalar(out=oh8[:], in0=iota8[:], scalar1=eidx[:, tt:tt + 1],
                                         scalar2=None, op0=ALU.is_equal))
                bar(dv, dv.tensor_mul(out=oh8[:], in0=oh8[:], in1=gbme[:]))
                bar(dv, dv.reduce_sum(out=t1[:], in_=oh8[:], axis=AX.X))
                bar(dv, dv.tensor_add(out=t1[:], in0=t1[:], in1=lpos[:, tt:tt + 1]))
                bar(dv, dv.tensor_single_scalar(out=t1[:], in_=t1[:], scalar=float(CAP), op=ALU.is_lt))
                bar(dv, dv.tensor_mul(out=gk16[:, tt:tt + 1] if False else t2[:],
                                      in0=t1[:], in1=gate[:, tt:tt + 1]))
                bar(dv, dv.tensor_copy(out=gk16[:, tt:tt + 1], in_=t2[:]))
                bar(dv, dv.tensor_single_scalar(out=t2[:], in_=lpos[:, tt:tt + 1],
                                                scalar=float(CLOC), op=ALU.is_lt))
                bar(dv, dv.scalar_tensor_tensor(out=t1[:], in0=eidx[:, tt:tt + 1],
                                                scalar=float(CLOC), in1=lpos[:, tt:tt + 1],
                                                op0=ALU.mult, op1=ALU.add))
                bar(dv, dv.tensor_scalar_sub(out=t1[:], in0=t1[:], scalar1=float(E * CLOC)))
                bar(dv, dv.tensor_mul(out=t1[:], in0=t1[:], in1=t2[:]))
                bar(dv, dv.tensor_scalar_add(out=t1[:], in0=t1[:], scalar1=float(E * CLOC)))
                bar(dv, dv.tensor_copy(out=pidx4[:, tt:tt + 1], in_=t1[:]))
            dv.memset(ztile[:], 0.0)
            bar(dv, dv.memset(zti[:], CAP))
            M["routmath"] = cnt["dv"]

        @block.gpsimd
        def _(gp):
            gp.wait_ge(sdv, M["routmath"])
            for b in range(16):
                gp.dma_start(out=RTD[b * 128:(b + 1) * 128, :], in_=ztile[:]).then_inc(sgp, 16)
                cnt["gp"] += 16
            gp.dma_start(out=RTD[TPC:TPC + 1, :], in_=ztile[0:1, :]).then_inc(sgp, 16)
            cnt["gp"] += 16
            gp.dma_start(out=TKM[0:E * CLOC, 0:1].rearrange("(p a) c -> p a c", a=32),
                         in_=zti[:].rearrange("p (a c) -> p a c", c=1)).then_inc(sgp, 16)
            cnt["gp"] += 16
            gp.dma_start(out=TKM[E * CLOC:E * CLOC + 1, 0:1], in_=zti[0:1, 0:1]).then_inc(sgp, 16)
            cnt["gp"] += 16
            gp.wait_ge(sgp, cnt["gp"])
            for tt in range(NTT):
                gp.indirect_dma_start(
                    out=TKM, out_offset=IndirectOffsetOnAxis(pidx4[:, tt:tt + 1], 0),
                    in_=toki[:, tt:tt + 1], in_offset=None,
                ).then_inc(sgp, 16)
                cnt["gp"] += 16
            gp.wait_ge(sgp, cnt["gp"])
            for rt2 in range(32):
                gp.dma_start(out=tmsb[:], in_=TKM[rt2 * 128:(rt2 + 1) * 128, 0:1]).then_inc(sgp, 16)
                gp.dma_start(out=rb[:], in_=RETR[rt2 * 128:(rt2 + 1) * 128, :]).then_inc(sgp, 16)
                cnt["gp"] += 32
                gp.wait_ge(sgp, cnt["gp"])
                gp.indirect_dma_start(
                    out=RTD, out_offset=IndirectOffsetOnAxis(tmsb[:, 0:1], 0),
                    in_=rb[:], in_offset=None,
                ).then_inc(sgp, 16)
                cnt["gp"] += 16
                gp.wait_ge(sgp, cnt["gp"])
            M["rscat"] = cnt["gp"]

        for tt in range(NTT):
            @block.sync
            def _(sy, tt=tt):
                sy.wait_ge(sgp, M["rscat"])
                if tt > 0:
                    sy.wait_ge(sdv, M[("tiledone", tt - 1)])
                sy.dma_start(out=rb[:], in_=RTD[tt * 128:(tt + 1) * 128, :]).then_inc(sld, 16)
                cnt["ld"] += 16
                sy.dma_start(out=s2pr[:], in_=S2P[tt * 128:(tt + 1) * 128, :]).then_inc(sld, 16)
                cnt["ld"] += 16
                M[("tl", tt)] = cnt["ld"]

            ac0 = cnt["ac"]
            dv0 = cnt["dv"]

            @block.scalar
            def _(ac, tt=tt, dv0=dv0):
                ac.wait_ge(sld, M[("tl", tt)])
                if tt > 0:
                    ac.wait_ge(sdv, M[("s2f", tt - 1)])
                ac.activation(out=rgf[:], in_=rb[:], func=AF.Copy,
                              scale=gk16[:, tt:tt + 1]).then_inc(sac, 1)
                cnt["ac"] += 1
                M[("rg", tt)] = cnt["ac"]
                ac.wait_ge(sdv, dv0 + 2)
                ac.activation(out=sqscr[:], in_=s2f[:], func=AF.Square,
                              accum_out=stats[:, 1:2]).then_inc(sac, 1)
                cnt["ac"] += 1
                M[("sq", tt)] = cnt["ac"]

            @block.vector
            def _(dv, tt=tt):
                dv.wait_ge(sac, M[("rg", tt)])
                dv.tensor_add(out=s2f[:], in0=s2pr[:], in1=rgf[:]).then_inc(sdv, 1)
                cnt["dv"] += 1
                M[("s2f", tt)] = cnt["dv"]
                dv.reduce_sum(out=stats[:, 0:1], in_=s2f[:], axis=AX.X).then_inc(sdv, 1)
                cnt["dv"] += 1
                dv.wait_ge(sac, M[("sq", tt)])
                dv.tensor_scalar_mul(out=stats[:, 0:1], in0=stats[:, 0:1], scalar1=1.0 / D)
                bar(dv, dv.tensor_scalar_mul(out=stats[:, 1:2], in0=stats[:, 1:2], scalar1=1.0 / D))
                bar(dv, dv.tensor_mul(out=stats[:, 6:7], in0=stats[:, 0:1], in1=stats[:, 0:1]))
                bar(dv, dv.tensor_sub(out=stats[:, 2:3], in0=stats[:, 1:2], in1=stats[:, 6:7]))
                dv.tensor_scalar_add(out=stats[:, 2:3], in0=stats[:, 2:3], scalar1=1e-5).then_inc(sdv, 1)
                cnt["dv"] += 1
                M[("var", tt)] = cnt["dv"]
                dv.wait_ge(sac, M[("sq", tt)] + 1)
                bar(dv, dv.tensor_copy(out=stats[:, 4:5], in_=stats[:, 3:4]))
                M[("lnv2", tt)] = cnt["dv"]
                dv.wait_ge(sac, M[("sq", tt)] + 2)
                dv.wait_ge(sdv, cnt["dv"])
                if tt > 0:
                    dv.wait_ge(sst, M[("ost", tt - 1)])
                dv.tensor_scalar(out=outf[:], in0=s2f[:], scalar1=stats[:, 0:1],
                                 scalar2=stats[:, 5:6], op0=ALU.subtract,
                                 op1=ALU.mult).then_inc(sdv, 1)
                cnt["dv"] += 1
                M[("outf", tt)] = cnt["dv"]

            @block.scalar
            def _(ac, tt=tt):
                ac.wait_ge(sdv, M[("var", tt)])
                ac.activation(out=stats[:, 3:4], in_=stats[:, 2:3], func=AF.Ln).then_inc(sac, 1)
                cnt["ac"] += 1
                ac.wait_ge(sdv, M[("lnv2", tt)])
                ac.activation(out=stats[:, 5:6], in_=stats[:, 4:5], func=AF.Exp,
                              scale=-0.5).then_inc(sac, 1)
                cnt["ac"] += 1

            @block.sync
            def _(sy, tt=tt):
                sy.wait_ge(sdv, M[("outf", tt)])
                sy.dma_start(out=OUT[tt * 128:(tt + 1) * 128, :], in_=outf[:]).then_inc(sst, 16)
                cnt["st"] += 16
                M[("ost", tt)] = cnt["st"]
                M[("tiledone", tt)] = cnt["dv"]

        @block.sync
        def _(sy):
            sy.wait_ge(sst, cnt["st"])

    es.close()
    return nc


_PROGS = {}


def _get(name, builder):
    if name not in _PROGS:
        _PROGS[name] = builder()
    return _PROGS[name]


def _run_l1(xs, WqkvT, cosT, sinT):
    nc1 = _get("l1", build_l1)
    psgn = np.asarray(_p_signed())
    idn = np.eye(128, dtype=ml_dtypes.bfloat16)
    in_maps = []
    for c in range(NC):
        sl = slice(c * TPC, (c + 1) * TPC)
        in_maps.append(dict(
            X=np.ascontiguousarray(xs[sl]),
            WQKVT=WqkvT,
            COSF=np.ascontiguousarray(cosT[:, sl]),
            SINF=np.ascontiguousarray(sinT[:, sl]),
            PSGN=psgn, IDN=idn,
        ))
    res = run_bass_kernel_spmd(nc1, in_maps, core_ids=list(range(NC)))
    return res.results




def _consts_l2():
    mask = np.zeros((128, 256), ml_dtypes.bfloat16)
    for m in range(128):
        mask[m, m:] = 1.0
    ut = np.zeros((128, 128), np.float32)
    for k in range(128):
        ut[k, k:] = 1.0
    return dict(
        MASK=mask,
        IDN=np.eye(128, dtype=ml_dtypes.bfloat16),
        IDNF=np.eye(128, dtype=np.float32),
        UT=ut,
        ONESR=np.ones((1, 128), np.float32),
        IOTA8=np.tile(np.arange(E, dtype=np.float32), (128, 1)),
    )


def _run_l2(l1res, xs, WoutT, Wg):
    nc2 = _get("l2", build_l2)
    consts = _consts_l2()
    szall = np.stack([np.asarray(l1res[c]["SZT"]) for c in range(NC)])
    in_maps = []
    for c in range(NC):
        sl = slice(c * TPC, (c + 1) * TPC)
        cpref = np.zeros((128, NC), np.float32)
        cpref[:, :c] = 1.0
        in_maps.append(dict(
            QF=np.asarray(l1res[c]["QF"]), KF=np.asarray(l1res[c]["KF"]),
            KT=np.asarray(l1res[c]["KT"]), VT=np.asarray(l1res[c]["VT"]),
            SZALL=szall, X=np.ascontiguousarray(xs[sl]),
            WOUTT=WoutT, WG=Wg, CPREFB=cpref, **consts,
        ))
    res = run_bass_kernel_spmd(nc2, in_maps, core_ids=list(range(NC)))
    return res.results




def _consts_l3():
    ejc = np.zeros((8, NC * 128), np.float32)
    for j in range(NC):
        ejc[j, j * 128:(j + 1) * 128] = 1.0
    slt = np.zeros((8, 8), np.float32)
    for k in range(8):
        slt[k, k + 1:] = 1.0
    lpc = (np.arange(128)[:, None] + 128 * np.arange(4)[None, :]).astype(np.float32)
    slc = (np.arange(128)[:, None] + 128 * np.arange(16)[None, :]).astype(np.float32)
    return dict(EJC=ejc, SLT=slt, LPC=lpc, SLC=slc,
                IDN=np.eye(128, dtype=ml_dtypes.bfloat16))


def _interleave13(w1, w3):
    a = w1.reshape(D, HF // 128, 128)
    b = w3.reshape(D, HF // 128, 128)
    return np.ascontiguousarray(np.concatenate([a, b], axis=2).reshape(D, 2 * HF))


def _run_l3(l2res, W1, W3, W2, sw1, sw3, sw2):
    nc3 = _get("l3", build_l3)
    consts = _consts_l3()
    cnta = np.concatenate([np.asarray(l2res[c]["CNT"]) for c in range(NC)], 0)
    pbufs = [np.asarray(l2res[c]["PBUF"]) for c in range(NC)]
    sw13 = _interleave13(sw1, sw3)
    in_maps = []
    for i in range(NC):
        runs = np.concatenate([pbufs[j][i * CLOC:(i + 1) * CLOC] for j in range(NC)], 0)
        selb = np.zeros((8, E), np.float32)
        selb[:, i] = 1.0
        in_maps.append(dict(
            RUNS=np.ascontiguousarray(runs), CNTA=cnta,
            X1F=np.asarray(l2res[i]["X1F"]), X1T=np.asarray(l2res[i]["X1T"]),
            W13E=_interleave13(W1[i], W3[i]),
            W2E=np.ascontiguousarray(W2[i]),
            SW13=sw13, SW2=np.ascontiguousarray(sw2), SELB=selb, **consts,
        ))
    res = run_bass_kernel_spmd(nc3, in_maps, core_ids=list(range(NC)))
    return res.results




def _run_l4(l2res, l3res):
    nc4 = _get("l4", build_l4)
    cnta = np.concatenate([np.asarray(l2res[c]["CNT"]) for c in range(NC)], 0)
    retps = [np.asarray(l3res[i]["RETP"]) for i in range(NC)]
    slt = np.zeros((8, 8), np.float32)
    for k in range(8):
        slt[k, k + 1:] = 1.0
    toki = (np.arange(128)[:, None] + 128 * np.arange(NTT)[None, :]).astype(np.int32)
    iota8 = np.tile(np.arange(E, dtype=np.float32), (128, 1))
    onesr = np.ones((1, 128), np.float32)
    in_maps = []
    for j in range(NC):
        retr = np.concatenate([retps[i][j * CLOC:(j + 1) * CLOC] for i in range(NC)], 0)
        selc = np.zeros((8, 1), np.float32)
        selc[j, 0] = 1.0
        in_maps.append(dict(
            RETR=np.ascontiguousarray(retr),
            EIDX=np.asarray(l2res[j]["EIDX"]), LPOS=np.asarray(l2res[j]["LPOS"]),
            GATE=np.asarray(l2res[j]["GATE"]), CNTA=cnta,
            S2P=np.asarray(l3res[j]["S2P"]),
            SELC=selc, SLT=slt, IOTA8=iota8, TOKI=toki, ONESR=onesr,
        ))
    res = run_bass_kernel_spmd(nc4, in_maps, core_ids=list(range(NC)))
    return res.results


def kernel(L0_attn_x, L0_attn_W_qkv, L0_attn_W_out, L0_ffn_Wg, L0_ffn_W1,
           L0_ffn_W3, L0_ffn_W2, L0_ffn_sw1, L0_ffn_sw3, L0_ffn_sw2):
    xs = np.ascontiguousarray(np.asarray(L0_attn_x, np.float32).reshape(T, D))
    WqkvT = np.ascontiguousarray(np.asarray(L0_attn_W_qkv, np.float32).T)
    WoutT = np.ascontiguousarray(np.asarray(L0_attn_W_out, np.float32).T)
    Wg = np.ascontiguousarray(np.asarray(L0_ffn_Wg, np.float32))
    cos, sin = _rope_tables()
    r1 = _run_l1(xs, WqkvT, np.ascontiguousarray(cos.T), np.ascontiguousarray(sin.T))
    r2 = _run_l2(r1, xs, WoutT, Wg)
    r3 = _run_l3(r2, np.asarray(L0_ffn_W1, np.float32), np.asarray(L0_ffn_W3, np.float32),
                 np.asarray(L0_ffn_W2, np.float32), np.ascontiguousarray(np.asarray(L0_ffn_sw1, np.float32)),
                 np.ascontiguousarray(np.asarray(L0_ffn_sw3, np.float32)),
                 np.ascontiguousarray(np.asarray(L0_ffn_sw2, np.float32)))
    r4 = _run_l4(r2, r3)
    out = np.concatenate([np.asarray(r4[c]["OUT"], np.float32) for c in range(NC)], 0)
    return out.reshape(1, T, D)
